# version 07
# baseline (speedup 1.0000x reference)
import numpy as np
import concourse.bass as bass
import concourse.mybir as mybir
from concourse.bass_types import AxisInfo
from concourse.bass_utils import run_bass_kernel_spmd

F32 = mybir.dt.float32
BF16 = mybir.dt.bfloat16
I32 = mybir.dt.int32
U32 = mybir.dt.uint32
AF = mybir.ActivationFunctionType
ALU = mybir.AluOpType
AX = mybir.AxisListType

NC = 8
B, S, H = 2, 4096, 2048
T = B * S
TOK = T // NC
E, K4, II, IS, CAP = 16, 4, 1408, 2816, 1024
HEADS, DH = 16, 128
KC = H // 128
NT = TOK // 128
SCALE = DH ** -0.5
EPS = 1e-5
DUMP = 2 * CAP
CKW = 512


class Sched:
    ENG = ("pe", "act", "dve", "pool", "sp")

    def __init__(self, nc):
        self.nc = nc
        self.h = {"pe": nc.tensor, "act": nc.scalar, "dve": nc.vector,
                  "pool": nc.gpsimd, "sp": nc.sync}
        self.sems = {}
        self.cnt = {}
        for e in self.ENG:
            self.sems[e] = nc.alloc_semaphore("sem_" + e)
            self.cnt[e] = 0
        self.lw = {}
        self.lr = {}
        self.waited = {e: {} for e in self.ENG}

    def sem(self, name):
        if name not in self.sems:
            self.sems[name] = self.nc.alloc_semaphore("sem_" + name)
        if name not in self.cnt:
            self.cnt[name] = 0
        return self.sems[name]

    def _wait(self, eng, sname, val):
        if val <= 0:
            return
        if eng == "pe" and sname == "pe":
            return
        if self.waited[eng].get(sname, 0) >= val:
            return
        self.h[eng].wait_ge(self.sems[sname], val)
        self.waited[eng][sname] = val

    def _pre(self, eng, r, w):
        for k in r:
            for sname, val in self.lw.get(k, {}).items():
                self._wait(eng, sname, val)
        for k in w:
            for sname, val in self.lw.get(k, {}).items():
                self._wait(eng, sname, val)
            for sname, val in self.lr.get(k, {}).items():
                self._wait(eng, sname, val)

    def _post(self, sname, val, r, w):
        for k in w:
            self.lw[k] = {sname: val}
            self.lr[k] = {}
        for k in r:
            d = self.lr.setdefault(k, {})
            if d.get(sname, 0) < val:
                d[sname] = val

    def cop(self, eng, emit, r=(), w=(), inc=1):
        self._pre(eng, r, w)
        inst = emit(self.h[eng])
        self.cnt[eng] += inc
        inst.then_inc(self.sems[eng], inc)
        self._post(eng, self.cnt[eng], r, w)

    def dma(self, q, emit, r=(), w=(), sem=None, inc=16):
        sname = sem or q
        self.sem(sname)
        self._pre(q, r, w)
        inst = emit(self.h[q])
        self.cnt[sname] += inc
        inst.then_inc(self.sems[sname], inc)
        self._post(sname, self.cnt[sname], r, w)

    def alias(self, dst_keys, src_keys):
        agg = {}
        for k in src_keys:
            for d in (self.lw.get(k, {}), self.lr.get(k, {})):
                for sn, v in d.items():
                    if agg.get(sn, 0) < v:
                        agg[sn] = v
        for k in dst_keys:
            self.lw[k] = dict(agg)
            self.lr[k] = {}

    def snap(self):
        import copy
        return (dict(self.cnt), copy.deepcopy(self.lw), copy.deepcopy(self.lr),
                copy.deepcopy(self.waited))

    def restore(self, s):
        import copy
        self.cnt = dict(s[0])
        self.lw = copy.deepcopy(s[1])
        self.lr = copy.deepcopy(s[2])
        self.waited = copy.deepcopy(s[3])

    def merge(self, states):
        cnts = [s[0] for s in states]
        for c in cnts[1:]:
            if c != cnts[0]:
                diff = {k: (cnts[0].get(k), c.get(k))
                        for k in set(cnts[0]) | set(c)
                        if cnts[0].get(k) != c.get(k)}
                raise AssertionError(f"switch branches unbalanced: {diff}")
        self.cnt = dict(cnts[0])
        lw, lr = {}, {}
        for s in states:
            for k, d in s[1].items():
                dd = lw.setdefault(k, {})
                for sn, val in d.items():
                    if dd.get(sn, 0) < val:
                        dd[sn] = val
            for k, d in s[2].items():
                dd = lr.setdefault(k, {})
                for sn, val in d.items():
                    if dd.get(sn, 0) < val:
                        dd[sn] = val
        self.lw, self.lr = lw, lr
        wt = {}
        for e in Sched.ENG:
            wt[e] = {}
            allk = set()
            for s in states:
                allk |= set(s[3][e].keys())
            for k in allk:
                vals = [s[3][e].get(k, 0) for s in states]
                if min(vals) > 0:
                    wt[e][k] = min(vals)
        self.waited = wt


def b2k(c0, c1):
    return [("b2g", g) for g in range(c0 // 512, (c1 + 511) // 512)]


def build(parts=("ffn0", "attn", "ffn1"), dbg=False):
    nc = bass.Bass("TRN2", target_bir_lowering=False)
    ap = {}

    def din(name, shape, dt=F32):
        ap[name] = nc.dram_tensor(name, shape, dt, kind="ExternalInput").ap()

    def dout(name, shape, dt=F32):
        ap[name] = nc.dram_tensor(name, shape, dt, kind="ExternalOutput").ap()

    def dint(name, shape, dt):
        ap[name] = nc.dram_tensor(name, shape, dt, kind="Internal").ap()

    din("x", [128, NT * H])
    din("w_qkv", [128, 3 * H * KC]); din("w_out", [128, H * KC])
    din("b_out", [1, H])
    din("gate_w", [E, H])
    din("wg", [128, 2 * H * II // 128]); din("wu", [128, 2 * H * II // 128])
    din("wd", [128, 2 * II * H // 128])
    din("swg", [128, H * IS // 128]); din("swu", [128, H * IS // 128])
    din("swd", [128, IS * H // 128])
    din("selT", [E, 2]); din("pm", [E, 16]); din("c64", [2, 1])
    dout("y", [TOK, H])

    dint("u_tok", [TOK, H], BF16)
    dint("ag_u", [T, H], BF16)
    dint("at_in", [E, TOK], F32)
    dint("at_out", [NC * E, TOK], F32)
    dint("eo_in", [2 * CAP, H], BF16)
    dint("eo_out", [E * CAP, H], BF16)
    dint("vln", [TOK, H], BF16)
    dint("qk_in", [2 * H, TOK], BF16)
    dint("qk_out", [4 * 2 * H, TOK], BF16)
    dint("v_in", [TOK, H], BF16)
    dint("v_out", [4 * TOK, H], BF16)
    dint("ao_in", [TOK, H], BF16)
    dint("ao_out", [4 * TOK, H], BF16)

    dbg_outs = {}
    if dbg:
        for name, shape, dt in [
            ("d_u", [TOK, H], BF16), ("d_p", [TOK, E], F32),
            ("d_idx4", [TOK, K4], F32), ("d_w4", [TOK, K4], F32),
            ("d_slot", [TOK, E], F32), ("d_gidx", [TOK, K4], I32),
            ("d_wfin", [TOK, K4], F32),
            ("d_x1", [TOK, H], F32), ("d_x2", [TOK, H], F32),
            ("d_eo", [2 * CAP, H], BF16), ("d_atl", [E, TOK], F32),
            ("d_bufidx", [128, 16], I32), ("d_slotc", [128, 128], F32),
            ("d_km", [1, 16], F32), ("d_negm", [128, 2], F32),
        ]:
            dout(name, shape, dt)
            dbg_outs[name] = ap[name]

    def asb(name, shape, dt):
        return nc.alloc_sbuf_tensor(name, shape, dt)

    x_sb = asb("x_sb", [128, NT * H], BF16)
    ufeat = asb("ufeat", [128, KC * TOK], BF16)
    big1 = asb("big1", [128, 22 * TOK], BF16)
    big2 = asb("big2", [128, 2 * 11264], BF16)
    stg = asb("stg", [128, 2 * H], BF16)
    ustg = asb("ustg", [128, 2 * H], BF16)
    evst = asb("evst", [128, 2 * 512], BF16)
    brep = asb("brep", [128, H], BF16)
    scr = asb("scr", [128, 2048], F32)
    iotc = asb("iotc", [128, 1], F32)
    iot16 = asb("iot16", [128, 16], F32)
    identf = asb("identf", [128, 128], F32)
    identb = asb("identb", [128, 128], BF16)
    onesb = asb("onesb", [128, 128], BF16)
    cmask = asb("cmask", [128, 896], BF16)
    gwT = asb("gwT", [128, KC * E], BF16)
    tokid = asb("tokid", [128, 128], I32)
    w4 = asb("w4", [128, NT * K4], F32)
    idx4f = asb("idx4f", [128, NT * K4], F32)
    a_tok = asb("a_tok", [128, NT * E], F32)
    slot_tok = asb("slot_tok", [128, NT * E], F32)
    gidx = asb("gidx", [128, NT * K4], I32)
    wfin = asb("wfin", [128, NT * K4], F32)
    p_sb = asb("p_sb", [128, NT * E], F32)
    mx8 = asb("mx8", [128, 8], F32)
    ixt = asb("ixt", [128, 8], U32)
    smalls = asb("smalls", [128, 16], F32)
    at_loc = asb("at_loc", [16, TOK], F32)
    at_loc2 = asb("at_loc2", [16, TOK], F32)
    colsc = asb("colsc", [16, 16], F32)
    scA = asb("scA", [16, CKW], F32)
    scB = asb("scB", [16, 2 * CKW], F32)
    scC = asb("scC", [16, CKW], F32)
    selT_sb = asb("selT_sb", [16, 2], F32)
    pm_sb = asb("pm_sb", [16, 16], F32)
    c64_sb = asb("c64_sb", [2, 1], F32)
    c64b = asb("c64b", [2, 1], BF16)
    onesf = asb("onesf", [1, 128], F32)
    slotcols = asb("slotcols", [128, 128], F32)
    tokhi = asb("tokhi", [128, 64], BF16)
    toklo = asb("toklo", [128, 1], F32)
    hilo = asb("hilo", [128, 128], BF16)

    bufidx = asb("bufidx", [128, 16], I32)
    ismall = asb("ismall", [128, 2], I32)
    buff8 = asb("buff8", [128, 16], F32)
    kmax8 = asb("kmax8", [1, 16], F32)
    pbuf = stg[:, 0:3072]
    qbuf = stg[:, 3072:4096]
    rds = ustg[:, 0:1024]
    ksq = evst[:, 0:512]
    kn2c = at_loc[0:1, 0:512]
    qn2 = at_loc[0:1, 512:1024]
    denr = at_loc2[0:1, 0:512]
    bufrow = ustg[0:2, 0:2048]
    buffv = [at_loc2[0:1, 0:1024], at_loc[0:1, 0:1024]]
    mrow = ustg[0:1, 0:512]
    gwstg = big1[0:16, 0:2048]
    boutstg = big1[0:1, 2048:4096]

    def wAv(slot):
        return big2[:, slot * 4096:(slot + 1) * 4096]

    def wBv(slot):
        return big2[:, 8192 + slot * 4096: 8192 + (slot + 1) * 4096]

    def wVv(slot):
        return big2[:, slot * 8192:(slot + 1) * 8192]

    wAk = [b2k(0, 4096), b2k(4096, 8192)]
    wBk = [b2k(8192, 12288), b2k(12288, 16384)]
    wVk = [b2k(0, 8192), b2k(8192, 16384)]

    pg = [nc.alloc_psum_tensor(f"pg{i}", [128, 512], F32) for i in range(6)]
    pbf2 = [nc.alloc_psum_tensor(f"pbf{i}", [128, 1024], BF16) for i in range(2)]
    pbf = pbf2[0]

    Sc = Sched(nc)
    evctr = [0]

    def evac_store(bk, out_dram_ap, dkeys):
        sl = evctr[0] % 2
        evctr[0] += 1
        ev = evst[:, sl * 512:(sl + 1) * 512]
        Sc.cop("act", lambda a: a.activation(out=ev, in_=pg[bk][:, :],
                                             func=AF.Copy),
               r=[("pg%d" % bk,)], w=[("ev", sl)])
        Sc.dma("sp", lambda s: s.dma_start(out=out_dram_ap, in_=ev),
               r=[("ev", sl)], w=dkeys, sem="evd%d" % sl)

    def ufk(c0, c1):
        return [("uf", k) for k in range(c0, c1)]

    def emit_init():
        Sc.dma("pool", lambda g: g.dma_start(out=x_sb[:, :], in_=ap["x"]),
               w=[("x", q) for q in range(NT)], sem="xin")
        Sc.cop("pool", lambda g: g.iota(iotc[:, :], pattern=[[0, 1]], base=0,
               channel_multiplier=1, allow_small_or_imprecise_dtypes=True),
               w=[("iotc",)])
        Sc.cop("pool", lambda g: g.iota(iot16[:, :], pattern=[[1, 16]], base=0,
               channel_multiplier=0, allow_small_or_imprecise_dtypes=True),
               w=[("iot16",)])
        Sc.cop("pool", lambda g: g.iota(scr[:, 0:896], pattern=[[1, 896]], base=0,
               channel_multiplier=0, allow_small_or_imprecise_dtypes=True),
               w=[("scr",)])
        Sc.cop("pool", lambda g: g.iota(scr[:, 1024:1088], pattern=[[2, 64]],
               base=0, channel_multiplier=0,
               allow_small_or_imprecise_dtypes=True),
               w=[("scrE",)])
        Sc.cop("dve", lambda v: v.tensor_scalar(out=identf[:, :], in0=scr[:, 0:128],
               scalar1=iotc[:, 0:1], scalar2=None, op0=ALU.is_equal),
               r=[("scr",), ("iotc",)], w=[("identf",)])
        Sc.cop("dve", lambda v: v.tensor_copy(identb[:, :], identf[:, :]),
               r=[("identf",)], w=[("identb",)])
        Sc.cop("dve", lambda v: v.memset(onesb[:, :], 1.0), w=[("onesb",)])
        Sc.cop("dve", lambda v: v.memset(onesf[:, :], 1.0), w=[("onesf",)])
        Sc.cop("dve", lambda v: v.tensor_scalar(out=smalls[:, 2:3],
               in0=iotc[:, 0:1], scalar1=63.5, scalar2=None, op0=ALU.is_gt),
               r=[("iotc",)], w=[("sm", 2)])
        Sc.cop("dve", lambda v: v.tensor_scalar(out=tokhi[:, :],
               in0=scr[:, 1024:1088], scalar1=smalls[:, 2:3], scalar2=None,
               op0=ALU.add),
               r=[("scrE",), ("sm", 2)], w=[("tokhi",)])
        Sc.cop("dve", lambda v: v.tensor_scalar(out=smalls[:, 3:4],
               in0=smalls[:, 2:3], scalar1=64.0, scalar2=None, op0=ALU.mult),
               r=[("sm", 2)], w=[("sm", 3)])
        Sc.cop("dve", lambda v: v.tensor_sub(smalls[:, 4:5], iotc[:, 0:1],
               smalls[:, 3:4]),
               r=[("iotc",), ("sm", 3)], w=[("sm", 4)])
        Sc.cop("dve", lambda v: v.tensor_copy(toklo[:, :], smalls[:, 4:5]),
               r=[("sm", 4)], w=[("toklo",)])
        Sc.cop("dve", lambda v: v.tensor_copy(
            hilo[:, :].rearrange("p (t two) -> p t two", two=2)[:, :, 0],
            tokhi[:, :]),
            r=[("tokhi",)], w=[("hilo",)])
        Sc.cop("dve", lambda v: v.tensor_scalar(
            out=hilo[:, :].rearrange("p (t two) -> p t two", two=2)[:, :, 1],
            in0=onesb[:, 0:64], scalar1=toklo[:, 0:1], scalar2=None,
            op0=ALU.mult),
            r=[("onesb",), ("toklo",)], w=[("hilo",)])

        Sc.cop("dve", lambda v: v.tensor_scalar(out=smalls[:, 0:1], in0=iotc[:, 0:1],
               scalar1=384.0, scalar2=None, op0=ALU.add),
               r=[("iotc",)], w=[("sm", 0)])
        Sc.cop("dve", lambda v: v.tensor_scalar(out=cmask[:, :], in0=scr[:, 0:896],
               scalar1=smalls[:, 0:1], scalar2=None, op0=ALU.is_ge),
               r=[("scr",), ("sm", 0)], w=[("cmask",)])

        for i, (nm, tile, key) in enumerate([
                ("selT", selT_sb, "selT"), ("pm", pm_sb, "pm"),
                ("c64", c64_sb, "c64")]):
            Sc.dma("sp", lambda s, nm=nm, tile=tile: s.dma_start(
                out=tile[:, :], in_=ap[nm]), w=[(key,)], sem="sml%d" % i)
        Sc.dma("pool", lambda g: g.dma_start(out=gwstg, in_=ap["gate_w"]),
               w=[("b1", 0, 0), ("b1", 0, 1), ("b1", 1, 0), ("b1", 1, 1)],
               sem="smlg")
        Sc.dma("pool", lambda g: g.dma_start(out=boutstg, in_=ap["b_out"]),
               w=[("b1", 2, 0), ("b1", 2, 1), ("b1", 3, 0), ("b1", 3, 1)],
               sem="smlb")
        Sc.cop("dve", lambda v: v.tensor_copy(c64b[:, :], c64_sb[:, :]),
               r=[("c64",)], w=[("c64b",)])
        for k in range(KC):
            pb = pbf2[k % 2]
            pbk = "pbf%d" % (k % 2)
            Sc.cop("pe", lambda t, k=k, pb=pb: t.transpose(
                pb[:, 0:16], gwstg[:, k * 128:(k + 1) * 128],
                identb[0:16, 0:16]),
                r=[("b1", k // 8, 0), ("b1", k // 8, 1), ("identb",)],
                w=[(pbk,)])
            Sc.cop("dve", lambda v, k=k, pb=pb: v.tensor_copy(
                gwT[:, k * 16:(k + 1) * 16], pb[:, 0:16]),
                r=[(pbk,)], w=[("gwT",)])
        for j in range(4):
            Sc.cop("pe", lambda t, j=j: t.matmul(
                pg[5][:, :], onesb[0:1, :], boutstg[:, j * 512:(j + 1) * 512],
                start=True, stop=True),
                r=[("b1", 2, 0), ("b1", 2, 1), ("b1", 3, 0), ("b1", 3, 1),
                   ("onesb",)],
                w=[("pg5",)])
            Sc.cop("dve", lambda v, j=j: v.tensor_copy(
                brep[:, j * 512:(j + 1) * 512], pg[5][:, :]),
                r=[("pg5",)], w=[("brep",)])

    def emit_ln(dname):
        dest = ap[dname] if dname is not None else None
        for q in range(NT):
            xs = x_sb[:, q * H:(q + 1) * H]
            us = ustg[:, (q % 2) * H:((q % 2) + 1) * H]
            c0, c1 = 2 * (q % 8), 2 * (q % 8) + 1
            Sc.cop("act", lambda a, xs=xs, c0=c0: a.activation(
                out=scr[:, 0:H], in_=xs, func=AF.Copy,
                accum_out=smalls[:, c0:c0 + 1]),
                r=[("x", q)], w=[("scr",), ("sm", c0)])
            Sc.cop("act", lambda a, xs=xs, c1=c1: a.activation(
                out=scr[:, 0:H], in_=xs, func=AF.Square,
                accum_out=smalls[:, c1:c1 + 1]),
                r=[("x", q)], w=[("scr",), ("sm", c1)])
            Sc.cop("dve", lambda v, c0=c0: v.tensor_scalar(
                out=smalls[:, c0:c0 + 1], in0=smalls[:, c0:c0 + 1],
                scalar1=-1.0 / H, scalar2=None, op0=ALU.mult),
                r=[("sm", c0)], w=[("sm", c0)])
            Sc.cop("dve", lambda v, c0=c0: v.tensor_scalar(
                out=colsc[0:16, 15:16] if False else smalls[:, 14:15],
                in0=smalls[:, c0:c0 + 1],
                scalar1=smalls[:, c0:c0 + 1], scalar2=None, op0=ALU.mult),
                r=[("sm", c0)], w=[("m2",)])
            Sc.cop("dve", lambda v, c1=c1: v.tensor_scalar(
                out=smalls[:, c1:c1 + 1], in0=smalls[:, c1:c1 + 1],
                scalar1=1.0 / H, scalar2=EPS, op0=ALU.mult, op1=ALU.add),
                r=[("sm", c1)], w=[("sm", c1)])
            Sc.cop("dve", lambda v, c1=c1: v.tensor_sub(
                smalls[:, c1:c1 + 1], smalls[:, c1:c1 + 1], smalls[:, 14:15]),
                r=[("sm", c1), ("m2",)], w=[("sm", c1)])
            Sc.cop("act", lambda a, c1=c1: a.activation(
                out=smalls[:, c1:c1 + 1], in_=smalls[:, c1:c1 + 1],
                func=AF.Sqrt),
                r=[("sm", c1)], w=[("sm", c1)])
            Sc.cop("dve", lambda v, c1=c1: v.reciprocal(
                smalls[:, c1:c1 + 1], smalls[:, c1:c1 + 1]),
                r=[("sm", c1)], w=[("sm", c1)])
            Sc.cop("dve", lambda v, xs=xs, us=us, c0=c0, c1=c1: v.tensor_scalar(
                out=us, in0=xs, scalar1=smalls[:, c0:c0 + 1],
                scalar2=smalls[:, c1:c1 + 1], op0=ALU.add, op1=ALU.mult),
                r=[("x", q), ("sm", c0), ("sm", c1)], w=[("ustg", q % 2)])
            if dname is not None:
                Sc.dma("sp", lambda s, us=us, q=q: s.dma_start(
                    out=dest[q * 128:(q + 1) * 128, :], in_=us),
                    r=[("ustg", q % 2)], w=[(dname, q)], sem="us%d" % (q % 2))
            for k in range(KC):
                pb = pbf2[k % 2]
                pbk = "pbf%d" % (k % 2)
                Sc.cop("pe", lambda t, us=us, k=k, pb=pb: t.transpose(
                    pb[:, 0:128], us[:, k * 128:(k + 1) * 128],
                    identb[:, :]),
                    r=[("ustg", q % 2), ("identb",)], w=[(pbk,)])
                Sc.cop("dve", lambda v, k=k, q=q, pb=pb: v.tensor_copy(
                    ufeat[:, k * TOK + q * 128: k * TOK + (q + 1) * 128],
                    pb[:, 0:128]),
                    r=[(pbk,)], w=[("uf", k)] if False else [("uf", k),
                                                             ("ufq", k, q)])

    def emit_routing(inv):
        for tau in range(2):
            bank = 4 + tau

            def mmgrp(t, tau=tau, bank=bank):
                last = None
                for k in range(KC):
                    last = t.matmul(
                        pg[bank][0:16, :], gwT[:, k * 16:(k + 1) * 16],
                        ufeat[:, k * TOK + tau * 512:k * TOK + (tau + 1) * 512],
                        start=(k == 0), stop=(k == KC - 1))
                return last
            Sc.cop("pe", mmgrp, r=[("gwT",)] + ufk(0, KC), w=[("pg%d" % bank,)])
            Sc.cop("dve", lambda v, tau=tau, bank=bank: v.tensor_copy(
                at_loc2[:, tau * 512:(tau + 1) * 512], pg[bank][0:16, :]),
                r=[("pg%d" % bank,)], w=[("atl2",)])
        for q in range(NT):
            Sc.cop("pe", lambda t, q=q: t.transpose(
                pg[5][:, 0:16], at_loc2[:, q * 128:(q + 1) * 128],
                identf[0:16, 0:16]),
                r=[("atl2",), ("identf",)], w=[("pg6",)])
            Sc.cop("dve", lambda v, q=q: v.tensor_copy(
                p_sb[:, q * E:(q + 1) * E], pg[5][0:128, 0:16]),
                r=[("pg6",)], w=[("p", q)])
        for q in range(NT):
            pq = p_sb[:, q * E:(q + 1) * E]
            Sc.cop("dve", lambda v, pq=pq: v.reduce_max(
                smalls[:, 0:1], pq, axis=AX.X), r=[("p", q)], w=[("sm", 0)])
            Sc.cop("dve", lambda v: v.tensor_scalar(
                out=smalls[:, 0:1], in0=smalls[:, 0:1], scalar1=-1.0,
                scalar2=None, op0=ALU.mult), r=[("sm", 0)], w=[("sm", 0)])
            Sc.cop("act", lambda a, pq=pq: a.activation(
                out=pq, in_=pq, func=AF.Exp, bias=smalls[:, 0:1],
                accum_out=smalls[:, 1:2]),
                r=[("p", q), ("sm", 0)], w=[("p", q), ("sm", 1)])
            Sc.cop("dve", lambda v: v.reciprocal(smalls[:, 1:2], smalls[:, 1:2]),
                   r=[("sm", 1)], w=[("sm", 1)])
            Sc.cop("dve", lambda v, pq=pq: v.tensor_scalar(
                out=pq, in0=pq, scalar1=smalls[:, 1:2], scalar2=None,
                op0=ALU.mult), r=[("p", q), ("sm", 1)], w=[("p", q)])
            Sc.cop("dve", lambda v, pq=pq: v.max(out=mx8[:, :], in_=pq),
                   r=[("p", q)], w=[("mx8",)])
            Sc.cop("dve", lambda v, pq=pq: v.max_index(
                out=ixt[:, :], in_max=mx8[:, :], in_values=pq),
                r=[("p", q), ("mx8",)], w=[("ixt",)])
            Sc.cop("dve", lambda v, q=q: v.tensor_copy(
                idx4f[:, q * K4:(q + 1) * K4], ixt[:, 0:4]),
                r=[("ixt",)], w=[("idx4", q)])
            Sc.cop("dve", lambda v: v.tensor_reduce(
                smalls[:, 2:3], mx8[:, 0:4], axis=AX.X, op=ALU.add),
                r=[("mx8",)], w=[("sm", 2)])
            Sc.cop("dve", lambda v: v.tensor_scalar(
                out=smalls[:, 2:3], in0=smalls[:, 2:3], scalar1=1e-20,
                scalar2=None, op0=ALU.add), r=[("sm", 2)], w=[("sm", 2)])
            Sc.cop("dve", lambda v: v.reciprocal(smalls[:, 2:3], smalls[:, 2:3]),
                   r=[("sm", 2)], w=[("sm", 2)])
            Sc.cop("dve", lambda v, q=q: v.tensor_scalar(
                out=w4[:, q * K4:(q + 1) * K4], in0=mx8[:, 0:4],
                scalar1=smalls[:, 2:3], scalar2=None, op0=ALU.mult),
                r=[("mx8",), ("sm", 2)], w=[("w4", q)])
            dst = a_tok[:, q * E:(q + 1) * E]
            for k in range(K4):
                if k == 0:
                    Sc.cop("dve", lambda v, q=q, k=k, dst=dst: v.tensor_scalar(
                        out=dst, in0=iot16[:, :],
                        scalar1=idx4f[:, q * K4 + k:q * K4 + k + 1],
                        scalar2=None, op0=ALU.is_equal),
                        r=[("iot16",), ("idx4", q)], w=[("A", q)])
                else:
                    Sc.cop("dve", lambda v, q=q, k=k: v.tensor_scalar(
                        out=p_sb[:, q * E:(q + 1) * E] if False
                        else smalls[:, 0:16] if False else ixt[:, 0:4] if False
                        else scr[0:128, 1024:1040], in0=iot16[:, :],
                        scalar1=idx4f[:, q * K4 + k:q * K4 + k + 1],
                        scalar2=None, op0=ALU.is_equal),
                        r=[("iot16",), ("idx4", q)], w=[("scrE",)])
                    Sc.cop("dve", lambda v, dst=dst, q=q: v.tensor_add(
                        dst, dst, scr[0:128, 1024:1040]),
                        r=[("A", q), ("scrE",)], w=[("A", q)])
        for q in range(NT):
            Sc.cop("pe", lambda t, q=q: t.transpose(
                pg[5][0:16, 0:128], a_tok[:, q * E:(q + 1) * E], identf[:, :]),
                r=[("A", q), ("identf",)], w=[("pg6",)])
            Sc.cop("dve", lambda v, q=q: v.tensor_copy(
                at_loc[0:16, q * 128:(q + 1) * 128], pg[5][0:16, 0:128]),
                r=[("pg6",)], w=[("atl",)])
        Sc.dma("sp", lambda s: s.dma_start(out=ap["at_in"][:, :], in_=at_loc[:, :]),
               r=[("atl",)], w=[("at_in",)], sem="atw")
        Sc.dma("pool", lambda g: g.collective_compute(
            "AllGather", ALU.bypass, replica_groups=[list(range(NC))],
            ins=[ap["at_in"].opt()], outs=[ap["at_out"].opt()]),
            r=[("at_in",)], w=[("at_out",)], sem="ccA%d" % inv, inc=1)

    def emit_gateup(wsrc_g, goff, wsrc_u, uoff, n_of):
        ogs = []
        o = 0
        while o < n_of:
            m = min(2, n_of - o)
            ogs.append((o, m))
            o += m
        coff = 0
        for oi, (o, m) in enumerate(ogs):
            slot = oi % 2
            fw = KC * m * 128
            for wsrc, base, wview, wkeys, wsem in (
                    (wsrc_g, goff, wAv(slot), wAk[slot], "wA"),
                    (wsrc_u, uoff, wBv(slot), wBk[slot], "wB")):
                Sc.dma("pool", lambda g, wsrc=wsrc, base=base, wview=wview,
                       coff=coff, fw=fw:
                       g.dma_start(out=wview[:, 0:fw],
                                   in_=wsrc[:, base + coff: base + coff + fw]),
                       w=wkeys, sem=wsem + str(slot))
            coff += fw
            for wview, wkeys, is_gate in ((wAv(slot), wAk[slot], True),
                                          (wBv(slot), wBk[slot], False)):
                for mi in range(m):
                    for tau in range(2):
                        bk = mi * 2 + tau
                        wspan = m * 128

                        def mm(t, mi=mi, tau=tau, bk=bk, wspan=wspan,
                               wview=wview):
                            last = None
                            for k in range(KC):
                                base = k * wspan + mi * 128
                                last = t.matmul(
                                    pg[bk][:, :], wview[:, base:base + 128],
                                    ufeat[:, k * TOK + tau * 512:
                                          k * TOK + (tau + 1) * 512],
                                    start=(k == 0), stop=(k == KC - 1))
                            return last
                        Sc.cop("pe", mm, r=list(wkeys) + ufk(0, KC),
                               w=[("pg%d" % bk,)])
                        c = o + mi
                        dst = big1[:, c * TOK + tau * 512:
                                   c * TOK + (tau + 1) * 512]
                        if is_gate:
                            Sc.cop("act", lambda a, bk=bk, dst=dst: a.activation(
                                out=dst, in_=pg[bk][:, :], func=AF.Silu),
                                r=[("pg%d" % bk,)], w=[("b1", c, tau)])
                        else:
                            Sc.cop("dve", lambda v, bk=bk, dst=dst: v.tensor_mul(
                                dst, dst, pg[bk][:, :]),
                                r=[("pg%d" % bk,), ("b1", c, tau)],
                                w=[("b1", c, tau)])

    def emit_down(wsrc, base, n_ic, out_cb):
        fw = n_ic * 512
        for quar in range(4):
            slot = quar % 2
            qsl = big2[:, slot * 11264: slot * 11264 + n_ic * 512]
            qkeys = b2k(slot * 11264, slot * 11264 + n_ic * 512)
            Sc.dma("pool", lambda g, quar=quar, qsl=qsl: g.dma_start(
                out=qsl[:, 0:fw],
                in_=wsrc[:, base + quar * fw: base + (quar + 1) * fw]),
                w=qkeys, sem="b2%d" % slot)
            for rt in range(NT):
                bk = 4 + (rt % 2)

                def dmm(t, rt=rt, bk=bk, qsl=qsl):
                    last = None
                    for ic in range(n_ic):
                        last = t.matmul(
                            pg[bk][:, :],
                            big1[:, ic * TOK + rt * 128: ic * TOK + rt * 128 + 128],
                            qsl[:, ic * 512:(ic + 1) * 512],
                            start=(ic == 0), stop=(ic == n_ic - 1))
                    return last
                rr = list(qkeys)
                for ic in range(n_ic):
                    rr += [("b1", ic, 0), ("b1", ic, 1)]
                Sc.cop("pe", dmm, r=rr, w=[("pg%d" % bk,)])
                out_cb(rt, quar, bk)

    def emit_meta(inv):
        Sc.cop("dve", lambda v: v.tensor_tensor_scan(
            out=at_loc2[:, :], data0=at_loc[:, :], data1=at_loc[:, :],
            initial=0.0, op0=ALU.add, op1=ALU.bypass),
            r=[("atl",)], w=[("atl2",)])
        Sc.cop("dve", lambda v: v.tensor_sub(at_loc2[:, :], at_loc2[:, :],
               at_loc[:, :]),
               r=[("atl",), ("atl2",)], w=[("atl2",)])
        nchunk = T // CKW
        for ck in range(nchunk):
            par = ck % 2
            rnk, hf = ck // 2, ck % 2
            Sc.dma("sp", lambda s, rnk=rnk, hf=hf: s.dma_start(
                out=scA[:, :],
                in_=ap["at_out"][rnk * 16:(rnk + 1) * 16,
                                 hf * CKW:(hf + 1) * CKW]),
                r=[("at_out",)], w=[("scA",)], sem="atc%d" % par)
            Sc.cop("dve", lambda v, ck=ck: v.tensor_reduce(
                colsc[0:16, ck:ck + 1], scA[:, :], axis=AX.X, op=ALU.add),
                r=[("scA",)], w=[("cnt", ck)])
            if ck == 0:
                Sc.cop("dve", lambda v, par=par: v.tensor_tensor_scan(
                    out=scB[:, par * CKW:(par + 1) * CKW], data0=scA[:, :],
                    data1=scA[:, :], initial=0.0, op0=ALU.add, op1=ALU.bypass),
                    r=[("scA",)], w=[("scB", par)])
            else:
                pp = (ck - 1) % 2
                Sc.cop("dve", lambda v, par=par, pp=pp: v.tensor_tensor_scan(
                    out=scB[:, par * CKW:(par + 1) * CKW], data0=scA[:, :],
                    data1=scA[:, :],
                    initial=scB[:, pp * CKW + CKW - 1: pp * CKW + CKW],
                    op0=ALU.add, op1=ALU.bypass),
                    r=[("scA",), ("scB", pp)], w=[("scB", par)])
            Sc.cop("dve", lambda v, par=par: v.tensor_sub(
                scC[:, :], scB[:, par * CKW:(par + 1) * CKW], scA[:, :]),
                r=[("scA",), ("scB", par)], w=[("scC",)])
            Sc.cop("dve", lambda v: v.tensor_scalar(
                out=scr[0:16, 1024:1024 + CKW], in0=scC[:, :],
                scalar1=float(CAP), scalar2=None, op0=ALU.is_lt),
                r=[("scC",)], w=[("scrE",)])
            Sc.cop("dve", lambda v: v.tensor_mul(scA[:, :], scA[:, :],
                   scr[0:16, 1024:1024 + CKW]),
                   r=[("scA",), ("scrE",)], w=[("scA",)])
            Sc.cop("dve", lambda v: v.tensor_scalar(
                out=scC[:, :], in0=scC[:, :], scalar1=1.0, scalar2=None,
                op0=ALU.add),
                r=[("scC",)], w=[("scC",)])
            Sc.cop("dve", lambda v: v.tensor_mul(scC[:, :], scC[:, :],
                   scA[:, :]),
                   r=[("scA",), ("scC",)], w=[("scC",)])
            Sc.cop("dve", lambda v: v.tensor_scalar(
                out=scC[:, :], in0=scC[:, :], scalar1=1.0, scalar2=None,
                op0=ALU.subtract),
                r=[("scC",)], w=[("scC",)])
            Sc.cop("pe", lambda t: t.matmul(
                pg[5][0:2, :], selT_sb[:, :], scC[:, :],
                start=True, stop=True),
                r=[("scC",), ("selT",)], w=[("pg5",)])
            Sc.cop("dve", lambda v: v.tensor_copy(scr[0:2, 0:CKW],
                   pg[5][0:2, :]),
                   r=[("pg5",)], w=[("msl",)])
            for i4 in range(CKW // 128):
                tt = ck * (CKW // 128) + i4
                Sc.cop("pe", lambda t, i4=i4: t.transpose(
                    pg[5][0:128, 16:18],
                    scr[0:2, i4 * 128:(i4 + 1) * 128], identf[0:2, 0:2]),
                    r=[("msl",), ("identf",)], w=[("pg5",)])
                Sc.cop("dve", lambda v, tt=tt: v.tensor_copy(
                    slotcols[:, tt * 2:(tt + 1) * 2], pg[5][0:128, 16:18]),
                    r=[("pg5",)], w=[("slc", tt)])
        Sc.cop("dve", lambda v: v.tensor_mul(
            scr[0:16, 1024:1040], colsc[0:16, 0:16], pm_sb[0:16, :]),
            r=[("pm",)] + [("cnt", ck2) for ck2 in range(16)], w=[("scrE",)])
        Sc.cop("dve", lambda v: v.tensor_reduce(
            colsc[0:16, 15:16] if False else smalls[0:16, 15:16] if False
            else colsc[0:16, 0:1], scr[0:16, 1024:1040], axis=AX.X, op=ALU.add),
            r=[("scrE",)], w=[("pfx",)])
        Sc.cop("dve", lambda v: v.tensor_scalar(
            out=at_loc2[:, :], in0=at_loc2[:, :], scalar1=colsc[0:16, 0:1],
            scalar2=None, op0=ALU.add),
            r=[("atl2",), ("pfx",)], w=[("atl2",)])
        for q in range(NT):
            Sc.cop("pe", lambda t, q=q: t.transpose(
                pg[5][:, 0:16], at_loc2[:, q * 128:(q + 1) * 128],
                identf[0:16, 0:16]),
                r=[("atl2",), ("identf",)], w=[("pg6",)])
            Sc.cop("dve", lambda v, q=q: v.tensor_copy(
                slot_tok[:, q * E:(q + 1) * E], pg[5][0:128, 0:16]),
                r=[("pg6",)], w=[("slot", q)])
        for q in range(NT):
            for k in range(K4):
                col = q * K4 + k
                Sc.cop("dve", lambda v, q=q, col=col: v.tensor_scalar(
                    out=scr[0:128, 1024:1040], in0=iot16[:, :],
                    scalar1=idx4f[:, col:col + 1], scalar2=None,
                    op0=ALU.is_equal),
                    r=[("iot16",), ("idx4", q)], w=[("scrE",)])
                Sc.cop("dve", lambda v, q=q: v.tensor_mul(
                    scr[0:128, 1024:1040], scr[0:128, 1024:1040],
                    slot_tok[:, q * E:(q + 1) * E]),
                    r=[("scrE",), ("slot", q)], w=[("scrE",)])
                Sc.cop("dve", lambda v: v.tensor_reduce(
                    smalls[:, 3:4], scr[0:128, 1024:1040], axis=AX.X,
                    op=ALU.add),
                    r=[("scrE",)], w=[("sm", 3)])
                Sc.cop("dve", lambda v: v.tensor_scalar(
                    out=smalls[:, 4:5], in0=smalls[:, 3:4], scalar1=float(CAP),
                    scalar2=None, op0=ALU.is_lt),
                    r=[("sm", 3)], w=[("sm", 4)])
                Sc.cop("dve", lambda v, col=col, q=q: v.tensor_copy(
                    ismall[:, 0:1], idx4f[:, col:col + 1]),
                    r=[("idx4", q)], w=[("ism",)])
                Sc.cop("dve", lambda v: v.tensor_scalar(
                    out=ismall[:, 1:2], in0=ismall[:, 0:1], scalar1=1,
                    scalar2=None, op0=ALU.bitwise_and),
                    r=[("ism",)], w=[("ism",)])
                Sc.cop("dve", lambda v: v.tensor_copy(
                    smalls[:, 6:7], ismall[:, 1:2]),
                    r=[("ism",)], w=[("sm", 6)])
                Sc.cop("dve", lambda v, col=col, q=q: v.tensor_scalar(
                    out=smalls[:, 5:6], in0=idx4f[:, col:col + 1],
                    scalar1=smalls[:, 6:7], scalar2=512.0,
                    op0=ALU.subtract, op1=ALU.mult),
                    r=[("idx4", q), ("sm", 6)], w=[("sm", 5)])
                Sc.cop("dve", lambda v: v.tensor_scalar(
                    out=smalls[:, 6:7], in0=smalls[:, 6:7], scalar1=8192.0,
                    scalar2=smalls[:, 3:4], op0=ALU.mult, op1=ALU.add),
                    r=[("sm", 6), ("sm", 3)], w=[("sm", 6)])
                Sc.cop("dve", lambda v: v.tensor_add(
                    smalls[:, 5:6], smalls[:, 5:6], smalls[:, 6:7]),
                    r=[("sm", 5), ("sm", 6)], w=[("sm", 5)])
                Sc.cop("dve", lambda v: v.tensor_mul(
                    smalls[:, 5:6], smalls[:, 5:6], smalls[:, 4:5]),
                    r=[("sm", 4), ("sm", 5)], w=[("sm", 5)])
                Sc.cop("dve", lambda v, col=col: v.tensor_copy(
                    gidx[:, col:col + 1], smalls[:, 5:6]),
                    r=[("sm", 5)], w=[("gidx", q)])
                Sc.cop("dve", lambda v, col=col, q=q: v.tensor_scalar(
                    out=wfin[:, col:col + 1], in0=w4[:, col:col + 1],
                    scalar1=smalls[:, 4:5], scalar2=0.5,
                    op0=ALU.mult, op1=ALU.mult),
                    r=[("w4", q), ("sm", 4)], w=[("wfin", q)])
        Sc.cop("pool", lambda g: g.iota(scr[:, 1024:2048], pattern=[[1, 1024]],
               base=0, channel_multiplier=0,
               allow_small_or_imprecise_dtypes=True),
               w=[("scr",), ("scrE",), ("msl",)])
        Sc.alias([("eqr", 0), ("eqr", 1), ("eqr", 2), ("eqr", 3)],
                 [("ustg", 0), ("ustg", 1)])
        for tt in range(64):
            for el in range(2):
                ering = (tt * 2 + el) % 4
                eqt = ustg[:, ering * 1024:(ering + 1) * 1024]
                eng = "dve"
                Sc.cop(eng, lambda v, eqt=eqt, tt=tt, el=el: v.tensor_scalar(
                    out=eqt, in0=scr[:, 1024:2048],
                    scalar1=slotcols[:, tt * 2 + el:tt * 2 + el + 1],
                    scalar2=None, op0=ALU.is_equal),
                    r=[("scr",), ("slc", tt)], w=[("eqr", ering)])
                for sh in range(2):
                    bk = 2 + sh * 2 + el
                    Sc.cop("pe", lambda t, tt=tt, eqt=eqt, sh=sh, bk=bk:
                           t.matmul(pg[bk][0:2, :], hilo[:, 2 * tt:2 * tt + 2],
                                    eqt[:, sh * 512:(sh + 1) * 512],
                                    start=(tt == 0), stop=(tt == 63)),
                           r=[("eqr", ering), ("hilo",)], w=[("pg%d" % bk,)])
        Sc.alias([("ustg", 0), ("ustg", 1)],
                 [("eqr", 0), ("eqr", 1), ("eqr", 2), ("eqr", 3)])
        Sc.alias([("bufrow",)], [("ustg", 0), ("ustg", 1),
                                 ("eqr", 0), ("eqr", 1), ("eqr", 2),
                                 ("eqr", 3)])
        Sc.alias([("buff", 0)], [("atl2",)])
        Sc.alias([("buff", 1)], [("atl",)])
        for el in range(2):
            for sh in range(2):
                bk = 2 + sh * 2 + el
                Sc.cop("dve", lambda v, el=el, sh=sh, bk=bk: v.tensor_copy(
                    bufrow[0:2, el * 1024 + sh * 512: el * 1024 + (sh + 1) * 512],
                    pg[bk][0:2, :]),
                    r=[("pg%d" % bk,)], w=[("bufrow",)])
        for el in range(2):
            for sh in range(2):
                Sc.cop("pe", lambda t, el=el, sh=sh: t.matmul(
                    pg[2][0:1, :], c64b[:, :],
                    bufrow[0:2, el * 1024 + sh * 512:
                           el * 1024 + (sh + 1) * 512],
                    start=True, stop=True),
                    r=[("bufrow",), ("c64b",)], w=[("pg2",)])
                Sc.cop("dve", lambda v, el=el, sh=sh: v.tensor_copy(
                    buffv[el][0:1, sh * 512:(sh + 1) * 512], pg[2][0:1, :]),
                    r=[("pg2",)], w=[("buff", el)])
        for el in range(2):
            for jj in range(8):
                Sc.cop("pe", lambda t, el=el, jj=jj: t.transpose(
                    pg[5][0:128, 20:21],
                    buffv[el][0:1, jj * 128:(jj + 1) * 128],
                    identf[0:1, 0:1]),
                    r=[("buff", el), ("identf",)], w=[("pg5",)])
                Sc.cop("dve", lambda v, el=el, jj=jj: v.tensor_copy(
                    buff8[:, el * 8 + jj: el * 8 + jj + 1], pg[5][0:128, 20:21]),
                    r=[("pg5",)], w=[("buff8",)])
        Sc.cop("dve", lambda v: v.tensor_copy(bufidx[:, :], buff8[:, :]),
               r=[("buff8",)], w=[("bufidx",)])
        Sc.alias([("ustg", 0), ("ustg", 1)], [("bufrow",)])
        Sc.alias([("atl2",)], [("buff", 0)])
        Sc.alias([("atl",)], [("buff", 1)])

    def emit_experts(inv):
        for el in range(2):
            for j in range(8):
                sl = j % 2
                st = stg[:, sl * H:(sl + 1) * H]
                col = el * 8 + j
                Sc.dma("pool", lambda g, col=col, st=st: g.indirect_dma_start(
                    out=st, out_offset=None,
                    in_=ap["ag_u"][:, :],
                    in_offset=bass.IndirectOffsetOnAxis(
                        bufidx[:, col:col + 1], 0)),
                    r=[("bufidx",), ("ag_u",)], w=[("stg", sl)],
                    sem="xgg%d" % sl)
                for k in range(KC):
                    pb = pbf2[k % 2]
                    pbk = "pbf%d" % (k % 2)
                    Sc.cop("pe", lambda t, st=st, k=k, pb=pb: t.transpose(
                        pb[:, 0:128], st[:, k * 128:(k + 1) * 128],
                        identb[:, :]),
                        r=[("stg", sl), ("identb",)], w=[(pbk,)])
                    Sc.cop("dve", lambda v, k=k, j=j, pb=pb: v.tensor_copy(
                        ufeat[:, k * TOK + j * 128: k * TOK + (j + 1) * 128],
                        pb[:, 0:128]),
                        r=[(pbk,)], w=[("uf", k)])
            emit_gateup(ap["wg"], el * (H * II // 128),
                        ap["wu"], el * (H * II // 128), II // 128)

            def eo_cb(rt, quar, bk, el=el):
                evac_store(bk,
                           ap["eo_in"][el * CAP + rt * 128:
                                       el * CAP + (rt + 1) * 128,
                                       quar * 512:(quar + 1) * 512],
                           [("eo_in", el)])
            emit_down(ap["wd"], el * (II * H // 128), II // 128, eo_cb)
            Sc.dma("pool", lambda g, el=el: g.collective_compute(
                "AllGather", ALU.bypass, replica_groups=[list(range(NC))],
                ins=[ap["eo_in"][el * CAP:(el + 1) * CAP, :].opt()],
                outs=[ap["eo_out"][el * NC * CAP:(el + 1) * NC * CAP, :].opt()]),
                r=[("eo_in", el)], w=[("eo_out", el)],
                sem="ccE%d_%d" % (inv, el), inc=1)

    def emit_combine(inv):
        for q in range(NT):
            for k in range(K4):
                sl = k % 2
                st = stg[:, sl * H:(sl + 1) * H]
                col = q * K4 + k
                Sc.dma("pool", lambda g, col=col, st=st: g.indirect_dma_start(
                    out=st, out_offset=None,
                    in_=ap["eo_out"][:, :],
                    in_offset=bass.IndirectOffsetOnAxis(gidx[:, col:col + 1], 0)),
                    r=[("gidx", q), ("eo_out", 0), ("eo_out", 1)],
                    w=[("stg", sl)],
                    sem="eg%d" % sl)
                if k == 0:
                    Sc.cop("dve", lambda v, col=col, st=st: v.tensor_scalar(
                        out=scr[:, 0:H], in0=st, scalar1=wfin[:, col:col + 1],
                        scalar2=None, op0=ALU.mult),
                        r=[("stg", sl), ("wfin", q)], w=[("scr",)])
                else:
                    Sc.cop("dve", lambda v, col=col, st=st: v.scalar_tensor_tensor(
                        out=scr[:, 0:H], in0=st, scalar=wfin[:, col:col + 1],
                        in1=scr[:, 0:H], op0=ALU.mult, op1=ALU.add),
                        r=[("stg", sl), ("wfin", q), ("scr",)], w=[("scr",)])
            xs = x_sb[:, q * H:(q + 1) * H]
            Sc.cop("dve", lambda v, xs=xs, q=q: v.tensor_add(xs, xs, scr[:, 0:H]),
                   r=[("scr",), ("x", q)], w=[("x", q)])

    def emit_ffn(inv):
        emit_ln("u_tok")
        Sc.dma("pool", lambda g: g.collective_compute(
            "AllGather", ALU.bypass, replica_groups=[list(range(NC))],
            ins=[ap["u_tok"].opt()], outs=[ap["ag_u"].opt()]),
            r=[("u_tok", q) for q in range(NT)], w=[("ag_u",)],
            sem="ccU%d" % inv, inc=1)
        emit_routing(inv)
        emit_gateup(ap["swg"], 0, ap["swu"], 0, IS // 128)

        def sh_cb(rt, quar, bk):
            xs = x_sb[:, rt * H + quar * 512: rt * H + (quar + 1) * 512]
            Sc.cop("dve", lambda v, bk=bk, xs=xs, rt=rt: v.scalar_tensor_tensor(
                out=xs, in0=pg[bk][:, :], scalar=0.5, in1=xs,
                op0=ALU.mult, op1=ALU.add),
                r=[("pg%d" % bk,), ("x", rt)], w=[("x", rt)])
        emit_down(ap["swd"], 0, IS // 128, sh_cb)
        emit_meta(inv)
        emit_experts(inv)
        emit_combine(inv)

    def emit_qkv():
        emit_ln(None)
        for og in range(16):
            slot = og % 2
            Sc.dma("pool", lambda g, og=og, slot=slot: g.dma_start(
                out=wAv(slot)[:, 0:4096],
                in_=ap["w_qkv"][:, og * 4096:(og + 1) * 4096]),
                w=wAk[slot], sem="wA%d" % slot)
            for mi in range(2):
                for tau in range(2):
                    bk = mi * 2 + tau

                    def qmm(t, slot=slot, mi=mi, tau=tau, bk=bk):
                        last = None
                        for k in range(KC):
                            base = k * 256 + mi * 128
                            last = t.matmul(
                                pg[bk][:, :], wAv(slot)[:, base:base + 128],
                                ufeat[:, k * TOK + tau * 512:
                                      k * TOK + (tau + 1) * 512],
                                start=(k == 0), stop=(k == KC - 1))
                        return last
                    Sc.cop("pe", qmm, r=list(wAk[slot]) + ufk(0, KC),
                           w=[("pg%d" % bk,)])
                    of = og * 2 + mi
                    evac_store(bk, ap["qk_in"][of * 128:(of + 1) * 128,
                                               tau * 512:(tau + 1) * 512],
                               [("qk_in",)])
        Sc.dma("pool", lambda g: g.collective_compute(
            "AllGather", ALU.bypass,
            replica_groups=[[0, 1, 2, 3], [4, 5, 6, 7]],
            ins=[ap["qk_in"].opt()], outs=[ap["qk_out"].opt()]),
            r=[("qk_in",)], w=[("qk_out",)], sem="ccQ", inc=1)
        for vog in range(4):
            slot = vog % 2
            vsl = wVv(slot)
            Sc.dma("pool", lambda g, vog=vog, vsl=vsl: g.dma_start(
                out=vsl[:, 0:8192],
                in_=ap["w_qkv"][:, 65536 + vog * 8192: 65536 + (vog + 1) * 8192]),
                w=wVk[slot], sem="wV%d" % slot)
            for rt in range(NT):
                bk = 4 + (rt % 2)

                def vmm(t, rt=rt, bk=bk, vsl=vsl):
                    last = None
                    for k in range(KC):
                        last = t.matmul(
                            pg[bk][:, :],
                            ufeat[:, k * TOK + rt * 128: k * TOK + rt * 128 + 128],
                            vsl[:, k * 512:(k + 1) * 512],
                            start=(k == 0), stop=(k == KC - 1))
                    return last
                Sc.cop("pe", vmm, r=list(wVk[slot]) + ufk(0, KC),
                       w=[("pg%d" % bk,)])
                evac_store(bk, ap["v_in"][rt * 128:(rt + 1) * 128,
                                          vog * 512:(vog + 1) * 512],
                           [("v_in",)])
        Sc.dma("pool", lambda g: g.collective_compute(
            "AllGather", ALU.bypass,
            replica_groups=[[0, 1, 2, 3], [4, 5, 6, 7]],
            ins=[ap["v_in"].opt()], outs=[ap["v_out"].opt()]),
            r=[("v_in",)], w=[("v_out",)], sem="ccV", inc=1)

    def emit_attn_core():
        Sc.alias([("kn2c",), ("qn2",)], [("atl",)])
        Sc.alias([("den",)], [("atl2",)])
        Sc.alias([("pb", i) for i in range(6)] + [("qb", 0), ("qb", 1)],
                 [("stg", 0), ("stg", 1)])
        Sc.alias([("rd", 0), ("rd", 1)], [("ustg", 0), ("ustg", 1)])
        Sc.alias([("ksq",)], [("ev", 0), ("ev", 1)])
        engs = [Sc.h[e] for e in ("pe", "act", "dve", "sp")]
        axi = AxisInfo(NC, 1)
        idxr = [nc.axis_index(axi, engines=[e.engine]) for e in engs]
        s0 = Sc.snap()
        states = []
        for c in nc.Switch(engines=engs, index=idxr, n=NC):
            Sc.restore(s0)
            b, a = c // 4, c % 4
            for h in range(HEADS):
                ksl = h % 2
                kh = ufeat[:, ksl * 4096:(ksl + 1) * 4096]
                vh = ufeat[:, 8192 + ksl * 4096: 8192 + (ksl + 1) * 4096]
                khk = ufk(4 * ksl, 4 * ksl + 4)
                vhk = ufk(8 + 4 * ksl, 8 + 4 * ksl + 4)
                Sc.dma("sp", lambda s, h=h, kh=kh: s.dma_start(
                    out=kh.rearrange("p (r t) -> p r t", t=TOK),
                    in_=ap["qk_out"].rearrange("(r f) t -> f r t", f=2 * H)
                    [2048 + h * 128: 2048 + (h + 1) * 128, :, :]),
                    r=[("qk_out",)], w=khk, sem="kk%d" % ksl)
                Sc.dma("sp", lambda s, h=h, vh=vh: s.dma_start(
                    out=vh.rearrange("p (kt d) -> p kt d", d=128),
                    in_=ap["v_out"].rearrange("(kt p) f -> p kt f", p=128)
                    [:, :, h * 128:(h + 1) * 128]),
                    r=[("v_out",)], w=vhk, sem="vv%d" % ksl)
                for j8 in range(8):
                    Sc.cop("dve", lambda v, kh=kh, j8=j8: v.tensor_mul(
                        ksq[:, :], kh[:, j8 * 512:(j8 + 1) * 512],
                        kh[:, j8 * 512:(j8 + 1) * 512]),
                        r=khk + [("ksq",)], w=[("ksq",)])
                    Sc.cop("pe", lambda t: t.matmul(
                        pg[0][0:1, :], onesb[:, 0:1], ksq[:, :],
                        start=True, stop=True),
                        r=[("ksq",), ("onesb",)], w=[("pg0",)])
                    Sc.cop("dve", lambda v: v.tensor_copy(kn2c, pg[0][0:1, :]),
                           r=[("pg0",)], w=[("kn2c",)])
                    Sc.cop("dve", lambda v, j8=j8: v.reduce_max(
                        kmax8[0:1, j8:j8 + 1], kn2c, axis=AX.X),
                        r=[("kn2c",)], w=[("km8", j8)])
                jqs = (a, 7 - a)
                nks = (4 * (a + 1), 4 * (8 - a))
                for ji, jq in enumerate(jqs):
                    nk = nks[ji]
                    rq = jq // 2
                    qh = qbuf[:, ji * 512:(ji + 1) * 512]
                    Sc.dma("sp", lambda s, rq=rq, h=h, jq=jq, qh=qh: s.dma_start(
                        out=qh,
                        in_=ap["qk_out"][rq * 2 * H + h * 128:
                                         rq * 2 * H + (h + 1) * 128,
                                         (jq % 2) * 512:((jq % 2) + 1) * 512]),
                        r=[("qk_out",)], w=[("qb", ji)], sem="qq%d" % ji)
                    Sc.cop("dve", lambda v, qh=qh: v.tensor_mul(
                        ksq[:, :], qh, qh),
                        r=[("qb", ji), ("ksq",)], w=[("ksq",)])
                    Sc.cop("pe", lambda t, ji=ji: t.matmul(
                        pg[ji][0:1, :], onesb[:, 0:1], ksq[:, :],
                        start=True, stop=True),
                        r=[("ksq",), ("onesb",)], w=[("pg%d" % ji,)])
                    Sc.cop("dve", lambda v, ji=ji: v.tensor_copy(
                        qn2, pg[ji][0:1, :]),
                        r=[("pg%d" % ji,)], w=[("qn2",)])
                    Sc.cop("dve", lambda v: v.reduce_max(
                        kmax8[0:1, 9:10], qn2, axis=AX.X),
                        r=[("qn2",)], w=[("qmx",)])
                    Sc.cop("dve", lambda v, nk=nk: v.reduce_max(
                        kmax8[0:1, 8:9], kmax8[0:1, 0:nk // 4], axis=AX.X),
                        r=[("km8", j8) for j8 in range(nk // 4)], w=[("kmx",)])
                    Sc.cop("dve", lambda v: v.tensor_scalar(
                        out=kmax8[0:1, 10:11], in0=kmax8[0:1, 9:10],
                        scalar1=kmax8[0:1, 8:9], scalar2=(1.05 * SCALE) ** 2,
                        op0=ALU.mult, op1=ALU.mult),
                        r=[("qmx",), ("kmx",)], w=[("m2s",)])
                    Sc.cop("act", lambda a_: a_.activation(
                        out=kmax8[0:1, 11:12], in_=kmax8[0:1, 10:11],
                        func=AF.Ln),
                        r=[("m2s",)], w=[("m2s",)])
                    Sc.cop("act", lambda a_: a_.activation(
                        out=kmax8[0:1, 11:12], in_=kmax8[0:1, 11:12],
                        func=AF.Exp, scale=0.5),
                        r=[("m2s",)], w=[("m2s",)])
                    Sc.cop("dve", lambda v: v.tensor_scalar(
                        out=kmax8[0:1, 11:12], in0=kmax8[0:1, 11:12],
                        scalar1=-1.0, scalar2=None, op0=ALU.mult),
                        r=[("m2s",)], w=[("m2s",)])
                    Sc.cop("pe", lambda t, ji=ji: t.matmul(
                        pg[ji][:, 0:1], onesf[0:1, :], kmax8[0:1, 11:12],
                        start=True, stop=True),
                        r=[("m2s",), ("onesf",)], w=[("pg%d" % ji,)])
                    Sc.cop("dve", lambda v, ji=ji: v.tensor_copy(
                        smalls[:, 8 + ji:9 + ji], pg[ji][:, 0:1]),
                        r=[("pg%d" % ji,)], w=[("sm", 8 + ji)])
                for step in range(max(nks) + 2):
                    for ji in range(2):
                        nk = nks[ji]
                        qh = qbuf[:, ji * 512:(ji + 1) * 512]
                        if 2 <= step < nk + 2:
                            kt = step - 2
                            prc = ji * 3 + kt % 3
                            Sc.cop("pe", lambda t, kt=kt, prc=prc, vh=vh,
                                   ji=ji, nk=nk:
                                   t.matmul(pg[2 + ji][:, :],
                                            vh[:, kt * 128:(kt + 1) * 128],
                                            pbuf[:, prc * 512:(prc + 1) * 512],
                                            start=(kt == 0),
                                            stop=(kt == nk - 1)),
                                   r=vhk + [("pb", prc)],
                                   w=[("pg%d" % (2 + ji),)])
                            Sc.cop("pe", lambda t, kt=kt, prc=prc, ji=ji,
                                   nk=nk:
                                   t.matmul(
                                       pg[4 + ji][0:1, :], onesb[:, 0:1],
                                       pbuf[:, prc * 512:(prc + 1) * 512],
                                       start=(kt == 0), stop=(kt == nk - 1)),
                                   r=[("pb", prc), ("onesb",)],
                                   w=[("pg%d" % (4 + ji),)])
                        if step < nk:
                            kt = step
                            stbk = ji
                            prc = ji * 3 + kt % 3
                            Sc.cop("pe", lambda t, kt=kt, stbk=stbk, kh=kh,
                                   qh=qh:
                                   t.matmul(pg[stbk][:, :],
                                            kh[:, kt * 128:(kt + 1) * 128], qh,
                                            start=True, stop=True),
                                   r=khk + [("qb", ji)],
                                   w=[("pg%d" % stbk,)])
                            Sc.cop("act", lambda a_, prc=prc, stbk=stbk, ji=ji:
                                   a_.activation(
                                       out=pbuf[:, prc * 512:(prc + 1) * 512],
                                       in_=pg[stbk][:, :], func=AF.Exp,
                                       scale=SCALE,
                                       bias=smalls[:, 8 + ji:9 + ji]),
                                   r=[("pg%d" % stbk,), ("sm", 8 + ji)],
                                   w=[("pb", prc)])
                            if kt >= nk - 4:
                                o = 384 - (kt - (nk - 4)) * 128
                                Sc.cop("dve", lambda v, prc=prc, o=o:
                                       v.tensor_mul(
                                           pbuf[:, prc * 512:(prc + 1) * 512],
                                           pbuf[:, prc * 512:(prc + 1) * 512],
                                           cmask[:, o:o + 512]),
                                       r=[("pb", prc), ("cmask",)],
                                       w=[("pb", prc)])
                for ji in range(2):
                    Sc.cop("dve", lambda v, ji=ji: v.tensor_copy(
                        denr, pg[4 + ji][0:1, :]),
                        r=[("pg%d" % (4 + ji),)], w=[("den",)])
                    Sc.cop("pe", lambda t, ji=ji: t.matmul(
                        pg[ji][:, :], onesf[0:1, :], denr,
                        start=True, stop=True),
                        r=[("den",), ("onesf",)], w=[("pg%d" % ji,)])
                    def _recip(v, ji=ji):
                        with nc.allow_low_precision(
                                reason="bf16 reciprocal of softmax denom"):
                            return v.reciprocal(
                                rds[:, ji * 512:(ji + 1) * 512], pg[ji][:, :])
                    Sc.cop("dve", _recip, r=[("pg%d" % ji,)], w=[("rd", ji)])
                    dst = big1[:, h * TOK + ji * 512: h * TOK + (ji + 1) * 512]
                    Sc.cop("dve", lambda v, ji=ji, dst=dst:
                           v.tensor_mul(dst, pg[2 + ji][:, :],
                                        rds[:, ji * 512:(ji + 1) * 512]),
                           r=[("pg%d" % (2 + ji),), ("rd", ji)],
                           w=[("b1", h, ji)])
            states.append(Sc.snap())
        Sc.merge(states)
        Sc.alias([("atl",)], [("kn2c",), ("qn2",)])
        Sc.alias([("atl2",)], [("den",)])
        Sc.alias([("stg", 0), ("stg", 1)],
                 [("pb", i) for i in range(6)] + [("qb", 0), ("qb", 1)])
        Sc.alias([("ustg", 0), ("ustg", 1)], [("rd", 0), ("rd", 1)])
        Sc.alias([("ev", 0), ("ev", 1)], [("ksq",)])

    def emit_outproj():
        def ao_cb(rt, quar, bk):
            evac_store(bk, ap["ao_in"][rt * 128:(rt + 1) * 128,
                                       quar * 512:(quar + 1) * 512],
                       [("ao_in",)])
        emit_down(ap["w_out"], 0, HEADS, ao_cb)
        Sc.dma("pool", lambda g: g.collective_compute(
            "AllGather", ALU.bypass,
            replica_groups=[[0, 1, 2, 3], [4, 5, 6, 7]],
            ins=[ap["ao_in"].opt()], outs=[ap["ao_out"].opt()]),
            r=[("ao_in",)], w=[("ao_out",)], sem="ccO", inc=1)

    def emit_attn_resid():
        engs = [Sc.h["sp"]]
        axi = AxisInfo(NC, 1)
        idxr = [nc.axis_index(axi, engines=[e.engine]) for e in engs]
        s0 = Sc.snap()
        states = []
        for c in nc.Switch(engines=engs, index=idxr, n=NC):
            Sc.restore(s0)
            b, a = c // 4, c % 4
            for gi, g in enumerate((2 * a, 2 * a + 1)):
                r_g = min(g, 7 - g)
                off4 = 0 if g <= 3 else 4
                Sc.dma("sp", lambda s, r_g=r_g, off4=off4, gi=gi: s.dma_start(
                    out=ufeat[:, gi * 4 * H:(gi + 1) * 4 * H]
                    .rearrange("p (q f) -> p q f", f=H),
                    in_=ap["ao_out"].rearrange("(rr p) f -> p rr f", p=128)
                    [:, r_g * 8 + off4: r_g * 8 + off4 + 4, :]),
                    r=[("ao_out",)], w=ufk(gi * 8, gi * 8 + 8),
                    sem="aop%d" % gi)
            states.append(Sc.snap())
        Sc.merge(states)
        for q in range(NT):
            gi, qq = q // 4, q % 4
            pk = ufeat[:, gi * 4 * H + qq * H: gi * 4 * H + (qq + 1) * H]
            pkk = ufk(gi * 8 + 2 * qq, gi * 8 + 2 * qq + 2)
            xs = x_sb[:, q * H:(q + 1) * H]
            Sc.cop("dve", lambda v, xs=xs, pk=pk: v.tensor_add(xs, xs, pk),
                   r=[("x", q)] + pkk, w=[("x", q)])
            Sc.cop("dve", lambda v, xs=xs: v.tensor_add(xs, xs, brep[:, :]),
                   r=[("x", q), ("brep",)], w=[("x", q)])

    def dump(name, src_ap, keys, cast=False):
        if not dbg:
            return
        q = "pool" if cast else "sp"
        Sc.dma(q, lambda s: s.dma_start(out=dbg_outs[name], in_=src_ap),
               r=keys, w=(), sem="dbg")

    emit_init()
    if "ffn0" in parts:
        emit_ffn(0)
        if dbg:
            dump("d_u", ap["u_tok"][:, :], [("u_tok", q) for q in range(NT)])
            dump("d_p", p_sb[:, :], [("p", q) for q in range(NT)])
            dump("d_idx4", idx4f[:, :], [("idx4", q) for q in range(NT)])
            dump("d_w4", w4[:, :], [("w4", q) for q in range(NT)])
            dump("d_slot", slot_tok[:, :], [("slot", q) for q in range(NT)])
            dump("d_gidx", gidx[:, :], [("gidx", q) for q in range(NT)])
            dump("d_wfin", wfin[:, :], [("wfin", q) for q in range(NT)])
            dump("d_eo", ap["eo_in"][:, :], [("eo_in", 0), ("eo_in", 1)])
            dump("d_atl", ap["at_in"][:, :], [("at_in",)])
            dump("d_bufidx", bufidx[:, :], [("bufidx",)])
            dump("d_slotc", slotcols[:, :], [("slc", tt2) for tt2 in range(64)])
            dump("d_x1", x_sb[:].rearrange("p (q f) -> p q f", f=H),
                 [("x", q) for q in range(NT)], cast=True)
    if "attn" in parts:
        emit_qkv()
        emit_attn_core()
        emit_outproj()
        emit_attn_resid()
        if dbg:
            dump("d_km", kmax8[:, :], [("m2s",), ("kmx",), ("qmx",)])
            dump("d_negm", smalls[:, 8:10], [("sm", 8), ("sm", 9)])
            dump("d_x2", x_sb[:].rearrange("p (q f) -> p q f", f=H),
                 [("x", q) for q in range(NT)], cast=True)
    if "ffn1" in parts:
        emit_ffn(1)
    for q in range(NT):
        Sc.dma("pool", lambda g, q=q: g.dma_start(
            out=ap["y"][q * 128:(q + 1) * 128, :],
            in_=x_sb[:, q * H:(q + 1) * H]),
            r=[("x", q)], w=(), sem="yo")
    Sc.h["pool"].wait_ge(Sc.sems["yo"], Sc.cnt["yo"])
    return nc


def _gu_blocks(n_of):
    ogs, o = [], 0
    while o < n_of:
        m = min(2, n_of - o)
        ogs.append((o, m))
        o += m
    return ogs


def pack_gateup(w, width=None):
    Wd = w.shape[1]
    a = w.reshape(KC, 128, Wd)
    blocks = []
    for (o, m) in _gu_blocks(Wd // 128):
        blocks.append(np.ascontiguousarray(
            a[:, :, o * 128:(o + m) * 128].transpose(1, 0, 2).reshape(128, -1)))
    return np.ascontiguousarray(np.concatenate(blocks, axis=1))


def pack_down(w):
    n_ic = w.shape[0] // 128
    a = w.reshape(n_ic, 128, w.shape[1])
    blocks = [np.ascontiguousarray(
        a[:, :, q * 512:(q + 1) * 512].transpose(1, 0, 2).reshape(128, -1))
        for q in range(4)]
    return np.ascontiguousarray(np.concatenate(blocks, axis=1))


def pack_qkv(w):
    a = w.reshape(KC, 128, 3 * H)
    blocks = []
    for og in range(16):
        blocks.append(np.ascontiguousarray(
            a[:, :, og * 256:(og + 1) * 256].transpose(1, 0, 2)
            .reshape(128, -1)))
    for vog in range(4):
        blocks.append(np.ascontiguousarray(
            a[:, :, 2 * H + vog * 512: 2 * H + (vog + 1) * 512]
            .transpose(1, 0, 2).reshape(128, -1)))
    return np.ascontiguousarray(np.concatenate(blocks, axis=1))


def make_inmaps(inputs):
    X = np.ascontiguousarray(np.asarray(inputs["L0_attn_x"],
                                        dtype=np.float32).reshape(T, H))
    wg = np.ascontiguousarray(np.asarray(inputs["L0_ffn_w_gate"], np.float32))
    wu = np.ascontiguousarray(np.asarray(inputs["L0_ffn_w_up"], np.float32))
    wd = np.ascontiguousarray(np.asarray(inputs["L0_ffn_w_down"], np.float32))
    w_qkv = np.ascontiguousarray(np.asarray(inputs["L0_attn_w_qkv"], np.float32))
    w_out = np.ascontiguousarray(np.asarray(inputs["L0_attn_w_out"], np.float32))
    b_out = np.ascontiguousarray(np.asarray(inputs["L0_attn_b_out"],
                                            np.float32).reshape(1, H))
    gate_w = np.ascontiguousarray(np.asarray(inputs["L0_ffn_gate_w"], np.float32))
    swg = np.ascontiguousarray(np.asarray(inputs["L0_ffn_sw_gate"], np.float32))
    swu = np.ascontiguousarray(np.asarray(inputs["L0_ffn_sw_up"], np.float32))
    swd = np.ascontiguousarray(np.asarray(inputs["L0_ffn_sw_down"], np.float32))
    swg_p = pack_gateup(swg)
    swu_p = pack_gateup(swu)
    swd_p = pack_down(swd)
    wqkv_p = pack_qkv(w_qkv)
    wout_p = pack_down(w_out)
    c64v = np.array([[64.0], [1.0]], np.float32)
    in_maps = []
    for c in range(NC):
        selT = np.zeros((E, 2), np.float32)
        selT[2 * c, 0] = 1.0
        selT[2 * c + 1, 1] = 1.0
        pm = np.zeros((E, 16), np.float32)
        pm[:, :2 * c] = 1.0
        wg2 = wg.reshape(E, H, II)[2 * c:2 * c + 2]
        wu2 = wu.reshape(E, H, II)[2 * c:2 * c + 2]
        wd2 = wd.reshape(E, II, H)[2 * c:2 * c + 2]
        xr = np.ascontiguousarray(
            X[c * TOK:(c + 1) * TOK].reshape(NT, 128, H)
            .transpose(1, 0, 2).reshape(128, NT * H))
        in_maps.append({
            "x": xr,
            "w_qkv": wqkv_p, "w_out": wout_p, "b_out": b_out,
            "gate_w": gate_w,
            "wg": np.concatenate([pack_gateup(wg2[e]) for e in range(2)],
                                 axis=1),
            "wu": np.concatenate([pack_gateup(wu2[e]) for e in range(2)],
                                 axis=1),
            "wd": np.concatenate([pack_down(wd2[e]) for e in range(2)],
                                 axis=1),
            "swg": swg_p, "swu": swu_p, "swd": swd_p,
            "selT": selT, "pm": pm, "c64": c64v,
        })
    return in_maps


def kernel(**inputs):
    nc = build()
    in_maps = make_inmaps(inputs)
    res = run_bass_kernel_spmd(nc, in_maps, core_ids=list(range(NC)))
    y = np.concatenate([res.results[c]["y"] for c in range(NC)], axis=0)
    return y.reshape(B, S, H).astype(np.float32)


# version 08 (final)
# speedup vs baseline: 1.0319x; 1.0319x over previous
import numpy as np
import concourse.bass as bass
import concourse.mybir as mybir
from concourse.bass_types import AxisInfo
from concourse.bass_utils import run_bass_kernel_spmd

F32 = mybir.dt.float32
BF16 = mybir.dt.bfloat16
I32 = mybir.dt.int32
U32 = mybir.dt.uint32
AF = mybir.ActivationFunctionType
ALU = mybir.AluOpType
AX = mybir.AxisListType

NC = 8
B, S, H = 2, 4096, 2048
T = B * S
TOK = T // NC
E, K4, II, IS, CAP = 16, 4, 1408, 2816, 1024
HEADS, DH = 16, 128
KC = H // 128
NT = TOK // 128
QKR = 2 * H + 32
SCALE = DH ** -0.5
EPS = 1e-5
DUMP = 2 * CAP
CKW = 512


class Sched:
    ENG = ("pe", "act", "dve", "pool", "sp")

    def __init__(self, nc):
        self.nc = nc
        self.h = {"pe": nc.tensor, "act": nc.scalar, "dve": nc.vector,
                  "pool": nc.gpsimd, "sp": nc.sync}
        self.sems = {}
        self.cnt = {}
        for e in self.ENG:
            self.sems[e] = nc.alloc_semaphore("sem_" + e)
            self.cnt[e] = 0
        self.lw = {}
        self.lr = {}
        self.waited = {e: {} for e in self.ENG}

    def sem(self, name):
        if name not in self.sems:
            self.sems[name] = self.nc.alloc_semaphore("sem_" + name)
        if name not in self.cnt:
            self.cnt[name] = 0
        return self.sems[name]

    def _wait(self, eng, sname, val):
        if val <= 0:
            return
        if eng == "pe" and sname == "pe":
            return
        if self.waited[eng].get(sname, 0) >= val:
            return
        self.h[eng].wait_ge(self.sems[sname], val)
        self.waited[eng][sname] = val

    def _pre(self, eng, r, w):
        for k in r:
            for sname, val in self.lw.get(k, {}).items():
                self._wait(eng, sname, val)
        for k in w:
            for sname, val in self.lw.get(k, {}).items():
                self._wait(eng, sname, val)
            for sname, val in self.lr.get(k, {}).items():
                self._wait(eng, sname, val)

    def _post(self, sname, val, r, w):
        for k in w:
            self.lw[k] = {sname: val}
            self.lr[k] = {}
        for k in r:
            d = self.lr.setdefault(k, {})
            if d.get(sname, 0) < val:
                d[sname] = val

    def cop(self, eng, emit, r=(), w=(), inc=1):
        self._pre(eng, r, w)
        inst = emit(self.h[eng])
        self.cnt[eng] += inc
        inst.then_inc(self.sems[eng], inc)
        self._post(eng, self.cnt[eng], r, w)

    def dma(self, q, emit, r=(), w=(), sem=None, inc=16):
        sname = sem or q
        self.sem(sname)
        self._pre(q, r, w)
        inst = emit(self.h[q])
        self.cnt[sname] += inc
        inst.then_inc(self.sems[sname], inc)
        self._post(sname, self.cnt[sname], r, w)

    def alias(self, dst_keys, src_keys):
        agg = {}
        for k in src_keys:
            for d in (self.lw.get(k, {}), self.lr.get(k, {})):
                for sn, v in d.items():
                    if agg.get(sn, 0) < v:
                        agg[sn] = v
        for k in dst_keys:
            self.lw[k] = dict(agg)
            self.lr[k] = {}

    def snap(self):
        import copy
        return (dict(self.cnt), copy.deepcopy(self.lw), copy.deepcopy(self.lr),
                copy.deepcopy(self.waited))

    def restore(self, s):
        import copy
        self.cnt = dict(s[0])
        self.lw = copy.deepcopy(s[1])
        self.lr = copy.deepcopy(s[2])
        self.waited = copy.deepcopy(s[3])

    def merge(self, states):
        cnts = [s[0] for s in states]
        for c in cnts[1:]:
            if c != cnts[0]:
                diff = {k: (cnts[0].get(k), c.get(k))
                        for k in set(cnts[0]) | set(c)
                        if cnts[0].get(k) != c.get(k)}
                raise AssertionError(f"switch branches unbalanced: {diff}")
        self.cnt = dict(cnts[0])
        lw, lr = {}, {}
        for s in states:
            for k, d in s[1].items():
                dd = lw.setdefault(k, {})
                for sn, val in d.items():
                    if dd.get(sn, 0) < val:
                        dd[sn] = val
            for k, d in s[2].items():
                dd = lr.setdefault(k, {})
                for sn, val in d.items():
                    if dd.get(sn, 0) < val:
                        dd[sn] = val
        self.lw, self.lr = lw, lr
        wt = {}
        for e in Sched.ENG:
            wt[e] = {}
            allk = set()
            for s in states:
                allk |= set(s[3][e].keys())
            for k in allk:
                vals = [s[3][e].get(k, 0) for s in states]
                if min(vals) > 0:
                    wt[e][k] = min(vals)
        self.waited = wt


def b2k(c0, c1):
    return [("b2g", g) for g in range(c0 // 512, (c1 + 511) // 512)]


def build(parts=("ffn0", "attn", "ffn1"), dbg=False):
    nc = bass.Bass("TRN2", target_bir_lowering=False)
    ap = {}

    def din(name, shape, dt=F32):
        ap[name] = nc.dram_tensor(name, shape, dt, kind="ExternalInput").ap()

    def dout(name, shape, dt=F32):
        ap[name] = nc.dram_tensor(name, shape, dt, kind="ExternalOutput").ap()

    def dint(name, shape, dt):
        ap[name] = nc.dram_tensor(name, shape, dt, kind="Internal").ap()

    din("x", [128, NT * H])
    din("w_qkv", [128, 3 * H * KC]); din("w_out", [128, H * KC])
    din("b_out", [1, H])
    din("gate_w", [E, H])
    din("wg", [128, 2 * H * II // 128]); din("wu", [128, 2 * H * II // 128])
    din("wd", [128, 2 * II * H // 128])
    din("swg", [128, H * IS // 128]); din("swu", [128, H * IS // 128])
    din("swd", [128, IS * H // 128])
    din("selT", [E, 2]); din("pm", [E, 16]); din("c64", [2, 1])
    dout("y", [TOK, H])

    dint("u_tok", [TOK, H], BF16)
    dint("ag_u", [T, H], BF16)
    dint("at_in", [E, TOK], F32)
    dint("at_out", [NC * E, TOK], F32)
    dint("eo_in", [2 * CAP, H], BF16)
    dint("eo_out", [E * CAP, H], BF16)
    dint("vln", [TOK, H], BF16)
    dint("qk_in", [QKR, TOK], BF16)
    dint("qk_out", [4 * QKR, TOK], BF16)
    dint("v_in", [TOK, H], BF16)
    dint("v_out", [4 * TOK, H], BF16)
    dint("ao_in", [TOK, H], BF16)
    dint("ao_out", [4 * TOK, H], BF16)

    dbg_outs = {}
    if dbg:
        for name, shape, dt in [
            ("d_u", [TOK, H], BF16), ("d_p", [TOK, E], F32),
            ("d_idx4", [TOK, K4], F32), ("d_w4", [TOK, K4], F32),
            ("d_slot", [TOK, E], F32), ("d_gidx", [TOK, K4], I32),
            ("d_wfin", [TOK, K4], F32),
            ("d_x1", [TOK, H], F32), ("d_x2", [TOK, H], F32),
            ("d_eo", [2 * CAP, H], BF16), ("d_atl", [E, TOK], F32),
            ("d_bufidx", [128, 16], I32), ("d_slotc", [128, 128], F32),
            ("d_km", [1, 16], F32), ("d_negm", [128, 2], F32),
        ]:
            dout(name, shape, dt)
            dbg_outs[name] = ap[name]

    def asb(name, shape, dt):
        return nc.alloc_sbuf_tensor(name, shape, dt)

    x_sb = asb("x_sb", [128, NT * H], BF16)
    ufeat = asb("ufeat", [128, KC * TOK], BF16)
    big1 = asb("big1", [128, 22 * TOK], BF16)
    big2 = asb("big2", [128, 2 * 11264], BF16)
    stg = asb("stg", [128, 2 * H], BF16)
    ustg = asb("ustg", [128, 2 * H], BF16)
    evst = asb("evst", [128, 2 * 512], BF16)
    brep = asb("brep", [128, H], BF16)
    scr = asb("scr", [128, 2048], F32)
    iotc = asb("iotc", [128, 1], F32)
    iot16 = asb("iot16", [128, 16], F32)
    identf = asb("identf", [128, 128], F32)
    identb = asb("identb", [128, 128], BF16)
    onesb = asb("onesb", [128, 128], BF16)
    cmask = asb("cmask", [128, 896], BF16)
    gwT = asb("gwT", [128, KC * E], BF16)
    tokid = asb("tokid", [128, 128], I32)
    w4 = asb("w4", [128, NT * K4], F32)
    idx4f = asb("idx4f", [128, NT * K4], F32)
    a_tok = asb("a_tok", [128, NT * E], F32)
    slot_tok = asb("slot_tok", [128, NT * E], F32)
    gidx = asb("gidx", [128, NT * K4], I32)
    wfin = asb("wfin", [128, NT * K4], F32)
    p_sb = asb("p_sb", [128, NT * E], F32)
    mx8 = asb("mx8", [128, 8], F32)
    ixt = asb("ixt", [128, 8], U32)
    smalls = asb("smalls", [128, 16], F32)
    at_loc = asb("at_loc", [16, TOK], F32)
    at_loc2 = asb("at_loc2", [16, TOK], F32)
    colsc = asb("colsc", [16, 16], F32)
    scA = asb("scA", [16, CKW], F32)
    scB = asb("scB", [16, 2 * CKW], F32)
    scC = asb("scC", [16, CKW], F32)
    selT_sb = asb("selT_sb", [16, 2], F32)
    pm_sb = asb("pm_sb", [16, 16], F32)
    c64_sb = asb("c64_sb", [2, 1], F32)
    c64b = asb("c64b", [2, 1], BF16)
    onesf = asb("onesf", [1, 128], F32)
    slotcols = asb("slotcols", [128, 128], F32)
    tokhi = asb("tokhi", [128, 64], BF16)
    toklo = asb("toklo", [128, 1], F32)
    hilo = asb("hilo", [128, 128], BF16)

    bufidx = asb("bufidx", [128, 16], I32)
    ismall = asb("ismall", [128, 2], I32)
    buff8 = asb("buff8", [128, 16], F32)
    norm48 = asb("norm48", [1, 64], F32)
    norm48b = asb("norm48b", [1, 64], BF16)
    normsc = asb("normsc", [1, 384], F32)
    pbuf = stg[:, 0:3072]
    qbuf = stg[:, 3072:4096]
    rds = ustg[:, 0:1024]
    ksq = evst[:, 0:512]
    kn2c = at_loc[0:1, 0:512]
    qn2 = at_loc[0:1, 512:1024]
    denr = at_loc2[0:1, 0:512]
    bufrow = ustg[0:2, 0:2048]
    buffv = [at_loc2[0:1, 0:1024], at_loc[0:1, 0:1024]]
    mrow = ustg[0:1, 0:512]
    gwstg = big1[0:16, 0:2048]
    boutstg = big1[0:1, 2048:4096]

    def wAv(slot):
        return big2[:, slot * 4096:(slot + 1) * 4096]

    def wBv(slot):
        return big2[:, 8192 + slot * 4096: 8192 + (slot + 1) * 4096]

    def wVv(slot):
        return big2[:, slot * 8192:(slot + 1) * 8192]

    wAk = [b2k(0, 4096), b2k(4096, 8192)]
    wBk = [b2k(8192, 12288), b2k(12288, 16384)]
    wVk = [b2k(0, 8192), b2k(8192, 16384)]

    pg = [nc.alloc_psum_tensor(f"pg{i}", [128, 512], F32) for i in range(6)]
    pbf2 = [nc.alloc_psum_tensor(f"pbf{i}", [128, 1024], BF16) for i in range(2)]
    pbf = pbf2[0]

    Sc = Sched(nc)
    evctr = [0]

    def evac_store(bk, out_dram_ap, dkeys):
        sl = evctr[0] % 2
        evctr[0] += 1
        ev = evst[:, sl * 512:(sl + 1) * 512]
        Sc.cop("act", lambda a: a.activation(out=ev, in_=pg[bk][:, :],
                                             func=AF.Copy),
               r=[("pg%d" % bk,)], w=[("ev", sl)])
        Sc.dma("sp", lambda s: s.dma_start(out=out_dram_ap, in_=ev),
               r=[("ev", sl)], w=dkeys, sem="evd%d" % sl)

    def ufk(c0, c1):
        return [("uf", k) for k in range(c0, c1)]

    def emit_init():
        Sc.dma("pool", lambda g: g.dma_start(out=x_sb[:, :], in_=ap["x"]),
               w=[("x", q) for q in range(NT)], sem="xin")
        Sc.cop("pool", lambda g: g.iota(iotc[:, :], pattern=[[0, 1]], base=0,
               channel_multiplier=1, allow_small_or_imprecise_dtypes=True),
               w=[("iotc",)])
        Sc.cop("pool", lambda g: g.iota(iot16[:, :], pattern=[[1, 16]], base=0,
               channel_multiplier=0, allow_small_or_imprecise_dtypes=True),
               w=[("iot16",)])
        Sc.cop("pool", lambda g: g.iota(scr[:, 0:896], pattern=[[1, 896]], base=0,
               channel_multiplier=0, allow_small_or_imprecise_dtypes=True),
               w=[("scr",)])
        Sc.cop("pool", lambda g: g.iota(scr[:, 1024:1088], pattern=[[2, 64]],
               base=0, channel_multiplier=0,
               allow_small_or_imprecise_dtypes=True),
               w=[("scrE",)])
        Sc.cop("dve", lambda v: v.tensor_scalar(out=identf[:, :], in0=scr[:, 0:128],
               scalar1=iotc[:, 0:1], scalar2=None, op0=ALU.is_equal),
               r=[("scr",), ("iotc",)], w=[("identf",)])
        Sc.cop("dve", lambda v: v.tensor_copy(identb[:, :], identf[:, :]),
               r=[("identf",)], w=[("identb",)])
        Sc.cop("dve", lambda v: v.memset(onesb[:, :], 1.0), w=[("onesb",)])
        Sc.cop("dve", lambda v: v.memset(onesf[:, :], 1.0), w=[("onesf",)])
        Sc.cop("dve", lambda v: v.tensor_scalar(out=smalls[:, 2:3],
               in0=iotc[:, 0:1], scalar1=63.5, scalar2=None, op0=ALU.is_gt),
               r=[("iotc",)], w=[("sm", 2)])
        Sc.cop("dve", lambda v: v.tensor_scalar(out=tokhi[:, :],
               in0=scr[:, 1024:1088], scalar1=smalls[:, 2:3], scalar2=None,
               op0=ALU.add),
               r=[("scrE",), ("sm", 2)], w=[("tokhi",)])
        Sc.cop("dve", lambda v: v.tensor_scalar(out=smalls[:, 3:4],
               in0=smalls[:, 2:3], scalar1=64.0, scalar2=None, op0=ALU.mult),
               r=[("sm", 2)], w=[("sm", 3)])
        Sc.cop("dve", lambda v: v.tensor_sub(smalls[:, 4:5], iotc[:, 0:1],
               smalls[:, 3:4]),
               r=[("iotc",), ("sm", 3)], w=[("sm", 4)])
        Sc.cop("dve", lambda v: v.tensor_copy(toklo[:, :], smalls[:, 4:5]),
               r=[("sm", 4)], w=[("toklo",)])
        Sc.cop("dve", lambda v: v.tensor_copy(
            hilo[:, :].rearrange("p (t two) -> p t two", two=2)[:, :, 0],
            tokhi[:, :]),
            r=[("tokhi",)], w=[("hilo",)])
        Sc.cop("dve", lambda v: v.tensor_scalar(
            out=hilo[:, :].rearrange("p (t two) -> p t two", two=2)[:, :, 1],
            in0=onesb[:, 0:64], scalar1=toklo[:, 0:1], scalar2=None,
            op0=ALU.mult),
            r=[("onesb",), ("toklo",)], w=[("hilo",)])

        Sc.cop("dve", lambda v: v.tensor_scalar(out=smalls[:, 0:1], in0=iotc[:, 0:1],
               scalar1=384.0, scalar2=None, op0=ALU.add),
               r=[("iotc",)], w=[("sm", 0)])
        Sc.cop("dve", lambda v: v.tensor_scalar(out=cmask[:, :], in0=scr[:, 0:896],
               scalar1=smalls[:, 0:1], scalar2=None, op0=ALU.is_ge),
               r=[("scr",), ("sm", 0)], w=[("cmask",)])

        for i, (nm, tile, key) in enumerate([
                ("selT", selT_sb, "selT"), ("pm", pm_sb, "pm"),
                ("c64", c64_sb, "c64")]):
            Sc.dma("sp", lambda s, nm=nm, tile=tile: s.dma_start(
                out=tile[:, :], in_=ap[nm]), w=[(key,)], sem="sml%d" % i)
        Sc.dma("pool", lambda g: g.dma_start(out=gwstg, in_=ap["gate_w"]),
               w=[("b1", 0, 0), ("b1", 0, 1), ("b1", 1, 0), ("b1", 1, 1)],
               sem="smlg")
        Sc.dma("pool", lambda g: g.dma_start(out=boutstg, in_=ap["b_out"]),
               w=[("b1", 2, 0), ("b1", 2, 1), ("b1", 3, 0), ("b1", 3, 1)],
               sem="smlb")
        Sc.cop("dve", lambda v: v.tensor_copy(c64b[:, :], c64_sb[:, :]),
               r=[("c64",)], w=[("c64b",)])
        for k in range(KC):
            pb = pbf2[k % 2]
            pbk = "pbf%d" % (k % 2)
            Sc.cop("pe", lambda t, k=k, pb=pb: t.transpose(
                pb[:, 0:16], gwstg[:, k * 128:(k + 1) * 128],
                identb[0:16, 0:16]),
                r=[("b1", k // 8, 0), ("b1", k // 8, 1), ("identb",)],
                w=[(pbk,)])
            Sc.cop("dve", lambda v, k=k, pb=pb: v.tensor_copy(
                gwT[:, k * 16:(k + 1) * 16], pb[:, 0:16]),
                r=[(pbk,)], w=[("gwT",)])
        for j in range(4):
            Sc.cop("pe", lambda t, j=j: t.matmul(
                pg[5][:, :], onesb[0:1, :], boutstg[:, j * 512:(j + 1) * 512],
                start=True, stop=True),
                r=[("b1", 2, 0), ("b1", 2, 1), ("b1", 3, 0), ("b1", 3, 1),
                   ("onesb",)],
                w=[("pg5",)])
            Sc.cop("dve", lambda v, j=j: v.tensor_copy(
                brep[:, j * 512:(j + 1) * 512], pg[5][:, :]),
                r=[("pg5",)], w=[("brep",)])

    def emit_ln(dname):
        dest = ap[dname] if dname is not None else None
        for q in range(NT):
            xs = x_sb[:, q * H:(q + 1) * H]
            us = ustg[:, (q % 2) * H:((q % 2) + 1) * H]
            c0, c1 = 2 * (q % 8), 2 * (q % 8) + 1
            Sc.cop("act", lambda a, xs=xs, c0=c0: a.activation(
                out=scr[:, 0:H], in_=xs, func=AF.Copy,
                accum_out=smalls[:, c0:c0 + 1]),
                r=[("x", q)], w=[("scr",), ("sm", c0)])
            Sc.cop("act", lambda a, xs=xs, c1=c1: a.activation(
                out=scr[:, 0:H], in_=xs, func=AF.Square,
                accum_out=smalls[:, c1:c1 + 1]),
                r=[("x", q)], w=[("scr",), ("sm", c1)])
            Sc.cop("dve", lambda v, c0=c0: v.tensor_scalar(
                out=smalls[:, c0:c0 + 1], in0=smalls[:, c0:c0 + 1],
                scalar1=-1.0 / H, scalar2=None, op0=ALU.mult),
                r=[("sm", c0)], w=[("sm", c0)])
            Sc.cop("dve", lambda v, c0=c0: v.tensor_scalar(
                out=colsc[0:16, 15:16] if False else smalls[:, 14:15],
                in0=smalls[:, c0:c0 + 1],
                scalar1=smalls[:, c0:c0 + 1], scalar2=None, op0=ALU.mult),
                r=[("sm", c0)], w=[("m2",)])
            Sc.cop("dve", lambda v, c1=c1: v.tensor_scalar(
                out=smalls[:, c1:c1 + 1], in0=smalls[:, c1:c1 + 1],
                scalar1=1.0 / H, scalar2=EPS, op0=ALU.mult, op1=ALU.add),
                r=[("sm", c1)], w=[("sm", c1)])
            Sc.cop("dve", lambda v, c1=c1: v.tensor_sub(
                smalls[:, c1:c1 + 1], smalls[:, c1:c1 + 1], smalls[:, 14:15]),
                r=[("sm", c1), ("m2",)], w=[("sm", c1)])
            Sc.cop("act", lambda a, c1=c1: a.activation(
                out=smalls[:, c1:c1 + 1], in_=smalls[:, c1:c1 + 1],
                func=AF.Sqrt),
                r=[("sm", c1)], w=[("sm", c1)])
            Sc.cop("dve", lambda v, c1=c1: v.reciprocal(
                smalls[:, c1:c1 + 1], smalls[:, c1:c1 + 1]),
                r=[("sm", c1)], w=[("sm", c1)])
            Sc.cop("dve", lambda v, xs=xs, us=us, c0=c0, c1=c1: v.tensor_scalar(
                out=us, in0=xs, scalar1=smalls[:, c0:c0 + 1],
                scalar2=smalls[:, c1:c1 + 1], op0=ALU.add, op1=ALU.mult),
                r=[("x", q), ("sm", c0), ("sm", c1)], w=[("ustg", q % 2)])
            if dname is not None:
                Sc.dma("sp", lambda s, us=us, q=q: s.dma_start(
                    out=dest[q * 128:(q + 1) * 128, :], in_=us),
                    r=[("ustg", q % 2)], w=[(dname, q)], sem="us%d" % (q % 2))
            for k in range(KC):
                pb = pbf2[k % 2]
                pbk = "pbf%d" % (k % 2)
                Sc.cop("pe", lambda t, us=us, k=k, pb=pb: t.transpose(
                    pb[:, 0:128], us[:, k * 128:(k + 1) * 128],
                    identb[:, :]),
                    r=[("ustg", q % 2), ("identb",)], w=[(pbk,)])
                Sc.cop("dve", lambda v, k=k, q=q, pb=pb: v.tensor_copy(
                    ufeat[:, k * TOK + q * 128: k * TOK + (q + 1) * 128],
                    pb[:, 0:128]),
                    r=[(pbk,)], w=[("uf", k)] if False else [("uf", k),
                                                             ("ufq", k, q)])

    def emit_routing(inv):
        for tau in range(2):
            bank = 4 + tau

            def mmgrp(t, tau=tau, bank=bank):
                last = None
                for k in range(KC):
                    last = t.matmul(
                        pg[bank][0:16, :], gwT[:, k * 16:(k + 1) * 16],
                        ufeat[:, k * TOK + tau * 512:k * TOK + (tau + 1) * 512],
                        start=(k == 0), stop=(k == KC - 1))
                return last
            Sc.cop("pe", mmgrp, r=[("gwT",)] + ufk(0, KC), w=[("pg%d" % bank,)])
            Sc.cop("dve", lambda v, tau=tau, bank=bank: v.tensor_copy(
                at_loc2[:, tau * 512:(tau + 1) * 512], pg[bank][0:16, :]),
                r=[("pg%d" % bank,)], w=[("atl2",)])
        for q in range(NT):
            Sc.cop("pe", lambda t, q=q: t.transpose(
                pg[5][:, 0:16], at_loc2[:, q * 128:(q + 1) * 128],
                identf[0:16, 0:16]),
                r=[("atl2",), ("identf",)], w=[("pg6",)])
            Sc.cop("dve", lambda v, q=q: v.tensor_copy(
                p_sb[:, q * E:(q + 1) * E], pg[5][0:128, 0:16]),
                r=[("pg6",)], w=[("p", q)])
        for q in range(NT):
            pq = p_sb[:, q * E:(q + 1) * E]
            Sc.cop("dve", lambda v, pq=pq: v.reduce_max(
                smalls[:, 0:1], pq, axis=AX.X), r=[("p", q)], w=[("sm", 0)])
            Sc.cop("dve", lambda v: v.tensor_scalar(
                out=smalls[:, 0:1], in0=smalls[:, 0:1], scalar1=-1.0,
                scalar2=None, op0=ALU.mult), r=[("sm", 0)], w=[("sm", 0)])
            Sc.cop("act", lambda a, pq=pq: a.activation(
                out=pq, in_=pq, func=AF.Exp, bias=smalls[:, 0:1],
                accum_out=smalls[:, 1:2]),
                r=[("p", q), ("sm", 0)], w=[("p", q), ("sm", 1)])
            Sc.cop("dve", lambda v: v.reciprocal(smalls[:, 1:2], smalls[:, 1:2]),
                   r=[("sm", 1)], w=[("sm", 1)])
            Sc.cop("dve", lambda v, pq=pq: v.tensor_scalar(
                out=pq, in0=pq, scalar1=smalls[:, 1:2], scalar2=None,
                op0=ALU.mult), r=[("p", q), ("sm", 1)], w=[("p", q)])
            Sc.cop("dve", lambda v, pq=pq: v.max(out=mx8[:, :], in_=pq),
                   r=[("p", q)], w=[("mx8",)])
            Sc.cop("dve", lambda v, pq=pq: v.max_index(
                out=ixt[:, :], in_max=mx8[:, :], in_values=pq),
                r=[("p", q), ("mx8",)], w=[("ixt",)])
            Sc.cop("dve", lambda v, q=q: v.tensor_copy(
                idx4f[:, q * K4:(q + 1) * K4], ixt[:, 0:4]),
                r=[("ixt",)], w=[("idx4", q)])
            Sc.cop("dve", lambda v: v.tensor_reduce(
                smalls[:, 2:3], mx8[:, 0:4], axis=AX.X, op=ALU.add),
                r=[("mx8",)], w=[("sm", 2)])
            Sc.cop("dve", lambda v: v.tensor_scalar(
                out=smalls[:, 2:3], in0=smalls[:, 2:3], scalar1=1e-20,
                scalar2=None, op0=ALU.add), r=[("sm", 2)], w=[("sm", 2)])
            Sc.cop("dve", lambda v: v.reciprocal(smalls[:, 2:3], smalls[:, 2:3]),
                   r=[("sm", 2)], w=[("sm", 2)])
            Sc.cop("dve", lambda v, q=q: v.tensor_scalar(
                out=w4[:, q * K4:(q + 1) * K4], in0=mx8[:, 0:4],
                scalar1=smalls[:, 2:3], scalar2=None, op0=ALU.mult),
                r=[("mx8",), ("sm", 2)], w=[("w4", q)])
            dst = a_tok[:, q * E:(q + 1) * E]
            for k in range(K4):
                if k == 0:
                    Sc.cop("dve", lambda v, q=q, k=k, dst=dst: v.tensor_scalar(
                        out=dst, in0=iot16[:, :],
                        scalar1=idx4f[:, q * K4 + k:q * K4 + k + 1],
                        scalar2=None, op0=ALU.is_equal),
                        r=[("iot16",), ("idx4", q)], w=[("A", q)])
                else:
                    Sc.cop("dve", lambda v, q=q, k=k: v.tensor_scalar(
                        out=p_sb[:, q * E:(q + 1) * E] if False
                        else smalls[:, 0:16] if False else ixt[:, 0:4] if False
                        else scr[0:128, 1024:1040], in0=iot16[:, :],
                        scalar1=idx4f[:, q * K4 + k:q * K4 + k + 1],
                        scalar2=None, op0=ALU.is_equal),
                        r=[("iot16",), ("idx4", q)], w=[("scrE",)])
                    Sc.cop("dve", lambda v, dst=dst, q=q: v.tensor_add(
                        dst, dst, scr[0:128, 1024:1040]),
                        r=[("A", q), ("scrE",)], w=[("A", q)])
        for q in range(NT):
            Sc.cop("pe", lambda t, q=q: t.transpose(
                pg[5][0:16, 0:128], a_tok[:, q * E:(q + 1) * E], identf[:, :]),
                r=[("A", q), ("identf",)], w=[("pg6",)])
            Sc.cop("dve", lambda v, q=q: v.tensor_copy(
                at_loc[0:16, q * 128:(q + 1) * 128], pg[5][0:16, 0:128]),
                r=[("pg6",)], w=[("atl",)])
        Sc.dma("sp", lambda s: s.dma_start(out=ap["at_in"][:, :], in_=at_loc[:, :]),
               r=[("atl",)], w=[("at_in",)], sem="atw")
        Sc.dma("pool", lambda g: g.collective_compute(
            "AllGather", ALU.bypass, replica_groups=[list(range(NC))],
            ins=[ap["at_in"].opt()], outs=[ap["at_out"].opt()]),
            r=[("at_in",)], w=[("at_out",)], sem="ccA%d" % inv, inc=1)

    def emit_gateup(wsrc_g, goff, wsrc_u, uoff, n_of):
        ogs = []
        o = 0
        while o < n_of:
            m = min(2, n_of - o)
            ogs.append((o, m))
            o += m
        coff = 0
        for oi, (o, m) in enumerate(ogs):
            slot = oi % 2
            fw = KC * m * 128
            for wsrc, base, wview, wkeys, wsem in (
                    (wsrc_g, goff, wAv(slot), wAk[slot], "wA"),
                    (wsrc_u, uoff, wBv(slot), wBk[slot], "wB")):
                Sc.dma("pool", lambda g, wsrc=wsrc, base=base, wview=wview,
                       coff=coff, fw=fw:
                       g.dma_start(out=wview[:, 0:fw],
                                   in_=wsrc[:, base + coff: base + coff + fw]),
                       w=wkeys, sem=wsem + str(slot))
            coff += fw
            for wview, wkeys, is_gate in ((wAv(slot), wAk[slot], True),
                                          (wBv(slot), wBk[slot], False)):
                for mi in range(m):
                    for tau in range(2):
                        bk = mi * 2 + tau
                        wspan = m * 128

                        def mm(t, mi=mi, tau=tau, bk=bk, wspan=wspan,
                               wview=wview):
                            last = None
                            for k in range(KC):
                                base = k * wspan + mi * 128
                                last = t.matmul(
                                    pg[bk][:, :], wview[:, base:base + 128],
                                    ufeat[:, k * TOK + tau * 512:
                                          k * TOK + (tau + 1) * 512],
                                    start=(k == 0), stop=(k == KC - 1))
                            return last
                        Sc.cop("pe", mm, r=list(wkeys) + ufk(0, KC),
                               w=[("pg%d" % bk,)])
                        c = o + mi
                        dst = big1[:, c * TOK + tau * 512:
                                   c * TOK + (tau + 1) * 512]
                        if is_gate:
                            Sc.cop("act", lambda a, bk=bk, dst=dst: a.activation(
                                out=dst, in_=pg[bk][:, :], func=AF.Silu),
                                r=[("pg%d" % bk,)], w=[("b1", c, tau)])
                        else:
                            Sc.cop("dve", lambda v, bk=bk, dst=dst: v.tensor_mul(
                                dst, dst, pg[bk][:, :]),
                                r=[("pg%d" % bk,), ("b1", c, tau)],
                                w=[("b1", c, tau)])

    def emit_down(wsrc, base, n_ic, out_cb):
        fw = n_ic * 512
        for quar in range(4):
            slot = quar % 2
            qsl = big2[:, slot * 11264: slot * 11264 + n_ic * 512]
            qkeys = b2k(slot * 11264, slot * 11264 + n_ic * 512)
            Sc.dma("pool", lambda g, quar=quar, qsl=qsl: g.dma_start(
                out=qsl[:, 0:fw],
                in_=wsrc[:, base + quar * fw: base + (quar + 1) * fw]),
                w=qkeys, sem="b2%d" % slot)
            for rt in range(NT):
                bk = 4 + (rt % 2)

                def dmm(t, rt=rt, bk=bk, qsl=qsl):
                    last = None
                    for ic in range(n_ic):
                        last = t.matmul(
                            pg[bk][:, :],
                            big1[:, ic * TOK + rt * 128: ic * TOK + rt * 128 + 128],
                            qsl[:, ic * 512:(ic + 1) * 512],
                            start=(ic == 0), stop=(ic == n_ic - 1))
                    return last
                rr = list(qkeys)
                for ic in range(n_ic):
                    rr += [("b1", ic, 0), ("b1", ic, 1)]
                Sc.cop("pe", dmm, r=rr, w=[("pg%d" % bk,)])
                out_cb(rt, quar, bk)

    def emit_meta(inv):
        Sc.cop("dve", lambda v: v.tensor_tensor_scan(
            out=at_loc2[:, :], data0=at_loc[:, :], data1=at_loc[:, :],
            initial=0.0, op0=ALU.add, op1=ALU.bypass),
            r=[("atl",)], w=[("atl2",)])
        Sc.cop("dve", lambda v: v.tensor_sub(at_loc2[:, :], at_loc2[:, :],
               at_loc[:, :]),
               r=[("atl",), ("atl2",)], w=[("atl2",)])
        nchunk = T // CKW
        for ck in range(nchunk):
            par = ck % 2
            rnk, hf = ck // 2, ck % 2
            Sc.dma("sp", lambda s, rnk=rnk, hf=hf: s.dma_start(
                out=scA[:, :],
                in_=ap["at_out"][rnk * 16:(rnk + 1) * 16,
                                 hf * CKW:(hf + 1) * CKW]),
                r=[("at_out",)], w=[("scA",)], sem="atc%d" % par)
            Sc.cop("dve", lambda v, ck=ck: v.tensor_reduce(
                colsc[0:16, ck:ck + 1], scA[:, :], axis=AX.X, op=ALU.add),
                r=[("scA",)], w=[("cnt", ck)])
            if ck == 0:
                Sc.cop("dve", lambda v, par=par: v.tensor_tensor_scan(
                    out=scB[:, par * CKW:(par + 1) * CKW], data0=scA[:, :],
                    data1=scA[:, :], initial=0.0, op0=ALU.add, op1=ALU.bypass),
                    r=[("scA",)], w=[("scB", par)])
            else:
                pp = (ck - 1) % 2
                Sc.cop("dve", lambda v, par=par, pp=pp: v.tensor_tensor_scan(
                    out=scB[:, par * CKW:(par + 1) * CKW], data0=scA[:, :],
                    data1=scA[:, :],
                    initial=scB[:, pp * CKW + CKW - 1: pp * CKW + CKW],
                    op0=ALU.add, op1=ALU.bypass),
                    r=[("scA",), ("scB", pp)], w=[("scB", par)])
            Sc.cop("dve", lambda v, par=par: v.tensor_sub(
                scC[:, :], scB[:, par * CKW:(par + 1) * CKW], scA[:, :]),
                r=[("scA",), ("scB", par)], w=[("scC",)])
            Sc.cop("dve", lambda v: v.tensor_scalar(
                out=scr[0:16, 1024:1024 + CKW], in0=scC[:, :],
                scalar1=float(CAP), scalar2=None, op0=ALU.is_lt),
                r=[("scC",)], w=[("scrE",)])
            Sc.cop("dve", lambda v: v.tensor_mul(scA[:, :], scA[:, :],
                   scr[0:16, 1024:1024 + CKW]),
                   r=[("scA",), ("scrE",)], w=[("scA",)])
            Sc.cop("dve", lambda v: v.tensor_scalar(
                out=scC[:, :], in0=scC[:, :], scalar1=1.0, scalar2=None,
                op0=ALU.add),
                r=[("scC",)], w=[("scC",)])
            Sc.cop("dve", lambda v: v.tensor_mul(scC[:, :], scC[:, :],
                   scA[:, :]),
                   r=[("scA",), ("scC",)], w=[("scC",)])
            Sc.cop("dve", lambda v: v.tensor_scalar(
                out=scC[:, :], in0=scC[:, :], scalar1=1.0, scalar2=None,
                op0=ALU.subtract),
                r=[("scC",)], w=[("scC",)])
            Sc.cop("pe", lambda t: t.matmul(
                pg[5][0:2, :], selT_sb[:, :], scC[:, :],
                start=True, stop=True),
                r=[("scC",), ("selT",)], w=[("pg5",)])
            Sc.cop("dve", lambda v: v.tensor_copy(scr[0:2, 0:CKW],
                   pg[5][0:2, :]),
                   r=[("pg5",)], w=[("msl",)])
            for i4 in range(CKW // 128):
                tt = ck * (CKW // 128) + i4
                Sc.cop("pe", lambda t, i4=i4: t.transpose(
                    pg[5][0:128, 16:18],
                    scr[0:2, i4 * 128:(i4 + 1) * 128], identf[0:2, 0:2]),
                    r=[("msl",), ("identf",)], w=[("pg5",)])
                Sc.cop("dve", lambda v, tt=tt: v.tensor_copy(
                    slotcols[:, tt * 2:(tt + 1) * 2], pg[5][0:128, 16:18]),
                    r=[("pg5",)], w=[("slc", tt)])
        Sc.cop("dve", lambda v: v.tensor_mul(
            scr[0:16, 1024:1040], colsc[0:16, 0:16], pm_sb[0:16, :]),
            r=[("pm",)] + [("cnt", ck2) for ck2 in range(16)], w=[("scrE",)])
        Sc.cop("dve", lambda v: v.tensor_reduce(
            colsc[0:16, 15:16] if False else smalls[0:16, 15:16] if False
            else colsc[0:16, 0:1], scr[0:16, 1024:1040], axis=AX.X, op=ALU.add),
            r=[("scrE",)], w=[("pfx",)])
        Sc.cop("dve", lambda v: v.tensor_scalar(
            out=at_loc2[:, :], in0=at_loc2[:, :], scalar1=colsc[0:16, 0:1],
            scalar2=None, op0=ALU.add),
            r=[("atl2",), ("pfx",)], w=[("atl2",)])
        for q in range(NT):
            Sc.cop("pe", lambda t, q=q: t.transpose(
                pg[5][:, 0:16], at_loc2[:, q * 128:(q + 1) * 128],
                identf[0:16, 0:16]),
                r=[("atl2",), ("identf",)], w=[("pg6",)])
            Sc.cop("dve", lambda v, q=q: v.tensor_copy(
                slot_tok[:, q * E:(q + 1) * E], pg[5][0:128, 0:16]),
                r=[("pg6",)], w=[("slot", q)])
        for q in range(NT):
            for k in range(K4):
                col = q * K4 + k
                Sc.cop("dve", lambda v, q=q, col=col: v.tensor_scalar(
                    out=scr[0:128, 1024:1040], in0=iot16[:, :],
                    scalar1=idx4f[:, col:col + 1], scalar2=None,
                    op0=ALU.is_equal),
                    r=[("iot16",), ("idx4", q)], w=[("scrE",)])
                Sc.cop("dve", lambda v, q=q: v.tensor_mul(
                    scr[0:128, 1024:1040], scr[0:128, 1024:1040],
                    slot_tok[:, q * E:(q + 1) * E]),
                    r=[("scrE",), ("slot", q)], w=[("scrE",)])
                Sc.cop("dve", lambda v: v.tensor_reduce(
                    smalls[:, 3:4], scr[0:128, 1024:1040], axis=AX.X,
                    op=ALU.add),
                    r=[("scrE",)], w=[("sm", 3)])
                Sc.cop("dve", lambda v: v.tensor_scalar(
                    out=smalls[:, 4:5], in0=smalls[:, 3:4], scalar1=float(CAP),
                    scalar2=None, op0=ALU.is_lt),
                    r=[("sm", 3)], w=[("sm", 4)])
                Sc.cop("dve", lambda v, col=col, q=q: v.tensor_copy(
                    ismall[:, 0:1], idx4f[:, col:col + 1]),
                    r=[("idx4", q)], w=[("ism",)])
                Sc.cop("dve", lambda v: v.tensor_scalar(
                    out=ismall[:, 1:2], in0=ismall[:, 0:1], scalar1=1,
                    scalar2=None, op0=ALU.bitwise_and),
                    r=[("ism",)], w=[("ism",)])
                Sc.cop("dve", lambda v: v.tensor_copy(
                    smalls[:, 6:7], ismall[:, 1:2]),
                    r=[("ism",)], w=[("sm", 6)])
                Sc.cop("dve", lambda v, col=col, q=q: v.tensor_scalar(
                    out=smalls[:, 5:6], in0=idx4f[:, col:col + 1],
                    scalar1=smalls[:, 6:7], scalar2=512.0,
                    op0=ALU.subtract, op1=ALU.mult),
                    r=[("idx4", q), ("sm", 6)], w=[("sm", 5)])
                Sc.cop("dve", lambda v: v.tensor_scalar(
                    out=smalls[:, 6:7], in0=smalls[:, 6:7], scalar1=8192.0,
                    scalar2=smalls[:, 3:4], op0=ALU.mult, op1=ALU.add),
                    r=[("sm", 6), ("sm", 3)], w=[("sm", 6)])
                Sc.cop("dve", lambda v: v.tensor_add(
                    smalls[:, 5:6], smalls[:, 5:6], smalls[:, 6:7]),
                    r=[("sm", 5), ("sm", 6)], w=[("sm", 5)])
                Sc.cop("dve", lambda v: v.tensor_mul(
                    smalls[:, 5:6], smalls[:, 5:6], smalls[:, 4:5]),
                    r=[("sm", 4), ("sm", 5)], w=[("sm", 5)])
                Sc.cop("dve", lambda v, col=col: v.tensor_copy(
                    gidx[:, col:col + 1], smalls[:, 5:6]),
                    r=[("sm", 5)], w=[("gidx", q)])
                Sc.cop("dve", lambda v, col=col, q=q: v.tensor_scalar(
                    out=wfin[:, col:col + 1], in0=w4[:, col:col + 1],
                    scalar1=smalls[:, 4:5], scalar2=0.5,
                    op0=ALU.mult, op1=ALU.mult),
                    r=[("w4", q), ("sm", 4)], w=[("wfin", q)])
        Sc.cop("pool", lambda g: g.iota(scr[:, 1024:2048], pattern=[[1, 1024]],
               base=0, channel_multiplier=0,
               allow_small_or_imprecise_dtypes=True),
               w=[("scr",), ("scrE",), ("msl",)])
        Sc.alias([("eqr", 0), ("eqr", 1), ("eqr", 2), ("eqr", 3)],
                 [("ustg", 0), ("ustg", 1)])
        for tt in range(64):
            for el in range(2):
                ering = (tt * 2 + el) % 4
                eqt = ustg[:, ering * 1024:(ering + 1) * 1024]
                eng = "dve"
                Sc.cop(eng, lambda v, eqt=eqt, tt=tt, el=el: v.tensor_scalar(
                    out=eqt, in0=scr[:, 1024:2048],
                    scalar1=slotcols[:, tt * 2 + el:tt * 2 + el + 1],
                    scalar2=None, op0=ALU.is_equal),
                    r=[("scr",), ("slc", tt)], w=[("eqr", ering)])
                for sh in range(2):
                    bk = 2 + sh * 2 + el
                    Sc.cop("pe", lambda t, tt=tt, eqt=eqt, sh=sh, bk=bk:
                           t.matmul(pg[bk][0:2, :], hilo[:, 2 * tt:2 * tt + 2],
                                    eqt[:, sh * 512:(sh + 1) * 512],
                                    start=(tt == 0), stop=(tt == 63)),
                           r=[("eqr", ering), ("hilo",)], w=[("pg%d" % bk,)])
        Sc.alias([("ustg", 0), ("ustg", 1)],
                 [("eqr", 0), ("eqr", 1), ("eqr", 2), ("eqr", 3)])
        Sc.alias([("bufrow",)], [("ustg", 0), ("ustg", 1),
                                 ("eqr", 0), ("eqr", 1), ("eqr", 2),
                                 ("eqr", 3)])
        Sc.alias([("buff", 0)], [("atl2",)])
        Sc.alias([("buff", 1)], [("atl",)])
        for el in range(2):
            for sh in range(2):
                bk = 2 + sh * 2 + el
                Sc.cop("dve", lambda v, el=el, sh=sh, bk=bk: v.tensor_copy(
                    bufrow[0:2, el * 1024 + sh * 512: el * 1024 + (sh + 1) * 512],
                    pg[bk][0:2, :]),
                    r=[("pg%d" % bk,)], w=[("bufrow",)])
        for el in range(2):
            for sh in range(2):
                Sc.cop("pe", lambda t, el=el, sh=sh: t.matmul(
                    pg[2][0:1, :], c64b[:, :],
                    bufrow[0:2, el * 1024 + sh * 512:
                           el * 1024 + (sh + 1) * 512],
                    start=True, stop=True),
                    r=[("bufrow",), ("c64b",)], w=[("pg2",)])
                Sc.cop("dve", lambda v, el=el, sh=sh: v.tensor_copy(
                    buffv[el][0:1, sh * 512:(sh + 1) * 512], pg[2][0:1, :]),
                    r=[("pg2",)], w=[("buff", el)])
        for el in range(2):
            for jj in range(8):
                Sc.cop("pe", lambda t, el=el, jj=jj: t.transpose(
                    pg[5][0:128, 20:21],
                    buffv[el][0:1, jj * 128:(jj + 1) * 128],
                    identf[0:1, 0:1]),
                    r=[("buff", el), ("identf",)], w=[("pg5",)])
                Sc.cop("dve", lambda v, el=el, jj=jj: v.tensor_copy(
                    buff8[:, el * 8 + jj: el * 8 + jj + 1], pg[5][0:128, 20:21]),
                    r=[("pg5",)], w=[("buff8",)])
        Sc.cop("dve", lambda v: v.tensor_copy(bufidx[:, :], buff8[:, :]),
               r=[("buff8",)], w=[("bufidx",)])
        Sc.alias([("ustg", 0), ("ustg", 1)], [("bufrow",)])
        Sc.alias([("atl2",)], [("buff", 0)])
        Sc.alias([("atl",)], [("buff", 1)])

    def emit_experts(inv):
        for el in range(2):
            for j in range(8):
                sl = j % 2
                st = stg[:, sl * H:(sl + 1) * H]
                col = el * 8 + j
                Sc.dma("pool", lambda g, col=col, st=st: g.indirect_dma_start(
                    out=st, out_offset=None,
                    in_=ap["ag_u"][:, :],
                    in_offset=bass.IndirectOffsetOnAxis(
                        bufidx[:, col:col + 1], 0)),
                    r=[("bufidx",), ("ag_u",)], w=[("stg", sl)],
                    sem="xgg%d" % sl)
                for k in range(KC):
                    pb = pbf2[k % 2]
                    pbk = "pbf%d" % (k % 2)
                    Sc.cop("pe", lambda t, st=st, k=k, pb=pb: t.transpose(
                        pb[:, 0:128], st[:, k * 128:(k + 1) * 128],
                        identb[:, :]),
                        r=[("stg", sl), ("identb",)], w=[(pbk,)])
                    Sc.cop("dve", lambda v, k=k, j=j, pb=pb: v.tensor_copy(
                        ufeat[:, k * TOK + j * 128: k * TOK + (j + 1) * 128],
                        pb[:, 0:128]),
                        r=[(pbk,)], w=[("uf", k)])
            emit_gateup(ap["wg"], el * (H * II // 128),
                        ap["wu"], el * (H * II // 128), II // 128)

            def eo_cb(rt, quar, bk, el=el):
                evac_store(bk,
                           ap["eo_in"][el * CAP + rt * 128:
                                       el * CAP + (rt + 1) * 128,
                                       quar * 512:(quar + 1) * 512],
                           [("eo_in", el)])
            emit_down(ap["wd"], el * (II * H // 128), II // 128, eo_cb)
            Sc.dma("pool", lambda g, el=el: g.collective_compute(
                "AllGather", ALU.bypass, replica_groups=[list(range(NC))],
                ins=[ap["eo_in"][el * CAP:(el + 1) * CAP, :].opt()],
                outs=[ap["eo_out"][el * NC * CAP:(el + 1) * NC * CAP, :].opt()]),
                r=[("eo_in", el)], w=[("eo_out", el)],
                sem="ccE%d_%d" % (inv, el), inc=1)

    def emit_combine(inv):
        for q in range(NT):
            for k in range(K4):
                sl = k % 2
                st = stg[:, sl * H:(sl + 1) * H]
                col = q * K4 + k
                Sc.dma("pool", lambda g, col=col, st=st: g.indirect_dma_start(
                    out=st, out_offset=None,
                    in_=ap["eo_out"][:, :],
                    in_offset=bass.IndirectOffsetOnAxis(gidx[:, col:col + 1], 0)),
                    r=[("gidx", q), ("eo_out", 0), ("eo_out", 1)],
                    w=[("stg", sl)],
                    sem="eg%d" % sl)
                if k == 0:
                    Sc.cop("dve", lambda v, col=col, st=st: v.tensor_scalar(
                        out=scr[:, 0:H], in0=st, scalar1=wfin[:, col:col + 1],
                        scalar2=None, op0=ALU.mult),
                        r=[("stg", sl), ("wfin", q)], w=[("scr",)])
                else:
                    Sc.cop("dve", lambda v, col=col, st=st: v.scalar_tensor_tensor(
                        out=scr[:, 0:H], in0=st, scalar=wfin[:, col:col + 1],
                        in1=scr[:, 0:H], op0=ALU.mult, op1=ALU.add),
                        r=[("stg", sl), ("wfin", q), ("scr",)], w=[("scr",)])
            xs = x_sb[:, q * H:(q + 1) * H]
            Sc.cop("dve", lambda v, xs=xs, q=q: v.tensor_add(xs, xs, scr[:, 0:H]),
                   r=[("scr",), ("x", q)], w=[("x", q)])

    def emit_ffn(inv):
        emit_ln("u_tok")
        Sc.dma("pool", lambda g: g.collective_compute(
            "AllGather", ALU.bypass, replica_groups=[list(range(NC))],
            ins=[ap["u_tok"].opt()], outs=[ap["ag_u"].opt()]),
            r=[("u_tok", q) for q in range(NT)], w=[("ag_u",)],
            sem="ccU%d" % inv, inc=1)
        emit_routing(inv)
        emit_gateup(ap["swg"], 0, ap["swu"], 0, IS // 128)

        def sh_cb(rt, quar, bk):
            xs = x_sb[:, rt * H + quar * 512: rt * H + (quar + 1) * 512]
            Sc.cop("dve", lambda v, bk=bk, xs=xs, rt=rt: v.scalar_tensor_tensor(
                out=xs, in0=pg[bk][:, :], scalar=0.5, in1=xs,
                op0=ALU.mult, op1=ALU.add),
                r=[("pg%d" % bk,), ("x", rt)], w=[("x", rt)])
        emit_down(ap["swd"], 0, IS // 128, sh_cb)
        emit_meta(inv)
        emit_experts(inv)
        emit_combine(inv)

    def emit_qkv():
        emit_ln(None)
        for og in range(16):
            slot = og % 2
            Sc.dma("pool", lambda g, og=og, slot=slot: g.dma_start(
                out=wAv(slot)[:, 0:4096],
                in_=ap["w_qkv"][:, og * 4096:(og + 1) * 4096]),
                w=wAk[slot], sem="wA%d" % slot)
            for mi in range(2):
                for tau in range(2):
                    bk = mi * 2 + tau

                    def qmm(t, slot=slot, mi=mi, tau=tau, bk=bk):
                        last = None
                        for k in range(KC):
                            base = k * 256 + mi * 128
                            last = t.matmul(
                                pg[bk][:, :], wAv(slot)[:, base:base + 128],
                                ufeat[:, k * TOK + tau * 512:
                                      k * TOK + (tau + 1) * 512],
                                start=(k == 0), stop=(k == KC - 1))
                        return last
                    Sc.cop("pe", qmm, r=list(wAk[slot]) + ufk(0, KC),
                           w=[("pg%d" % bk,)])
                    of = og * 2 + mi
                    sl = evctr[0] % 2
                    evac_store(bk, ap["qk_in"][of * 128:(of + 1) * 128,
                                               tau * 512:(tau + 1) * 512],
                               [("qk_in",)])
                    ev = evst[:, sl * 512:(sl + 1) * 512]
                    nbk = 4 + (evctr[0] % 2)
                    Sc.cop("dve", lambda v, ev=ev: v.tensor_mul(
                        stg[:, 0:512], ev, ev),
                        r=[("ev", sl), ("stg", 0)], w=[("stg", 0)])
                    Sc.cop("pe", lambda t, nbk=nbk: t.matmul(
                        pg[nbk][0:1, :], onesb[:, 0:1], stg[:, 0:512],
                        start=True, stop=True),
                        r=[("stg", 0), ("onesb",)], w=[("pg%d" % nbk,)])
                    if of < 16:
                        ncol = tau * 16 + of
                        Sc.cop("dve", lambda v, nbk=nbk, ncol=ncol:
                               v.reduce_max(norm48[0:1, ncol:ncol + 1],
                                            pg[nbk][0:1, :], axis=AX.X),
                               r=[("pg%d" % nbk,)], w=[("n48", ncol)])
                    else:
                        ncol = 32 + (of - 16)
                        if tau == 0:
                            Sc.cop("dve", lambda v, nbk=nbk, ncol=ncol:
                                   v.reduce_max(norm48[0:1, ncol:ncol + 1],
                                                pg[nbk][0:1, :], axis=AX.X),
                                   r=[("pg%d" % nbk,)], w=[("n48", ncol)])
                        else:
                            Sc.cop("dve", lambda v, nbk=nbk, ncol=ncol:
                                   v.reduce_max(norm48[0:1, 48:49],
                                                pg[nbk][0:1, :], axis=AX.X),
                                   r=[("pg%d" % nbk,)], w=[("n48x",)])
                            Sc.cop("dve", lambda v, ncol=ncol: v.tensor_max(
                                norm48[0:1, ncol:ncol + 1],
                                norm48[0:1, ncol:ncol + 1], norm48[0:1, 48:49]),
                                r=[("n48", ncol), ("n48x",)],
                                w=[("n48", ncol)])
        Sc.cop("dve", lambda v: v.tensor_copy(norm48b[0:1, 0:48],
               norm48[0:1, 0:48]),
               r=[("n48", i) for i in range(48)], w=[("n48b",)])
        Sc.dma("sp", lambda s: s.dma_start(
            out=ap["qk_in"][4096:4097, 0:64], in_=norm48b[0:1, :]),
            r=[("n48b",)], w=[("qk_in",)], sem="nrw")
        Sc.dma("pool", lambda g: g.collective_compute(
            "AllGather", ALU.bypass,
            replica_groups=[[0, 1, 2, 3], [4, 5, 6, 7]],
            ins=[ap["qk_in"].opt()], outs=[ap["qk_out"].opt()]),
            r=[("qk_in",)], w=[("qk_out",)], sem="ccQ", inc=1)
        Sc.dma("pool", lambda g: g.dma_start(
            out=normsc[0:1, 0:192].rearrange("a (r c) -> a r c", c=48),
            in_=ap["qk_out"].rearrange("(r f) t -> f r t", f=QKR)
            [4096:4097, :, 0:48]),
            r=[("qk_out",)], w=[("nsc",)], sem="nrr")
        Sc.cop("dve", lambda v: v.tensor_copy(
            normsc[0:1, 192:208], normsc[0:1, 32:48]),
            r=[("nsc",)], w=[("nsc",)])
        for rq in range(1, 4):
            Sc.cop("dve", lambda v, rq=rq: v.tensor_max(
                normsc[0:1, 192 + rq * 16:192 + (rq + 1) * 16],
                normsc[0:1, 192 + (rq - 1) * 16:192 + rq * 16],
                normsc[0:1, rq * 48 + 32:rq * 48 + 48]),
                r=[("nsc",)], w=[("nsc",)])
        for rq in range(4):
            for tau in range(2):
                base = 256 + rq * 32 + tau * 16
                Sc.cop("dve", lambda v, rq=rq, tau=tau, base=base: v.tensor_mul(
                    normsc[0:1, base:base + 16],
                    normsc[0:1, rq * 48 + tau * 16:rq * 48 + (tau + 1) * 16],
                    normsc[0:1, 192 + rq * 16:192 + (rq + 1) * 16]),
                    r=[("nsc",)], w=[("nsc",)])
        Sc.cop("act", lambda a: a.activation(
            out=normsc[0:1, 256:384], in_=normsc[0:1, 256:384], func=AF.Ln,
            scale=(1.05 * SCALE) ** 2),
            r=[("nsc",)], w=[("nsc",)])
        Sc.cop("act", lambda a: a.activation(
            out=normsc[0:1, 256:384], in_=normsc[0:1, 256:384], func=AF.Exp,
            scale=0.5),
            r=[("nsc",)], w=[("nsc",)])
        Sc.cop("dve", lambda v: v.tensor_scalar(
            out=normsc[0:1, 256:384], in0=normsc[0:1, 256:384],
            scalar1=-1.0, scalar2=None, op0=ALU.mult),
            r=[("nsc",)], w=[("nsc",)])
        for vog in range(4):
            slot = vog % 2
            vsl = wVv(slot)
            Sc.dma("pool", lambda g, vog=vog, vsl=vsl: g.dma_start(
                out=vsl[:, 0:8192],
                in_=ap["w_qkv"][:, 65536 + vog * 8192: 65536 + (vog + 1) * 8192]),
                w=wVk[slot], sem="wV%d" % slot)
            for rt in range(NT):
                bk = 4 + (rt % 2)

                def vmm(t, rt=rt, bk=bk, vsl=vsl):
                    last = None
                    for k in range(KC):
                        last = t.matmul(
                            pg[bk][:, :],
                            ufeat[:, k * TOK + rt * 128: k * TOK + rt * 128 + 128],
                            vsl[:, k * 512:(k + 1) * 512],
                            start=(k == 0), stop=(k == KC - 1))
                    return last
                Sc.cop("pe", vmm, r=list(wVk[slot]) + ufk(0, KC),
                       w=[("pg%d" % bk,)])
                evac_store(bk, ap["v_in"][rt * 128:(rt + 1) * 128,
                                          vog * 512:(vog + 1) * 512],
                           [("v_in",)])
        Sc.dma("pool", lambda g: g.collective_compute(
            "AllGather", ALU.bypass,
            replica_groups=[[0, 1, 2, 3], [4, 5, 6, 7]],
            ins=[ap["v_in"].opt()], outs=[ap["v_out"].opt()]),
            r=[("v_in",)], w=[("v_out",)], sem="ccV", inc=1)

    def emit_attn_core():
        Sc.alias([("den",)], [("atl2",)])
        Sc.alias([("pb", i) for i in range(6)] + [("qb", 0), ("qb", 1)],
                 [("stg", 0), ("stg", 1)])
        Sc.alias([("rd", 0), ("rd", 1)], [("ustg", 0), ("ustg", 1)])

        engs = [Sc.h[e] for e in ("pe", "act", "dve", "sp")]
        axi = AxisInfo(NC, 1)
        idxr = [nc.axis_index(axi, engines=[e.engine]) for e in engs]
        s0 = Sc.snap()
        states = []
        for c in nc.Switch(engines=engs, index=idxr, n=NC):
            Sc.restore(s0)
            b, a = c // 4, c % 4
            for h in range(HEADS):
                ksl = h % 2
                kh = ufeat[:, ksl * 4096:(ksl + 1) * 4096]
                vh = ufeat[:, 8192 + ksl * 4096: 8192 + (ksl + 1) * 4096]
                khk = ufk(4 * ksl, 4 * ksl + 4)
                vhk = ufk(8 + 4 * ksl, 8 + 4 * ksl + 4)
                Sc.dma("sp", lambda s, h=h, kh=kh: s.dma_start(
                    out=kh.rearrange("p (r t) -> p r t", t=TOK),
                    in_=ap["qk_out"].rearrange("(r f) t -> f r t", f=QKR)
                    [2048 + h * 128: 2048 + (h + 1) * 128, :, :]),
                    r=[("qk_out",)], w=khk, sem="kk%d" % ksl)
                Sc.dma("sp", lambda s, h=h, vh=vh: s.dma_start(
                    out=vh.rearrange("p (kt d) -> p kt d", d=128),
                    in_=ap["v_out"].rearrange("(kt p) f -> p kt f", p=128)
                    [:, :, h * 128:(h + 1) * 128]),
                    r=[("v_out",)], w=vhk, sem="vv%d" % ksl)
                jqs = (a, 7 - a)
                nks = (4 * (a + 1), 4 * (8 - a))
                for ji, jq in enumerate(jqs):
                    rq = jq // 2
                    qh = qbuf[:, ji * 512:(ji + 1) * 512]
                    Sc.dma("sp", lambda s, rq=rq, h=h, jq=jq, qh=qh: s.dma_start(
                        out=qh,
                        in_=ap["qk_out"][rq * QKR + h * 128:
                                         rq * QKR + (h + 1) * 128,
                                         (jq % 2) * 512:((jq % 2) + 1) * 512]),
                        r=[("qk_out",)], w=[("qb", ji)], sem="qq%d" % ji)
                    ncol = 256 + rq * 32 + (jq % 2) * 16 + h
                    Sc.cop("pe", lambda t, ji=ji, ncol=ncol: t.matmul(
                        pg[4 + ji][:, 0:1], onesf[0:1, :],
                        normsc[0:1, ncol:ncol + 1],
                        start=True, stop=True),
                        r=[("nsc",), ("onesf",)], w=[("pg%d" % (4 + ji),)])
                    Sc.cop("dve", lambda v, ji=ji: v.tensor_copy(
                        smalls[:, 8 + ji:9 + ji], pg[4 + ji][:, 0:1]),
                        r=[("pg%d" % (4 + ji),)], w=[("sm", 8 + ji)])
                for step in range(max(nks) + 2):
                    for ji in range(2):
                        nk = nks[ji]
                        qh = qbuf[:, ji * 512:(ji + 1) * 512]
                        if 2 <= step < nk + 2:
                            kt = step - 2
                            prc = ji * 3 + kt % 3
                            Sc.cop("pe", lambda t, kt=kt, prc=prc, vh=vh,
                                   ji=ji, nk=nk:
                                   t.matmul(pg[2 + ji][:, :],
                                            vh[:, kt * 128:(kt + 1) * 128],
                                            pbuf[:, prc * 512:(prc + 1) * 512],
                                            start=(kt == 0),
                                            stop=(kt == nk - 1)),
                                   r=vhk + [("pb", prc)],
                                   w=[("pg%d" % (2 + ji),)])
                            Sc.cop("pe", lambda t, kt=kt, prc=prc, ji=ji,
                                   nk=nk:
                                   t.matmul(
                                       pg[4 + ji][0:1, :], onesb[:, 0:1],
                                       pbuf[:, prc * 512:(prc + 1) * 512],
                                       start=(kt == 0), stop=(kt == nk - 1)),
                                   r=[("pb", prc), ("onesb",)],
                                   w=[("pg%d" % (4 + ji),)])
                        if step < nk:
                            kt = step
                            stbk = ji
                            prc = ji * 3 + kt % 3
                            Sc.cop("pe", lambda t, kt=kt, stbk=stbk, kh=kh,
                                   qh=qh:
                                   t.matmul(pg[stbk][:, :],
                                            kh[:, kt * 128:(kt + 1) * 128], qh,
                                            start=True, stop=True),
                                   r=khk + [("qb", ji)],
                                   w=[("pg%d" % stbk,)])
                            Sc.cop("act", lambda a_, prc=prc, stbk=stbk, ji=ji:
                                   a_.activation(
                                       out=pbuf[:, prc * 512:(prc + 1) * 512],
                                       in_=pg[stbk][:, :], func=AF.Exp,
                                       scale=SCALE,
                                       bias=smalls[:, 8 + ji:9 + ji]),
                                   r=[("pg%d" % stbk,), ("sm", 8 + ji)],
                                   w=[("pb", prc)])
                            if kt >= nk - 4:
                                o = 384 - (kt - (nk - 4)) * 128
                                Sc.cop("dve", lambda v, prc=prc, o=o:
                                       v.tensor_mul(
                                           pbuf[:, prc * 512:(prc + 1) * 512],
                                           pbuf[:, prc * 512:(prc + 1) * 512],
                                           cmask[:, o:o + 512]),
                                       r=[("pb", prc), ("cmask",)],
                                       w=[("pb", prc)])
                for ji in range(2):
                    Sc.cop("dve", lambda v, ji=ji: v.tensor_copy(
                        denr, pg[4 + ji][0:1, :]),
                        r=[("pg%d" % (4 + ji),)], w=[("den",)])
                    Sc.cop("pe", lambda t, ji=ji: t.matmul(
                        pg[ji][:, :], onesf[0:1, :], denr,
                        start=True, stop=True),
                        r=[("den",), ("onesf",)], w=[("pg%d" % ji,)])
                    def _recip(v, ji=ji):
                        with nc.allow_low_precision(
                                reason="bf16 reciprocal of softmax denom"):
                            return v.reciprocal(
                                rds[:, ji * 512:(ji + 1) * 512], pg[ji][:, :])
                    Sc.cop("dve", _recip, r=[("pg%d" % ji,)], w=[("rd", ji)])
                    dst = big1[:, h * TOK + ji * 512: h * TOK + (ji + 1) * 512]
                    Sc.cop("dve", lambda v, ji=ji, dst=dst:
                           v.tensor_mul(dst, pg[2 + ji][:, :],
                                        rds[:, ji * 512:(ji + 1) * 512]),
                           r=[("pg%d" % (2 + ji),), ("rd", ji)],
                           w=[("b1", h, ji)])
            states.append(Sc.snap())
        Sc.merge(states)
        Sc.alias([("atl2",)], [("den",)])
        Sc.alias([("stg", 0), ("stg", 1)],
                 [("pb", i) for i in range(6)] + [("qb", 0), ("qb", 1)])
        Sc.alias([("ustg", 0), ("ustg", 1)], [("rd", 0), ("rd", 1)])


    def emit_outproj():
        def ao_cb(rt, quar, bk):
            evac_store(bk, ap["ao_in"][rt * 128:(rt + 1) * 128,
                                       quar * 512:(quar + 1) * 512],
                       [("ao_in",)])
        emit_down(ap["w_out"], 0, HEADS, ao_cb)
        Sc.dma("pool", lambda g: g.collective_compute(
            "AllGather", ALU.bypass,
            replica_groups=[[0, 1, 2, 3], [4, 5, 6, 7]],
            ins=[ap["ao_in"].opt()], outs=[ap["ao_out"].opt()]),
            r=[("ao_in",)], w=[("ao_out",)], sem="ccO", inc=1)

    def emit_attn_resid():
        engs = [Sc.h["sp"]]
        axi = AxisInfo(NC, 1)
        idxr = [nc.axis_index(axi, engines=[e.engine]) for e in engs]
        s0 = Sc.snap()
        states = []
        for c in nc.Switch(engines=engs, index=idxr, n=NC):
            Sc.restore(s0)
            b, a = c // 4, c % 4
            for gi, g in enumerate((2 * a, 2 * a + 1)):
                r_g = min(g, 7 - g)
                off4 = 0 if g <= 3 else 4
                Sc.dma("sp", lambda s, r_g=r_g, off4=off4, gi=gi: s.dma_start(
                    out=ufeat[:, gi * 4 * H:(gi + 1) * 4 * H]
                    .rearrange("p (q f) -> p q f", f=H),
                    in_=ap["ao_out"].rearrange("(rr p) f -> p rr f", p=128)
                    [:, r_g * 8 + off4: r_g * 8 + off4 + 4, :]),
                    r=[("ao_out",)], w=ufk(gi * 8, gi * 8 + 8),
                    sem="aop%d" % gi)
            states.append(Sc.snap())
        Sc.merge(states)
        for q in range(NT):
            gi, qq = q // 4, q % 4
            pk = ufeat[:, gi * 4 * H + qq * H: gi * 4 * H + (qq + 1) * H]
            pkk = ufk(gi * 8 + 2 * qq, gi * 8 + 2 * qq + 2)
            xs = x_sb[:, q * H:(q + 1) * H]
            Sc.cop("dve", lambda v, xs=xs, pk=pk: v.tensor_add(xs, xs, pk),
                   r=[("x", q)] + pkk, w=[("x", q)])
            Sc.cop("dve", lambda v, xs=xs: v.tensor_add(xs, xs, brep[:, :]),
                   r=[("x", q), ("brep",)], w=[("x", q)])

    def dump(name, src_ap, keys, cast=False):
        if not dbg:
            return
        q = "pool" if cast else "sp"
        Sc.dma(q, lambda s: s.dma_start(out=dbg_outs[name], in_=src_ap),
               r=keys, w=(), sem="dbg")

    emit_init()
    if "ffn0" in parts:
        emit_ffn(0)
        if dbg:
            dump("d_u", ap["u_tok"][:, :], [("u_tok", q) for q in range(NT)])
            dump("d_p", p_sb[:, :], [("p", q) for q in range(NT)])
            dump("d_idx4", idx4f[:, :], [("idx4", q) for q in range(NT)])
            dump("d_w4", w4[:, :], [("w4", q) for q in range(NT)])
            dump("d_slot", slot_tok[:, :], [("slot", q) for q in range(NT)])
            dump("d_gidx", gidx[:, :], [("gidx", q) for q in range(NT)])
            dump("d_wfin", wfin[:, :], [("wfin", q) for q in range(NT)])
            dump("d_eo", ap["eo_in"][:, :], [("eo_in", 0), ("eo_in", 1)])
            dump("d_atl", ap["at_in"][:, :], [("at_in",)])
            dump("d_bufidx", bufidx[:, :], [("bufidx",)])
            dump("d_slotc", slotcols[:, :], [("slc", tt2) for tt2 in range(64)])
            dump("d_x1", x_sb[:].rearrange("p (q f) -> p q f", f=H),
                 [("x", q) for q in range(NT)], cast=True)
    if "attn" in parts:
        emit_qkv()
        emit_attn_core()
        emit_outproj()
        emit_attn_resid()
        if dbg:
            dump("d_km", kmax8[:, :], [("m2s",), ("kmx",), ("qmx",)])
            dump("d_negm", smalls[:, 8:10], [("sm", 8), ("sm", 9)])
            dump("d_x2", x_sb[:].rearrange("p (q f) -> p q f", f=H),
                 [("x", q) for q in range(NT)], cast=True)
    if "ffn1" in parts:
        emit_ffn(1)
    for q in range(NT):
        Sc.dma("pool", lambda g, q=q: g.dma_start(
            out=ap["y"][q * 128:(q + 1) * 128, :],
            in_=x_sb[:, q * H:(q + 1) * H]),
            r=[("x", q)], w=(), sem="yo")
    Sc.h["pool"].wait_ge(Sc.sems["yo"], Sc.cnt["yo"])
    return nc


def _gu_blocks(n_of):
    ogs, o = [], 0
    while o < n_of:
        m = min(2, n_of - o)
        ogs.append((o, m))
        o += m
    return ogs


def pack_gateup(w, width=None):
    Wd = w.shape[1]
    a = w.reshape(KC, 128, Wd)
    blocks = []
    for (o, m) in _gu_blocks(Wd // 128):
        blocks.append(np.ascontiguousarray(
            a[:, :, o * 128:(o + m) * 128].transpose(1, 0, 2).reshape(128, -1)))
    return np.ascontiguousarray(np.concatenate(blocks, axis=1))


def pack_down(w):
    n_ic = w.shape[0] // 128
    a = w.reshape(n_ic, 128, w.shape[1])
    blocks = [np.ascontiguousarray(
        a[:, :, q * 512:(q + 1) * 512].transpose(1, 0, 2).reshape(128, -1))
        for q in range(4)]
    return np.ascontiguousarray(np.concatenate(blocks, axis=1))


def pack_qkv(w):
    a = w.reshape(KC, 128, 3 * H)
    blocks = []
    for og in range(16):
        blocks.append(np.ascontiguousarray(
            a[:, :, og * 256:(og + 1) * 256].transpose(1, 0, 2)
            .reshape(128, -1)))
    for vog in range(4):
        blocks.append(np.ascontiguousarray(
            a[:, :, 2 * H + vog * 512: 2 * H + (vog + 1) * 512]
            .transpose(1, 0, 2).reshape(128, -1)))
    return np.ascontiguousarray(np.concatenate(blocks, axis=1))


def make_inmaps(inputs):
    X = np.ascontiguousarray(np.asarray(inputs["L0_attn_x"],
                                        dtype=np.float32).reshape(T, H))
    wg = np.ascontiguousarray(np.asarray(inputs["L0_ffn_w_gate"], np.float32))
    wu = np.ascontiguousarray(np.asarray(inputs["L0_ffn_w_up"], np.float32))
    wd = np.ascontiguousarray(np.asarray(inputs["L0_ffn_w_down"], np.float32))
    w_qkv = np.ascontiguousarray(np.asarray(inputs["L0_attn_w_qkv"], np.float32))
    w_out = np.ascontiguousarray(np.asarray(inputs["L0_attn_w_out"], np.float32))
    b_out = np.ascontiguousarray(np.asarray(inputs["L0_attn_b_out"],
                                            np.float32).reshape(1, H))
    gate_w = np.ascontiguousarray(np.asarray(inputs["L0_ffn_gate_w"], np.float32))
    swg = np.ascontiguousarray(np.asarray(inputs["L0_ffn_sw_gate"], np.float32))
    swu = np.ascontiguousarray(np.asarray(inputs["L0_ffn_sw_up"], np.float32))
    swd = np.ascontiguousarray(np.asarray(inputs["L0_ffn_sw_down"], np.float32))
    swg_p = pack_gateup(swg)
    swu_p = pack_gateup(swu)
    swd_p = pack_down(swd)
    wqkv_p = pack_qkv(w_qkv)
    wout_p = pack_down(w_out)
    c64v = np.array([[64.0], [1.0]], np.float32)
    in_maps = []
    for c in range(NC):
        selT = np.zeros((E, 2), np.float32)
        selT[2 * c, 0] = 1.0
        selT[2 * c + 1, 1] = 1.0
        pm = np.zeros((E, 16), np.float32)
        pm[:, :2 * c] = 1.0
        wg2 = wg.reshape(E, H, II)[2 * c:2 * c + 2]
        wu2 = wu.reshape(E, H, II)[2 * c:2 * c + 2]
        wd2 = wd.reshape(E, II, H)[2 * c:2 * c + 2]
        xr = np.ascontiguousarray(
            X[c * TOK:(c + 1) * TOK].reshape(NT, 128, H)
            .transpose(1, 0, 2).reshape(128, NT * H))
        in_maps.append({
            "x": xr,
            "w_qkv": wqkv_p, "w_out": wout_p, "b_out": b_out,
            "gate_w": gate_w,
            "wg": np.concatenate([pack_gateup(wg2[e]) for e in range(2)],
                                 axis=1),
            "wu": np.concatenate([pack_gateup(wu2[e]) for e in range(2)],
                                 axis=1),
            "wd": np.concatenate([pack_down(wd2[e]) for e in range(2)],
                                 axis=1),
            "swg": swg_p, "swu": swu_p, "swd": swd_p,
            "selT": selT, "pm": pm, "c64": c64v,
        })
    return in_maps


def kernel(**inputs):
    nc = build()
    in_maps = make_inmaps(inputs)
    res = run_bass_kernel_spmd(nc, in_maps, core_ids=list(range(NC)))
    y = np.concatenate([res.results[c]["y"] for c in range(NC)], axis=0)
    return y.reshape(B, S, H).astype(np.float32)


# version 09
# speedup vs baseline: 1.0399x; 1.0077x over previous
import numpy as np
import concourse.bass as bass
import concourse.mybir as mybir
from concourse.bass_types import AxisInfo
from concourse.bass_utils import run_bass_kernel_spmd

F32 = mybir.dt.float32
BF16 = mybir.dt.bfloat16
I32 = mybir.dt.int32
U32 = mybir.dt.uint32
AF = mybir.ActivationFunctionType
ALU = mybir.AluOpType
AX = mybir.AxisListType

NC = 8
B, S, H = 2, 4096, 2048
T = B * S
TOK = T // NC
E, K4, II, IS, CAP = 16, 4, 1408, 2816, 1024
HEADS, DH = 16, 128
KC = H // 128
NT = TOK // 128
QKR = 2 * H + 32
SCALE = DH ** -0.5
EPS = 1e-5
DUMP = 2 * CAP
CKW = 512


class Sched:
    ENG = ("pe", "act", "dve", "pool", "sp")

    def __init__(self, nc):
        self.nc = nc
        self.h = {"pe": nc.tensor, "act": nc.scalar, "dve": nc.vector,
                  "pool": nc.gpsimd, "sp": nc.sync}
        self.sems = {}
        self.cnt = {}
        for e in self.ENG:
            self.sems[e] = nc.alloc_semaphore("sem_" + e)
            self.cnt[e] = 0
        self.lw = {}
        self.lr = {}
        self.waited = {e: {} for e in self.ENG}

    def sem(self, name):
        if name not in self.sems:
            self.sems[name] = self.nc.alloc_semaphore("sem_" + name)
        if name not in self.cnt:
            self.cnt[name] = 0
        return self.sems[name]

    def _wait(self, eng, sname, val):
        if val <= 0:
            return
        if eng == "pe" and sname == "pe":
            return
        if self.waited[eng].get(sname, 0) >= val:
            return
        self.h[eng].wait_ge(self.sems[sname], val)
        self.waited[eng][sname] = val

    def _pre(self, eng, r, w):
        for k in r:
            for sname, val in self.lw.get(k, {}).items():
                self._wait(eng, sname, val)
        for k in w:
            for sname, val in self.lw.get(k, {}).items():
                self._wait(eng, sname, val)
            for sname, val in self.lr.get(k, {}).items():
                self._wait(eng, sname, val)

    def _post(self, sname, val, r, w):
        for k in w:
            self.lw[k] = {sname: val}
            self.lr[k] = {}
        for k in r:
            d = self.lr.setdefault(k, {})
            if d.get(sname, 0) < val:
                d[sname] = val

    def cop(self, eng, emit, r=(), w=(), inc=1):
        self._pre(eng, r, w)
        inst = emit(self.h[eng])
        self.cnt[eng] += inc
        inst.then_inc(self.sems[eng], inc)
        self._post(eng, self.cnt[eng], r, w)

    def dma(self, q, emit, r=(), w=(), sem=None, inc=16):
        sname = sem or q
        self.sem(sname)
        self._pre(q, r, w)
        inst = emit(self.h[q])
        self.cnt[sname] += inc
        inst.then_inc(self.sems[sname], inc)
        self._post(sname, self.cnt[sname], r, w)

    def alias(self, dst_keys, src_keys):
        agg = {}
        for k in src_keys:
            for d in (self.lw.get(k, {}), self.lr.get(k, {})):
                for sn, v in d.items():
                    if agg.get(sn, 0) < v:
                        agg[sn] = v
        for k in dst_keys:
            self.lw[k] = dict(agg)
            self.lr[k] = {}

    def snap(self):
        import copy
        return (dict(self.cnt), copy.deepcopy(self.lw), copy.deepcopy(self.lr),
                copy.deepcopy(self.waited))

    def restore(self, s):
        import copy
        self.cnt = dict(s[0])
        self.lw = copy.deepcopy(s[1])
        self.lr = copy.deepcopy(s[2])
        self.waited = copy.deepcopy(s[3])

    def merge(self, states):
        cnts = [s[0] for s in states]
        for c in cnts[1:]:
            if c != cnts[0]:
                diff = {k: (cnts[0].get(k), c.get(k))
                        for k in set(cnts[0]) | set(c)
                        if cnts[0].get(k) != c.get(k)}
                raise AssertionError(f"switch branches unbalanced: {diff}")
        self.cnt = dict(cnts[0])
        lw, lr = {}, {}
        for s in states:
            for k, d in s[1].items():
                dd = lw.setdefault(k, {})
                for sn, val in d.items():
                    if dd.get(sn, 0) < val:
                        dd[sn] = val
            for k, d in s[2].items():
                dd = lr.setdefault(k, {})
                for sn, val in d.items():
                    if dd.get(sn, 0) < val:
                        dd[sn] = val
        self.lw, self.lr = lw, lr
        wt = {}
        for e in Sched.ENG:
            wt[e] = {}
            allk = set()
            for s in states:
                allk |= set(s[3][e].keys())
            for k in allk:
                vals = [s[3][e].get(k, 0) for s in states]
                if min(vals) > 0:
                    wt[e][k] = min(vals)
        self.waited = wt


def b2k(c0, c1):
    return [("b2g", g) for g in range(c0 // 512, (c1 + 511) // 512)]


def build(parts=("ffn0", "attn", "ffn1"), dbg=False):
    nc = bass.Bass("TRN2", target_bir_lowering=False)
    ap = {}

    def din(name, shape, dt=F32):
        ap[name] = nc.dram_tensor(name, shape, dt, kind="ExternalInput").ap()

    def dout(name, shape, dt=F32):
        ap[name] = nc.dram_tensor(name, shape, dt, kind="ExternalOutput").ap()

    def dint(name, shape, dt):
        ap[name] = nc.dram_tensor(name, shape, dt, kind="Internal").ap()

    din("x", [128, NT * H])
    din("w_qkv", [128, 3 * H * KC]); din("w_out", [128, H * KC])
    din("b_out", [1, H])
    din("gate_w", [E, H])
    din("wg", [128, 2 * H * II // 128]); din("wu", [128, 2 * H * II // 128])
    din("wd", [128, 2 * II * H // 128])
    din("swg", [128, H * IS // 128]); din("swu", [128, H * IS // 128])
    din("swd", [128, IS * H // 128])
    din("selT", [E, 2]); din("pm", [E, 16]); din("c64", [2, 1])
    dout("y", [TOK, H])

    dint("u_tok", [TOK, H], BF16)
    dint("ag_u", [T, H], BF16)
    dint("at_in", [E, TOK], F32)
    dint("at_out", [NC * E, TOK], F32)
    dint("eo_in", [2 * CAP, H], BF16)
    dint("eo_out", [E * CAP, H], BF16)
    dint("vln", [TOK, H], BF16)
    dint("qk_in", [QKR, TOK], BF16)
    dint("qk_out", [4 * QKR, TOK], BF16)
    dint("v_in", [TOK, H], BF16)
    dint("v_out", [4 * TOK, H], BF16)
    dint("ao_in", [TOK, H], BF16)
    dint("ao_out", [4 * TOK, H], BF16)

    dbg_outs = {}
    if dbg:
        for name, shape, dt in [
            ("d_u", [TOK, H], BF16), ("d_p", [TOK, E], F32),
            ("d_idx4", [TOK, K4], F32), ("d_w4", [TOK, K4], F32),
            ("d_slot", [TOK, E], F32), ("d_gidx", [TOK, K4], I32),
            ("d_wfin", [TOK, K4], F32),
            ("d_x1", [TOK, H], F32), ("d_x2", [TOK, H], F32),
            ("d_eo", [2 * CAP, H], BF16), ("d_atl", [E, TOK], F32),
            ("d_bufidx", [128, 16], I32), ("d_slotc", [128, 128], F32),
            ("d_km", [1, 16], F32), ("d_negm", [128, 2], F32),
        ]:
            dout(name, shape, dt)
            dbg_outs[name] = ap[name]

    def asb(name, shape, dt):
        return nc.alloc_sbuf_tensor(name, shape, dt)

    x_sb = asb("x_sb", [128, NT * H], BF16)
    ufeat = asb("ufeat", [128, KC * TOK], BF16)
    big1 = asb("big1", [128, 22 * TOK], BF16)
    big2 = asb("big2", [128, 2 * 11264], BF16)
    stg = asb("stg", [128, 2 * H], BF16)
    ustg = asb("ustg", [128, 2 * H], BF16)
    evst = asb("evst", [128, 2 * 512], BF16)
    brep = asb("brep", [128, H], BF16)
    scr = asb("scr", [128, 2048], F32)
    iotc = asb("iotc", [128, 1], F32)
    iot16 = asb("iot16", [128, 16], F32)
    identf = asb("identf", [128, 128], F32)
    identb = asb("identb", [128, 128], BF16)
    onesb = asb("onesb", [128, 128], BF16)
    cmask = asb("cmask", [128, 896], BF16)
    gwT = asb("gwT", [128, KC * E], BF16)
    tokid = asb("tokid", [128, 128], I32)
    w4 = asb("w4", [128, NT * K4], F32)
    idx4f = asb("idx4f", [128, NT * K4], F32)
    a_tok = asb("a_tok", [128, NT * E], F32)
    slot_tok = asb("slot_tok", [128, NT * E], F32)
    gidx = asb("gidx", [128, NT * K4], I32)
    wfin = asb("wfin", [128, NT * K4], F32)
    p_sb = asb("p_sb", [128, NT * E], F32)
    mx8 = asb("mx8", [128, 8], F32)
    ixt = asb("ixt", [128, 8], U32)
    smalls = asb("smalls", [128, 16], F32)
    at_loc = asb("at_loc", [16, TOK], F32)
    at_loc2 = asb("at_loc2", [16, TOK], F32)
    colsc = asb("colsc", [16, 16], F32)
    scA = asb("scA", [16, CKW], F32)
    scB = asb("scB", [16, 2 * CKW], F32)
    scC = asb("scC", [16, CKW], F32)
    selT_sb = asb("selT_sb", [16, 2], F32)
    pm_sb = asb("pm_sb", [16, 16], F32)
    c64_sb = asb("c64_sb", [2, 1], F32)
    c64b = asb("c64b", [2, 1], BF16)
    onesf = asb("onesf", [1, 128], F32)
    slotcols = asb("slotcols", [128, 128], F32)
    tokhi = asb("tokhi", [128, 64], BF16)
    toklo = asb("toklo", [128, 1], F32)
    hilo = asb("hilo", [128, 128], BF16)

    bufidx = asb("bufidx", [128, 16], I32)
    ismall = asb("ismall", [128, 2], I32)
    buff8 = asb("buff8", [128, 16], F32)
    norm48 = asb("norm48", [1, 64], F32)
    norm48b = asb("norm48b", [1, 64], BF16)
    normsc = asb("normsc", [1, 384], F32)
    pbuf = stg[:, 0:3072]
    qbuf = stg[:, 3072:4096]
    rds = ustg[:, 0:1024]
    ksq = evst[:, 0:512]
    kn2c = at_loc[0:1, 0:512]
    qn2 = at_loc[0:1, 512:1024]
    denr = at_loc2[0:1, 0:512]
    bufrow = ustg[0:2, 0:2048]
    buffv = [at_loc2[0:1, 0:1024], at_loc[0:1, 0:1024]]
    mrow = ustg[0:1, 0:512]
    gwstg = big1[0:16, 0:2048]
    boutstg = big1[0:1, 2048:4096]

    def wAv(slot):
        return big2[:, slot * 4096:(slot + 1) * 4096]

    def wBv(slot):
        return big2[:, 8192 + slot * 4096: 8192 + (slot + 1) * 4096]

    def wVv(slot):
        return big2[:, slot * 8192:(slot + 1) * 8192]

    wAk = [b2k(0, 4096), b2k(4096, 8192)]
    wBk = [b2k(8192, 12288), b2k(12288, 16384)]
    wVk = [b2k(0, 8192), b2k(8192, 16384)]

    pg = [nc.alloc_psum_tensor(f"pg{i}", [128, 512], F32) for i in range(6)]
    pbf2 = [nc.alloc_psum_tensor(f"pbf{i}", [128, 1024], BF16) for i in range(2)]
    pbf = pbf2[0]

    Sc = Sched(nc)
    evctr = [0]

    def evac_store(bk, out_dram_ap, dkeys):
        sl = evctr[0] % 2
        evctr[0] += 1
        ev = evst[:, sl * 512:(sl + 1) * 512]
        Sc.cop("act", lambda a: a.activation(out=ev, in_=pg[bk][:, :],
                                             func=AF.Copy),
               r=[("pg%d" % bk,)], w=[("ev", sl)])
        Sc.dma("sp", lambda s: s.dma_start(out=out_dram_ap, in_=ev),
               r=[("ev", sl)], w=dkeys, sem="evd%d" % sl)

    def ufk(c0, c1):
        return [("uf", k) for k in range(c0, c1)]

    def emit_init():
        Sc.dma("pool", lambda g: g.dma_start(out=x_sb[:, :], in_=ap["x"]),
               w=[("x", q) for q in range(NT)], sem="xin")
        Sc.cop("pool", lambda g: g.iota(iotc[:, :], pattern=[[0, 1]], base=0,
               channel_multiplier=1, allow_small_or_imprecise_dtypes=True),
               w=[("iotc",)])
        Sc.cop("pool", lambda g: g.iota(iot16[:, :], pattern=[[1, 16]], base=0,
               channel_multiplier=0, allow_small_or_imprecise_dtypes=True),
               w=[("iot16",)])
        Sc.cop("pool", lambda g: g.iota(scr[:, 0:896], pattern=[[1, 896]], base=0,
               channel_multiplier=0, allow_small_or_imprecise_dtypes=True),
               w=[("scr",)])
        Sc.cop("pool", lambda g: g.iota(scr[:, 1024:1088], pattern=[[2, 64]],
               base=0, channel_multiplier=0,
               allow_small_or_imprecise_dtypes=True),
               w=[("scrE",)])
        Sc.cop("dve", lambda v: v.tensor_scalar(out=identf[:, :], in0=scr[:, 0:128],
               scalar1=iotc[:, 0:1], scalar2=None, op0=ALU.is_equal),
               r=[("scr",), ("iotc",)], w=[("identf",)])
        Sc.cop("dve", lambda v: v.tensor_copy(identb[:, :], identf[:, :]),
               r=[("identf",)], w=[("identb",)])
        Sc.cop("dve", lambda v: v.memset(onesb[:, :], 1.0), w=[("onesb",)])
        Sc.cop("dve", lambda v: v.memset(onesf[:, :], 1.0), w=[("onesf",)])
        Sc.cop("dve", lambda v: v.tensor_scalar(out=smalls[:, 2:3],
               in0=iotc[:, 0:1], scalar1=63.5, scalar2=None, op0=ALU.is_gt),
               r=[("iotc",)], w=[("sm", 2)])
        Sc.cop("dve", lambda v: v.tensor_scalar(out=tokhi[:, :],
               in0=scr[:, 1024:1088], scalar1=smalls[:, 2:3], scalar2=None,
               op0=ALU.add),
               r=[("scrE",), ("sm", 2)], w=[("tokhi",)])
        Sc.cop("dve", lambda v: v.tensor_scalar(out=smalls[:, 3:4],
               in0=smalls[:, 2:3], scalar1=64.0, scalar2=None, op0=ALU.mult),
               r=[("sm", 2)], w=[("sm", 3)])
        Sc.cop("dve", lambda v: v.tensor_sub(smalls[:, 4:5], iotc[:, 0:1],
               smalls[:, 3:4]),
               r=[("iotc",), ("sm", 3)], w=[("sm", 4)])
        Sc.cop("dve", lambda v: v.tensor_copy(toklo[:, :], smalls[:, 4:5]),
               r=[("sm", 4)], w=[("toklo",)])
        Sc.cop("dve", lambda v: v.tensor_copy(
            hilo[:, :].rearrange("p (t two) -> p t two", two=2)[:, :, 0],
            tokhi[:, :]),
            r=[("tokhi",)], w=[("hilo",)])
        Sc.cop("dve", lambda v: v.tensor_scalar(
            out=hilo[:, :].rearrange("p (t two) -> p t two", two=2)[:, :, 1],
            in0=onesb[:, 0:64], scalar1=toklo[:, 0:1], scalar2=None,
            op0=ALU.mult),
            r=[("onesb",), ("toklo",)], w=[("hilo",)])

        Sc.cop("dve", lambda v: v.tensor_scalar(out=smalls[:, 0:1], in0=iotc[:, 0:1],
               scalar1=384.0, scalar2=None, op0=ALU.add),
               r=[("iotc",)], w=[("sm", 0)])
        Sc.cop("dve", lambda v: v.tensor_scalar(out=cmask[:, :], in0=scr[:, 0:896],
               scalar1=smalls[:, 0:1], scalar2=None, op0=ALU.is_ge),
               r=[("scr",), ("sm", 0)], w=[("cmask",)])

        for i, (nm, tile, key) in enumerate([
                ("selT", selT_sb, "selT"), ("pm", pm_sb, "pm"),
                ("c64", c64_sb, "c64")]):
            Sc.dma("sp", lambda s, nm=nm, tile=tile: s.dma_start(
                out=tile[:, :], in_=ap[nm]), w=[(key,)], sem="sml%d" % i)
        Sc.dma("pool", lambda g: g.dma_start(out=gwstg, in_=ap["gate_w"]),
               w=[("b1", 0, 0), ("b1", 0, 1), ("b1", 1, 0), ("b1", 1, 1)],
               sem="smlg")
        Sc.dma("pool", lambda g: g.dma_start(out=boutstg, in_=ap["b_out"]),
               w=[("b1", 2, 0), ("b1", 2, 1), ("b1", 3, 0), ("b1", 3, 1)],
               sem="smlb")
        Sc.cop("dve", lambda v: v.tensor_copy(c64b[:, :], c64_sb[:, :]),
               r=[("c64",)], w=[("c64b",)])
        for k in range(KC):
            pb = pbf2[k % 2]
            pbk = "pbf%d" % (k % 2)
            Sc.cop("pe", lambda t, k=k, pb=pb: t.transpose(
                pb[:, 0:16], gwstg[:, k * 128:(k + 1) * 128],
                identb[0:16, 0:16]),
                r=[("b1", k // 8, 0), ("b1", k // 8, 1), ("identb",)],
                w=[(pbk,)])
            Sc.cop("dve", lambda v, k=k, pb=pb: v.tensor_copy(
                gwT[:, k * 16:(k + 1) * 16], pb[:, 0:16]),
                r=[(pbk,)], w=[("gwT",)])
        for j in range(4):
            Sc.cop("pe", lambda t, j=j: t.matmul(
                pg[5][:, :], onesb[0:1, :], boutstg[:, j * 512:(j + 1) * 512],
                start=True, stop=True),
                r=[("b1", 2, 0), ("b1", 2, 1), ("b1", 3, 0), ("b1", 3, 1),
                   ("onesb",)],
                w=[("pg5",)])
            Sc.cop("dve", lambda v, j=j: v.tensor_copy(
                brep[:, j * 512:(j + 1) * 512], pg[5][:, :]),
                r=[("pg5",)], w=[("brep",)])

    def emit_ln(dname):
        dest = ap[dname] if dname is not None else None
        for q in range(NT):
            xs = x_sb[:, q * H:(q + 1) * H]
            us = ustg[:, (q % 2) * H:((q % 2) + 1) * H]
            c0, c1 = 2 * (q % 8), 2 * (q % 8) + 1
            Sc.cop("act", lambda a, xs=xs, c0=c0: a.activation(
                out=scr[:, 0:H], in_=xs, func=AF.Copy,
                accum_out=smalls[:, c0:c0 + 1]),
                r=[("x", q)], w=[("scr",), ("sm", c0)])
            Sc.cop("act", lambda a, xs=xs, c1=c1: a.activation(
                out=scr[:, 0:H], in_=xs, func=AF.Square,
                accum_out=smalls[:, c1:c1 + 1]),
                r=[("x", q)], w=[("scr",), ("sm", c1)])
            Sc.cop("dve", lambda v, c0=c0: v.tensor_scalar(
                out=smalls[:, c0:c0 + 1], in0=smalls[:, c0:c0 + 1],
                scalar1=-1.0 / H, scalar2=None, op0=ALU.mult),
                r=[("sm", c0)], w=[("sm", c0)])
            Sc.cop("dve", lambda v, c0=c0: v.tensor_scalar(
                out=colsc[0:16, 15:16] if False else smalls[:, 14:15],
                in0=smalls[:, c0:c0 + 1],
                scalar1=smalls[:, c0:c0 + 1], scalar2=None, op0=ALU.mult),
                r=[("sm", c0)], w=[("m2",)])
            Sc.cop("dve", lambda v, c1=c1: v.tensor_scalar(
                out=smalls[:, c1:c1 + 1], in0=smalls[:, c1:c1 + 1],
                scalar1=1.0 / H, scalar2=EPS, op0=ALU.mult, op1=ALU.add),
                r=[("sm", c1)], w=[("sm", c1)])
            Sc.cop("dve", lambda v, c1=c1: v.tensor_sub(
                smalls[:, c1:c1 + 1], smalls[:, c1:c1 + 1], smalls[:, 14:15]),
                r=[("sm", c1), ("m2",)], w=[("sm", c1)])
            Sc.cop("act", lambda a, c1=c1: a.activation(
                out=smalls[:, c1:c1 + 1], in_=smalls[:, c1:c1 + 1],
                func=AF.Sqrt),
                r=[("sm", c1)], w=[("sm", c1)])
            Sc.cop("dve", lambda v, c1=c1: v.reciprocal(
                smalls[:, c1:c1 + 1], smalls[:, c1:c1 + 1]),
                r=[("sm", c1)], w=[("sm", c1)])
            Sc.cop("dve", lambda v, xs=xs, us=us, c0=c0, c1=c1: v.tensor_scalar(
                out=us, in0=xs, scalar1=smalls[:, c0:c0 + 1],
                scalar2=smalls[:, c1:c1 + 1], op0=ALU.add, op1=ALU.mult),
                r=[("x", q), ("sm", c0), ("sm", c1)], w=[("ustg", q % 2)])
            if dname is not None:
                Sc.dma("sp", lambda s, us=us, q=q: s.dma_start(
                    out=dest[q * 128:(q + 1) * 128, :], in_=us),
                    r=[("ustg", q % 2)], w=[(dname, q)], sem="us%d" % (q % 2))
            for k in range(KC):
                pb = pbf2[k % 2]
                pbk = "pbf%d" % (k % 2)
                Sc.cop("pe", lambda t, us=us, k=k, pb=pb: t.transpose(
                    pb[:, 0:128], us[:, k * 128:(k + 1) * 128],
                    identb[:, :]),
                    r=[("ustg", q % 2), ("identb",)], w=[(pbk,)])
                Sc.cop("dve", lambda v, k=k, q=q, pb=pb: v.tensor_copy(
                    ufeat[:, k * TOK + q * 128: k * TOK + (q + 1) * 128],
                    pb[:, 0:128]),
                    r=[(pbk,)], w=[("uf", k)] if False else [("uf", k),
                                                             ("ufq", k, q)])

    def emit_routing(inv):
        for tau in range(2):
            bank = 4 + tau

            def mmgrp(t, tau=tau, bank=bank):
                last = None
                for k in range(KC):
                    last = t.matmul(
                        pg[bank][0:16, :], gwT[:, k * 16:(k + 1) * 16],
                        ufeat[:, k * TOK + tau * 512:k * TOK + (tau + 1) * 512],
                        start=(k == 0), stop=(k == KC - 1))
                return last
            Sc.cop("pe", mmgrp, r=[("gwT",)] + ufk(0, KC), w=[("pg%d" % bank,)])
            Sc.cop("dve", lambda v, tau=tau, bank=bank: v.tensor_copy(
                at_loc2[:, tau * 512:(tau + 1) * 512], pg[bank][0:16, :]),
                r=[("pg%d" % bank,)], w=[("atl2",)])
        for q in range(NT):
            Sc.cop("pe", lambda t, q=q: t.transpose(
                pg[5][:, 0:16], at_loc2[:, q * 128:(q + 1) * 128],
                identf[0:16, 0:16]),
                r=[("atl2",), ("identf",)], w=[("pg6",)])
            Sc.cop("dve", lambda v, q=q: v.tensor_copy(
                p_sb[:, q * E:(q + 1) * E], pg[5][0:128, 0:16]),
                r=[("pg6",)], w=[("p", q)])
        for q in range(NT):
            pq = p_sb[:, q * E:(q + 1) * E]
            Sc.cop("dve", lambda v, pq=pq: v.reduce_max(
                smalls[:, 0:1], pq, axis=AX.X), r=[("p", q)], w=[("sm", 0)])
            Sc.cop("dve", lambda v: v.tensor_scalar(
                out=smalls[:, 0:1], in0=smalls[:, 0:1], scalar1=-1.0,
                scalar2=None, op0=ALU.mult), r=[("sm", 0)], w=[("sm", 0)])
            Sc.cop("act", lambda a, pq=pq: a.activation(
                out=pq, in_=pq, func=AF.Exp, bias=smalls[:, 0:1],
                accum_out=smalls[:, 1:2]),
                r=[("p", q), ("sm", 0)], w=[("p", q), ("sm", 1)])
            Sc.cop("dve", lambda v: v.reciprocal(smalls[:, 1:2], smalls[:, 1:2]),
                   r=[("sm", 1)], w=[("sm", 1)])
            Sc.cop("dve", lambda v, pq=pq: v.tensor_scalar(
                out=pq, in0=pq, scalar1=smalls[:, 1:2], scalar2=None,
                op0=ALU.mult), r=[("p", q), ("sm", 1)], w=[("p", q)])
            Sc.cop("dve", lambda v, pq=pq: v.max(out=mx8[:, :], in_=pq),
                   r=[("p", q)], w=[("mx8",)])
            Sc.cop("dve", lambda v, pq=pq: v.max_index(
                out=ixt[:, :], in_max=mx8[:, :], in_values=pq),
                r=[("p", q), ("mx8",)], w=[("ixt",)])
            Sc.cop("dve", lambda v, q=q: v.tensor_copy(
                idx4f[:, q * K4:(q + 1) * K4], ixt[:, 0:4]),
                r=[("ixt",)], w=[("idx4", q)])
            Sc.cop("dve", lambda v: v.tensor_reduce(
                smalls[:, 2:3], mx8[:, 0:4], axis=AX.X, op=ALU.add),
                r=[("mx8",)], w=[("sm", 2)])
            Sc.cop("dve", lambda v: v.tensor_scalar(
                out=smalls[:, 2:3], in0=smalls[:, 2:3], scalar1=1e-20,
                scalar2=None, op0=ALU.add), r=[("sm", 2)], w=[("sm", 2)])
            Sc.cop("dve", lambda v: v.reciprocal(smalls[:, 2:3], smalls[:, 2:3]),
                   r=[("sm", 2)], w=[("sm", 2)])
            Sc.cop("dve", lambda v, q=q: v.tensor_scalar(
                out=w4[:, q * K4:(q + 1) * K4], in0=mx8[:, 0:4],
                scalar1=smalls[:, 2:3], scalar2=None, op0=ALU.mult),
                r=[("mx8",), ("sm", 2)], w=[("w4", q)])
            dst = a_tok[:, q * E:(q + 1) * E]
            for k in range(K4):
                if k == 0:
                    Sc.cop("dve", lambda v, q=q, k=k, dst=dst: v.tensor_scalar(
                        out=dst, in0=iot16[:, :],
                        scalar1=idx4f[:, q * K4 + k:q * K4 + k + 1],
                        scalar2=None, op0=ALU.is_equal),
                        r=[("iot16",), ("idx4", q)], w=[("A", q)])
                else:
                    Sc.cop("dve", lambda v, q=q, k=k: v.tensor_scalar(
                        out=p_sb[:, q * E:(q + 1) * E] if False
                        else smalls[:, 0:16] if False else ixt[:, 0:4] if False
                        else scr[0:128, 1024:1040], in0=iot16[:, :],
                        scalar1=idx4f[:, q * K4 + k:q * K4 + k + 1],
                        scalar2=None, op0=ALU.is_equal),
                        r=[("iot16",), ("idx4", q)], w=[("scrE",)])
                    Sc.cop("dve", lambda v, dst=dst, q=q: v.tensor_add(
                        dst, dst, scr[0:128, 1024:1040]),
                        r=[("A", q), ("scrE",)], w=[("A", q)])
        for q in range(NT):
            Sc.cop("pe", lambda t, q=q: t.transpose(
                pg[5][0:16, 0:128], a_tok[:, q * E:(q + 1) * E], identf[:, :]),
                r=[("A", q), ("identf",)], w=[("pg6",)])
            Sc.cop("dve", lambda v, q=q: v.tensor_copy(
                at_loc[0:16, q * 128:(q + 1) * 128], pg[5][0:16, 0:128]),
                r=[("pg6",)], w=[("atl",)])
        Sc.dma("sp", lambda s: s.dma_start(out=ap["at_in"][:, :], in_=at_loc[:, :]),
               r=[("atl",)], w=[("at_in",)], sem="atw")
        Sc.dma("pool", lambda g: g.collective_compute(
            "AllGather", ALU.bypass, replica_groups=[list(range(NC))],
            ins=[ap["at_in"].opt()], outs=[ap["at_out"].opt()]),
            r=[("at_in",)], w=[("at_out",)], sem="ccA%d" % inv, inc=1)

    def emit_gateup(wsrc_g, goff, wsrc_u, uoff, n_of):
        ogs = []
        o = 0
        while o < n_of:
            m = min(2, n_of - o)
            ogs.append((o, m))
            o += m
        coff = 0
        for oi, (o, m) in enumerate(ogs):
            slot = oi % 2
            fw = KC * m * 128
            for wsrc, base, wview, wkeys, wsem in (
                    (wsrc_g, goff, wAv(slot), wAk[slot], "wA"),
                    (wsrc_u, uoff, wBv(slot), wBk[slot], "wB")):
                Sc.dma("pool", lambda g, wsrc=wsrc, base=base, wview=wview,
                       coff=coff, fw=fw:
                       g.dma_start(out=wview[:, 0:fw],
                                   in_=wsrc[:, base + coff: base + coff + fw]),
                       w=wkeys, sem=wsem + str(slot))
            coff += fw
            for wview, wkeys, is_gate in ((wAv(slot), wAk[slot], True),
                                          (wBv(slot), wBk[slot], False)):
                for mi in range(m):
                    for tau in range(2):
                        bk = mi * 2 + tau
                        wspan = m * 128

                        def mm(t, mi=mi, tau=tau, bk=bk, wspan=wspan,
                               wview=wview):
                            last = None
                            for k in range(KC):
                                base = k * wspan + mi * 128
                                last = t.matmul(
                                    pg[bk][:, :], wview[:, base:base + 128],
                                    ufeat[:, k * TOK + tau * 512:
                                          k * TOK + (tau + 1) * 512],
                                    start=(k == 0), stop=(k == KC - 1))
                            return last
                        Sc.cop("pe", mm, r=list(wkeys) + ufk(0, KC),
                               w=[("pg%d" % bk,)])
                        c = o + mi
                        dst = big1[:, c * TOK + tau * 512:
                                   c * TOK + (tau + 1) * 512]
                        if is_gate:
                            Sc.cop("act", lambda a, bk=bk, dst=dst: a.activation(
                                out=dst, in_=pg[bk][:, :], func=AF.Silu),
                                r=[("pg%d" % bk,)], w=[("b1", c, tau)])
                        else:
                            Sc.cop("dve", lambda v, bk=bk, dst=dst: v.tensor_mul(
                                dst, dst, pg[bk][:, :]),
                                r=[("pg%d" % bk,), ("b1", c, tau)],
                                w=[("b1", c, tau)])

    def emit_down(wsrc, base, n_ic, out_cb):
        fw = n_ic * 512
        for quar in range(4):
            slot = quar % 2
            qsl = big2[:, slot * 11264: slot * 11264 + n_ic * 512]
            qkeys = b2k(slot * 11264, slot * 11264 + n_ic * 512)
            Sc.dma("pool", lambda g, quar=quar, qsl=qsl: g.dma_start(
                out=qsl[:, 0:fw],
                in_=wsrc[:, base + quar * fw: base + (quar + 1) * fw]),
                w=qkeys, sem="b2%d" % slot)
            for rt in range(NT):
                bk = 4 + (rt % 2)

                def dmm(t, rt=rt, bk=bk, qsl=qsl):
                    last = None
                    for ic in range(n_ic):
                        last = t.matmul(
                            pg[bk][:, :],
                            big1[:, ic * TOK + rt * 128: ic * TOK + rt * 128 + 128],
                            qsl[:, ic * 512:(ic + 1) * 512],
                            start=(ic == 0), stop=(ic == n_ic - 1))
                    return last
                rr = list(qkeys)
                for ic in range(n_ic):
                    rr += [("b1", ic, 0), ("b1", ic, 1)]
                Sc.cop("pe", dmm, r=rr, w=[("pg%d" % bk,)])
                out_cb(rt, quar, bk)

    def emit_meta(inv):
        Sc.cop("dve", lambda v: v.tensor_tensor_scan(
            out=at_loc2[:, :], data0=at_loc[:, :], data1=at_loc[:, :],
            initial=0.0, op0=ALU.add, op1=ALU.bypass),
            r=[("atl",)], w=[("atl2",)])
        Sc.cop("dve", lambda v: v.tensor_sub(at_loc2[:, :], at_loc2[:, :],
               at_loc[:, :]),
               r=[("atl",), ("atl2",)], w=[("atl2",)])
        nchunk = T // CKW
        for ck in range(nchunk):
            par = ck % 2
            rnk, hf = ck // 2, ck % 2
            Sc.dma("sp", lambda s, rnk=rnk, hf=hf: s.dma_start(
                out=scA[:, :],
                in_=ap["at_out"][rnk * 16:(rnk + 1) * 16,
                                 hf * CKW:(hf + 1) * CKW]),
                r=[("at_out",)], w=[("scA",)], sem="atc%d" % par)
            Sc.cop("dve", lambda v, ck=ck: v.tensor_reduce(
                colsc[0:16, ck:ck + 1], scA[:, :], axis=AX.X, op=ALU.add),
                r=[("scA",)], w=[("cnt", ck)])
            if ck == 0:
                Sc.cop("dve", lambda v, par=par: v.tensor_tensor_scan(
                    out=scB[:, par * CKW:(par + 1) * CKW], data0=scA[:, :],
                    data1=scA[:, :], initial=0.0, op0=ALU.add, op1=ALU.bypass),
                    r=[("scA",)], w=[("scB", par)])
            else:
                pp = (ck - 1) % 2
                Sc.cop("dve", lambda v, par=par, pp=pp: v.tensor_tensor_scan(
                    out=scB[:, par * CKW:(par + 1) * CKW], data0=scA[:, :],
                    data1=scA[:, :],
                    initial=scB[:, pp * CKW + CKW - 1: pp * CKW + CKW],
                    op0=ALU.add, op1=ALU.bypass),
                    r=[("scA",), ("scB", pp)], w=[("scB", par)])
            Sc.cop("dve", lambda v, par=par: v.tensor_sub(
                scC[:, :], scB[:, par * CKW:(par + 1) * CKW], scA[:, :]),
                r=[("scA",), ("scB", par)], w=[("scC",)])
            Sc.cop("dve", lambda v: v.tensor_scalar(
                out=scr[0:16, 1024:1024 + CKW], in0=scC[:, :],
                scalar1=float(CAP), scalar2=None, op0=ALU.is_lt),
                r=[("scC",)], w=[("scrE",)])
            Sc.cop("dve", lambda v: v.tensor_mul(scA[:, :], scA[:, :],
                   scr[0:16, 1024:1024 + CKW]),
                   r=[("scA",), ("scrE",)], w=[("scA",)])
            Sc.cop("dve", lambda v: v.tensor_scalar(
                out=scC[:, :], in0=scC[:, :], scalar1=1.0, scalar2=None,
                op0=ALU.add),
                r=[("scC",)], w=[("scC",)])
            Sc.cop("dve", lambda v: v.tensor_mul(scC[:, :], scC[:, :],
                   scA[:, :]),
                   r=[("scA",), ("scC",)], w=[("scC",)])
            Sc.cop("dve", lambda v: v.tensor_scalar(
                out=scC[:, :], in0=scC[:, :], scalar1=1.0, scalar2=None,
                op0=ALU.subtract),
                r=[("scC",)], w=[("scC",)])
            Sc.cop("pe", lambda t: t.matmul(
                pg[5][0:2, :], selT_sb[:, :], scC[:, :],
                start=True, stop=True),
                r=[("scC",), ("selT",)], w=[("pg5",)])
            Sc.cop("dve", lambda v: v.tensor_copy(scr[0:2, 0:CKW],
                   pg[5][0:2, :]),
                   r=[("pg5",)], w=[("msl",)])
            for i4 in range(CKW // 128):
                tt = ck * (CKW // 128) + i4
                Sc.cop("pe", lambda t, i4=i4: t.transpose(
                    pg[5][0:128, 16:18],
                    scr[0:2, i4 * 128:(i4 + 1) * 128], identf[0:2, 0:2]),
                    r=[("msl",), ("identf",)], w=[("pg5",)])
                Sc.cop("dve", lambda v, tt=tt: v.tensor_copy(
                    slotcols[:, tt * 2:(tt + 1) * 2], pg[5][0:128, 16:18]),
                    r=[("pg5",)], w=[("slc", tt)])
        Sc.cop("dve", lambda v: v.tensor_mul(
            scr[0:16, 1024:1040], colsc[0:16, 0:16], pm_sb[0:16, :]),
            r=[("pm",)] + [("cnt", ck2) for ck2 in range(16)], w=[("scrE",)])
        Sc.cop("dve", lambda v: v.tensor_reduce(
            colsc[0:16, 15:16] if False else smalls[0:16, 15:16] if False
            else colsc[0:16, 0:1], scr[0:16, 1024:1040], axis=AX.X, op=ALU.add),
            r=[("scrE",)], w=[("pfx",)])
        Sc.cop("dve", lambda v: v.tensor_scalar(
            out=at_loc2[:, :], in0=at_loc2[:, :], scalar1=colsc[0:16, 0:1],
            scalar2=None, op0=ALU.add),
            r=[("atl2",), ("pfx",)], w=[("atl2",)])
        for q in range(NT):
            Sc.cop("pe", lambda t, q=q: t.transpose(
                pg[5][:, 0:16], at_loc2[:, q * 128:(q + 1) * 128],
                identf[0:16, 0:16]),
                r=[("atl2",), ("identf",)], w=[("pg6",)])
            Sc.cop("dve", lambda v, q=q: v.tensor_copy(
                slot_tok[:, q * E:(q + 1) * E], pg[5][0:128, 0:16]),
                r=[("pg6",)], w=[("slot", q)])
        for q in range(NT):
            for k in range(K4):
                col = q * K4 + k
                Sc.cop("dve", lambda v, q=q, col=col: v.tensor_scalar(
                    out=scr[0:128, 1024:1040], in0=iot16[:, :],
                    scalar1=idx4f[:, col:col + 1], scalar2=None,
                    op0=ALU.is_equal),
                    r=[("iot16",), ("idx4", q)], w=[("scrE",)])
                Sc.cop("dve", lambda v, q=q: v.tensor_mul(
                    scr[0:128, 1024:1040], scr[0:128, 1024:1040],
                    slot_tok[:, q * E:(q + 1) * E]),
                    r=[("scrE",), ("slot", q)], w=[("scrE",)])
                Sc.cop("dve", lambda v: v.tensor_reduce(
                    smalls[:, 3:4], scr[0:128, 1024:1040], axis=AX.X,
                    op=ALU.add),
                    r=[("scrE",)], w=[("sm", 3)])
                Sc.cop("dve", lambda v: v.tensor_scalar(
                    out=smalls[:, 4:5], in0=smalls[:, 3:4], scalar1=float(CAP),
                    scalar2=None, op0=ALU.is_lt),
                    r=[("sm", 3)], w=[("sm", 4)])
                Sc.cop("dve", lambda v, col=col, q=q: v.tensor_copy(
                    ismall[:, 0:1], idx4f[:, col:col + 1]),
                    r=[("idx4", q)], w=[("ism",)])
                Sc.cop("dve", lambda v: v.tensor_scalar(
                    out=ismall[:, 1:2], in0=ismall[:, 0:1], scalar1=1,
                    scalar2=None, op0=ALU.bitwise_and),
                    r=[("ism",)], w=[("ism",)])
                Sc.cop("dve", lambda v: v.tensor_copy(
                    smalls[:, 6:7], ismall[:, 1:2]),
                    r=[("ism",)], w=[("sm", 6)])
                Sc.cop("dve", lambda v, col=col, q=q: v.tensor_scalar(
                    out=smalls[:, 5:6], in0=idx4f[:, col:col + 1],
                    scalar1=smalls[:, 6:7], scalar2=512.0,
                    op0=ALU.subtract, op1=ALU.mult),
                    r=[("idx4", q), ("sm", 6)], w=[("sm", 5)])
                Sc.cop("dve", lambda v: v.tensor_scalar(
                    out=smalls[:, 6:7], in0=smalls[:, 6:7], scalar1=8192.0,
                    scalar2=smalls[:, 3:4], op0=ALU.mult, op1=ALU.add),
                    r=[("sm", 6), ("sm", 3)], w=[("sm", 6)])
                Sc.cop("dve", lambda v: v.tensor_add(
                    smalls[:, 5:6], smalls[:, 5:6], smalls[:, 6:7]),
                    r=[("sm", 5), ("sm", 6)], w=[("sm", 5)])
                Sc.cop("dve", lambda v: v.tensor_mul(
                    smalls[:, 5:6], smalls[:, 5:6], smalls[:, 4:5]),
                    r=[("sm", 4), ("sm", 5)], w=[("sm", 5)])
                Sc.cop("dve", lambda v, col=col: v.tensor_copy(
                    gidx[:, col:col + 1], smalls[:, 5:6]),
                    r=[("sm", 5)], w=[("gidx", q)])
                Sc.cop("dve", lambda v, col=col, q=q: v.tensor_scalar(
                    out=wfin[:, col:col + 1], in0=w4[:, col:col + 1],
                    scalar1=smalls[:, 4:5], scalar2=0.5,
                    op0=ALU.mult, op1=ALU.mult),
                    r=[("w4", q), ("sm", 4)], w=[("wfin", q)])
        Sc.cop("pool", lambda g: g.iota(scr[:, 1024:2048], pattern=[[1, 1024]],
               base=0, channel_multiplier=0,
               allow_small_or_imprecise_dtypes=True),
               w=[("scr",), ("scrE",), ("msl",)])
        Sc.alias([("eqr", 0), ("eqr", 1), ("eqr", 2), ("eqr", 3)],
                 [("ustg", 0), ("ustg", 1)])
        for tt in range(64):
            for el in range(2):
                ering = (tt * 2 + el) % 4
                eqt = ustg[:, ering * 1024:(ering + 1) * 1024]
                eng = "dve"
                Sc.cop(eng, lambda v, eqt=eqt, tt=tt, el=el: v.tensor_scalar(
                    out=eqt, in0=scr[:, 1024:2048],
                    scalar1=slotcols[:, tt * 2 + el:tt * 2 + el + 1],
                    scalar2=None, op0=ALU.is_equal),
                    r=[("scr",), ("slc", tt)], w=[("eqr", ering)])
                for sh in range(2):
                    bk = 2 + sh * 2 + el
                    Sc.cop("pe", lambda t, tt=tt, eqt=eqt, sh=sh, bk=bk:
                           t.matmul(pg[bk][0:2, :], hilo[:, 2 * tt:2 * tt + 2],
                                    eqt[:, sh * 512:(sh + 1) * 512],
                                    start=(tt == 0), stop=(tt == 63)),
                           r=[("eqr", ering), ("hilo",)], w=[("pg%d" % bk,)])
        Sc.alias([("ustg", 0), ("ustg", 1)],
                 [("eqr", 0), ("eqr", 1), ("eqr", 2), ("eqr", 3)])
        Sc.alias([("bufrow",)], [("ustg", 0), ("ustg", 1),
                                 ("eqr", 0), ("eqr", 1), ("eqr", 2),
                                 ("eqr", 3)])
        Sc.alias([("buff", 0)], [("atl2",)])
        Sc.alias([("buff", 1)], [("atl",)])
        for el in range(2):
            for sh in range(2):
                bk = 2 + sh * 2 + el
                Sc.cop("dve", lambda v, el=el, sh=sh, bk=bk: v.tensor_copy(
                    bufrow[0:2, el * 1024 + sh * 512: el * 1024 + (sh + 1) * 512],
                    pg[bk][0:2, :]),
                    r=[("pg%d" % bk,)], w=[("bufrow",)])
        for el in range(2):
            for sh in range(2):
                Sc.cop("pe", lambda t, el=el, sh=sh: t.matmul(
                    pg[2][0:1, :], c64b[:, :],
                    bufrow[0:2, el * 1024 + sh * 512:
                           el * 1024 + (sh + 1) * 512],
                    start=True, stop=True),
                    r=[("bufrow",), ("c64b",)], w=[("pg2",)])
                Sc.cop("dve", lambda v, el=el, sh=sh: v.tensor_copy(
                    buffv[el][0:1, sh * 512:(sh + 1) * 512], pg[2][0:1, :]),
                    r=[("pg2",)], w=[("buff", el)])
        for el in range(2):
            for jj in range(8):
                Sc.cop("pe", lambda t, el=el, jj=jj: t.transpose(
                    pg[5][0:128, 20:21],
                    buffv[el][0:1, jj * 128:(jj + 1) * 128],
                    identf[0:1, 0:1]),
                    r=[("buff", el), ("identf",)], w=[("pg5",)])
                Sc.cop("dve", lambda v, el=el, jj=jj: v.tensor_copy(
                    buff8[:, el * 8 + jj: el * 8 + jj + 1], pg[5][0:128, 20:21]),
                    r=[("pg5",)], w=[("buff8",)])
        Sc.cop("dve", lambda v: v.tensor_copy(bufidx[:, :], buff8[:, :]),
               r=[("buff8",)], w=[("bufidx",)])
        Sc.alias([("ustg", 0), ("ustg", 1)], [("bufrow",)])
        Sc.alias([("atl2",)], [("buff", 0)])
        Sc.alias([("atl",)], [("buff", 1)])

    def emit_experts(inv):
        for el in range(2):
            for j in range(8):
                sl = j % 2
                st = stg[:, sl * H:(sl + 1) * H]
                col = el * 8 + j
                Sc.dma("pool", lambda g, col=col, st=st: g.indirect_dma_start(
                    out=st, out_offset=None,
                    in_=ap["ag_u"][:, :],
                    in_offset=bass.IndirectOffsetOnAxis(
                        bufidx[:, col:col + 1], 0)),
                    r=[("bufidx",), ("ag_u",)], w=[("stg", sl)],
                    sem="xgg%d" % sl)
                for k in range(KC):
                    pb = pbf2[k % 2]
                    pbk = "pbf%d" % (k % 2)
                    Sc.cop("pe", lambda t, st=st, k=k, pb=pb: t.transpose(
                        pb[:, 0:128], st[:, k * 128:(k + 1) * 128],
                        identb[:, :]),
                        r=[("stg", sl), ("identb",)], w=[(pbk,)])
                    Sc.cop("dve", lambda v, k=k, j=j, pb=pb: v.tensor_copy(
                        ufeat[:, k * TOK + j * 128: k * TOK + (j + 1) * 128],
                        pb[:, 0:128]),
                        r=[(pbk,)], w=[("uf", k)])
            emit_gateup(ap["wg"], el * (H * II // 128),
                        ap["wu"], el * (H * II // 128), II // 128)

            def eo_cb(rt, quar, bk, el=el):
                evac_store(bk,
                           ap["eo_in"][el * CAP + rt * 128:
                                       el * CAP + (rt + 1) * 128,
                                       quar * 512:(quar + 1) * 512],
                           [("eo_in", el)])
            emit_down(ap["wd"], el * (II * H // 128), II // 128, eo_cb)
            Sc.dma("pool", lambda g, el=el: g.collective_compute(
                "AllGather", ALU.bypass, replica_groups=[list(range(NC))],
                ins=[ap["eo_in"][el * CAP:(el + 1) * CAP, :].opt()],
                outs=[ap["eo_out"][el * NC * CAP:(el + 1) * NC * CAP, :].opt()]),
                r=[("eo_in", el)], w=[("eo_out", el)],
                sem="ccE%d_%d" % (inv, el), inc=1)

    def emit_combine(inv):
        for q in range(NT):
            for k in range(K4):
                sl = k % 2
                st = stg[:, sl * H:(sl + 1) * H]
                col = q * K4 + k
                Sc.dma("pool", lambda g, col=col, st=st: g.indirect_dma_start(
                    out=st, out_offset=None,
                    in_=ap["eo_out"][:, :],
                    in_offset=bass.IndirectOffsetOnAxis(gidx[:, col:col + 1], 0)),
                    r=[("gidx", q), ("eo_out", 0), ("eo_out", 1)],
                    w=[("stg", sl)],
                    sem="eg%d" % sl)
                if k == 0:
                    Sc.cop("dve", lambda v, col=col, st=st: v.tensor_scalar(
                        out=scr[:, 0:H], in0=st, scalar1=wfin[:, col:col + 1],
                        scalar2=None, op0=ALU.mult),
                        r=[("stg", sl), ("wfin", q)], w=[("scr",)])
                else:
                    Sc.cop("dve", lambda v, col=col, st=st: v.scalar_tensor_tensor(
                        out=scr[:, 0:H], in0=st, scalar=wfin[:, col:col + 1],
                        in1=scr[:, 0:H], op0=ALU.mult, op1=ALU.add),
                        r=[("stg", sl), ("wfin", q), ("scr",)], w=[("scr",)])
            xs = x_sb[:, q * H:(q + 1) * H]
            Sc.cop("dve", lambda v, xs=xs, q=q: v.tensor_add(xs, xs, scr[:, 0:H]),
                   r=[("scr",), ("x", q)], w=[("x", q)])

    def emit_ffn(inv):
        emit_ln("u_tok")
        Sc.dma("pool", lambda g: g.collective_compute(
            "AllGather", ALU.bypass, replica_groups=[list(range(NC))],
            ins=[ap["u_tok"].opt()], outs=[ap["ag_u"].opt()]),
            r=[("u_tok", q) for q in range(NT)], w=[("ag_u",)],
            sem="ccU%d" % inv, inc=1)
        emit_routing(inv)
        emit_gateup(ap["swg"], 0, ap["swu"], 0, IS // 128)

        def sh_cb(rt, quar, bk):
            xs = x_sb[:, rt * H + quar * 512: rt * H + (quar + 1) * 512]
            Sc.cop("dve", lambda v, bk=bk, xs=xs, rt=rt: v.scalar_tensor_tensor(
                out=xs, in0=pg[bk][:, :], scalar=0.5, in1=xs,
                op0=ALU.mult, op1=ALU.add),
                r=[("pg%d" % bk,), ("x", rt)], w=[("x", rt)])
        emit_down(ap["swd"], 0, IS // 128, sh_cb)
        emit_meta(inv)
        emit_experts(inv)
        emit_combine(inv)

    def emit_qkv():
        emit_ln(None)
        for og in range(16):
            slot = og % 2
            Sc.dma("pool", lambda g, og=og, slot=slot: g.dma_start(
                out=wAv(slot)[:, 0:4096],
                in_=ap["w_qkv"][:, og * 4096:(og + 1) * 4096]),
                w=wAk[slot], sem="wA%d" % slot)
            for mi in range(2):
                for tau in range(2):
                    bk = mi * 2 + tau

                    def qmm(t, slot=slot, mi=mi, tau=tau, bk=bk):
                        last = None
                        for k in range(KC):
                            base = k * 256 + mi * 128
                            last = t.matmul(
                                pg[bk][:, :], wAv(slot)[:, base:base + 128],
                                ufeat[:, k * TOK + tau * 512:
                                      k * TOK + (tau + 1) * 512],
                                start=(k == 0), stop=(k == KC - 1))
                        return last
                    Sc.cop("pe", qmm, r=list(wAk[slot]) + ufk(0, KC),
                           w=[("pg%d" % bk,)])
                    of = og * 2 + mi
                    sl = evctr[0] % 2
                    evac_store(bk, ap["qk_in"][of * 128:(of + 1) * 128,
                                               tau * 512:(tau + 1) * 512],
                               [("qk_in",)])
                    nsl = evctr[0] % 2
                    nbk = 4 + nsl
                    Sc.cop("act", lambda a, bk=bk, nsl=nsl: a.activation(
                        out=stg[:, nsl * 512:(nsl + 1) * 512], in_=pg[bk][:, :],
                        func=AF.Square),
                        r=[("pg%d" % bk,)], w=[("stg", nsl)])
                    Sc.cop("pe", lambda t, nbk=nbk, nsl=nsl: t.matmul(
                        pg[nbk][0:1, :], onesb[:, 0:1],
                        stg[:, nsl * 512:(nsl + 1) * 512],
                        start=True, stop=True),
                        r=[("stg", nsl), ("onesb",)], w=[("pg%d" % nbk,)])
                    if of < 16:
                        ncol = tau * 16 + of
                        Sc.cop("dve", lambda v, nbk=nbk, ncol=ncol:
                               v.reduce_max(norm48[0:1, ncol:ncol + 1],
                                            pg[nbk][0:1, :], axis=AX.X),
                               r=[("pg%d" % nbk,)], w=[("n48", ncol)])
                    else:
                        ncol = 32 + (of - 16)
                        if tau == 0:
                            Sc.cop("dve", lambda v, nbk=nbk, ncol=ncol:
                                   v.reduce_max(norm48[0:1, ncol:ncol + 1],
                                                pg[nbk][0:1, :], axis=AX.X),
                                   r=[("pg%d" % nbk,)], w=[("n48", ncol)])
                        else:
                            Sc.cop("dve", lambda v, nbk=nbk, ncol=ncol:
                                   v.reduce_max(norm48[0:1, 48:49],
                                                pg[nbk][0:1, :], axis=AX.X),
                                   r=[("pg%d" % nbk,)], w=[("n48x",)])
                            Sc.cop("dve", lambda v, ncol=ncol: v.tensor_max(
                                norm48[0:1, ncol:ncol + 1],
                                norm48[0:1, ncol:ncol + 1], norm48[0:1, 48:49]),
                                r=[("n48", ncol), ("n48x",)],
                                w=[("n48", ncol)])
        Sc.cop("dve", lambda v: v.tensor_copy(norm48b[0:1, 0:48],
               norm48[0:1, 0:48]),
               r=[("n48", i) for i in range(48)], w=[("n48b",)])
        Sc.dma("sp", lambda s: s.dma_start(
            out=ap["qk_in"][4096:4097, 0:64], in_=norm48b[0:1, :]),
            r=[("n48b",)], w=[("qk_in",)], sem="nrw")
        Sc.dma("pool", lambda g: g.collective_compute(
            "AllGather", ALU.bypass,
            replica_groups=[[0, 1, 2, 3], [4, 5, 6, 7]],
            ins=[ap["qk_in"].opt()], outs=[ap["qk_out"].opt()]),
            r=[("qk_in",)], w=[("qk_out",)], sem="ccQ", inc=1)
        Sc.dma("pool", lambda g: g.dma_start(
            out=normsc[0:1, 0:192].rearrange("a (r c) -> a r c", c=48),
            in_=ap["qk_out"].rearrange("(r f) t -> f r t", f=QKR)
            [4096:4097, :, 0:48]),
            r=[("qk_out",)], w=[("nsc",)], sem="nrr")
        Sc.cop("dve", lambda v: v.tensor_copy(
            normsc[0:1, 192:208], normsc[0:1, 32:48]),
            r=[("nsc",)], w=[("nsc",)])
        for rq in range(1, 4):
            Sc.cop("dve", lambda v, rq=rq: v.tensor_max(
                normsc[0:1, 192 + rq * 16:192 + (rq + 1) * 16],
                normsc[0:1, 192 + (rq - 1) * 16:192 + rq * 16],
                normsc[0:1, rq * 48 + 32:rq * 48 + 48]),
                r=[("nsc",)], w=[("nsc",)])
        for rq in range(4):
            for tau in range(2):
                base = 256 + rq * 32 + tau * 16
                Sc.cop("dve", lambda v, rq=rq, tau=tau, base=base: v.tensor_mul(
                    normsc[0:1, base:base + 16],
                    normsc[0:1, rq * 48 + tau * 16:rq * 48 + (tau + 1) * 16],
                    normsc[0:1, 192 + rq * 16:192 + (rq + 1) * 16]),
                    r=[("nsc",)], w=[("nsc",)])
        Sc.cop("act", lambda a: a.activation(
            out=normsc[0:1, 256:384], in_=normsc[0:1, 256:384], func=AF.Ln,
            scale=(1.05 * SCALE) ** 2),
            r=[("nsc",)], w=[("nsc",)])
        Sc.cop("act", lambda a: a.activation(
            out=normsc[0:1, 256:384], in_=normsc[0:1, 256:384], func=AF.Exp,
            scale=0.5),
            r=[("nsc",)], w=[("nsc",)])
        Sc.cop("dve", lambda v: v.tensor_scalar(
            out=normsc[0:1, 256:384], in0=normsc[0:1, 256:384],
            scalar1=-1.0, scalar2=None, op0=ALU.mult),
            r=[("nsc",)], w=[("nsc",)])
        for vog in range(4):
            slot = vog % 2
            vsl = wVv(slot)
            Sc.dma("pool", lambda g, vog=vog, vsl=vsl: g.dma_start(
                out=vsl[:, 0:8192],
                in_=ap["w_qkv"][:, 65536 + vog * 8192: 65536 + (vog + 1) * 8192]),
                w=wVk[slot], sem="wV%d" % slot)
            for rt in range(NT):
                bk = 4 + (rt % 2)

                def vmm(t, rt=rt, bk=bk, vsl=vsl):
                    last = None
                    for k in range(KC):
                        last = t.matmul(
                            pg[bk][:, :],
                            ufeat[:, k * TOK + rt * 128: k * TOK + rt * 128 + 128],
                            vsl[:, k * 512:(k + 1) * 512],
                            start=(k == 0), stop=(k == KC - 1))
                    return last
                Sc.cop("pe", vmm, r=list(wVk[slot]) + ufk(0, KC),
                       w=[("pg%d" % bk,)])
                evac_store(bk, ap["v_in"][rt * 128:(rt + 1) * 128,
                                          vog * 512:(vog + 1) * 512],
                           [("v_in",)])
        Sc.dma("pool", lambda g: g.collective_compute(
            "AllGather", ALU.bypass,
            replica_groups=[[0, 1, 2, 3], [4, 5, 6, 7]],
            ins=[ap["v_in"].opt()], outs=[ap["v_out"].opt()]),
            r=[("v_in",)], w=[("v_out",)], sem="ccV", inc=1)

    def emit_attn_core():
        Sc.alias([("den",)], [("atl2",)])
        Sc.alias([("pb", i) for i in range(6)] + [("qb", 0), ("qb", 1)],
                 [("stg", 0), ("stg", 1)])
        Sc.alias([("rd", 0), ("rd", 1)], [("ustg", 0), ("ustg", 1)])

        engs = [Sc.h[e] for e in ("pe", "act", "dve", "sp")]
        axi = AxisInfo(NC, 1)
        idxr = [nc.axis_index(axi, engines=[e.engine]) for e in engs]
        s0 = Sc.snap()
        states = []
        for c in nc.Switch(engines=engs, index=idxr, n=NC):
            Sc.restore(s0)
            b, a = c // 4, c % 4
            for h in range(HEADS):
                ksl = h % 2
                kh = ufeat[:, ksl * 4096:(ksl + 1) * 4096]
                vh = ufeat[:, 8192 + ksl * 4096: 8192 + (ksl + 1) * 4096]
                khk = ufk(4 * ksl, 4 * ksl + 4)
                vhk = ufk(8 + 4 * ksl, 8 + 4 * ksl + 4)
                Sc.dma("sp", lambda s, h=h, kh=kh: s.dma_start(
                    out=kh.rearrange("p (r t) -> p r t", t=TOK),
                    in_=ap["qk_out"].rearrange("(r f) t -> f r t", f=QKR)
                    [2048 + h * 128: 2048 + (h + 1) * 128, :, :]),
                    r=[("qk_out",)], w=khk, sem="kk%d" % ksl)
                Sc.dma("sp", lambda s, h=h, vh=vh: s.dma_start(
                    out=vh.rearrange("p (kt d) -> p kt d", d=128),
                    in_=ap["v_out"].rearrange("(kt p) f -> p kt f", p=128)
                    [:, :, h * 128:(h + 1) * 128]),
                    r=[("v_out",)], w=vhk, sem="vv%d" % ksl)
                jqs = (a, 7 - a)
                nks = (4 * (a + 1), 4 * (8 - a))
                for ji, jq in enumerate(jqs):
                    rq = jq // 2
                    qh = qbuf[:, ji * 512:(ji + 1) * 512]
                    Sc.dma("sp", lambda s, rq=rq, h=h, jq=jq, qh=qh: s.dma_start(
                        out=qh,
                        in_=ap["qk_out"][rq * QKR + h * 128:
                                         rq * QKR + (h + 1) * 128,
                                         (jq % 2) * 512:((jq % 2) + 1) * 512]),
                        r=[("qk_out",)], w=[("qb", ji)], sem="qq%d" % ji)
                    ncol = 256 + rq * 32 + (jq % 2) * 16 + h
                    Sc.cop("pe", lambda t, ji=ji, ncol=ncol: t.matmul(
                        pg[4 + ji][:, 0:1], onesf[0:1, :],
                        normsc[0:1, ncol:ncol + 1],
                        start=True, stop=True),
                        r=[("nsc",), ("onesf",)], w=[("pg%d" % (4 + ji),)])
                    Sc.cop("dve", lambda v, ji=ji: v.tensor_copy(
                        smalls[:, 8 + ji:9 + ji], pg[4 + ji][:, 0:1]),
                        r=[("pg%d" % (4 + ji),)], w=[("sm", 8 + ji)])
                for step in range(max(nks) + 2):
                    for ji in range(2):
                        nk = nks[ji]
                        qh = qbuf[:, ji * 512:(ji + 1) * 512]
                        if 2 <= step < nk + 2:
                            kt = step - 2
                            prc = ji * 3 + kt % 3
                            Sc.cop("pe", lambda t, kt=kt, prc=prc, vh=vh,
                                   ji=ji, nk=nk:
                                   t.matmul(pg[2 + ji][:, :],
                                            vh[:, kt * 128:(kt + 1) * 128],
                                            pbuf[:, prc * 512:(prc + 1) * 512],
                                            start=(kt == 0),
                                            stop=(kt == nk - 1)),
                                   r=vhk + [("pb", prc)],
                                   w=[("pg%d" % (2 + ji),)])
                            Sc.cop("pe", lambda t, kt=kt, prc=prc, ji=ji,
                                   nk=nk:
                                   t.matmul(
                                       pg[4 + ji][0:1, :], onesb[:, 0:1],
                                       pbuf[:, prc * 512:(prc + 1) * 512],
                                       start=(kt == 0), stop=(kt == nk - 1)),
                                   r=[("pb", prc), ("onesb",)],
                                   w=[("pg%d" % (4 + ji),)])
                        if step < nk:
                            kt = step
                            stbk = ji
                            prc = ji * 3 + kt % 3
                            Sc.cop("pe", lambda t, kt=kt, stbk=stbk, kh=kh,
                                   qh=qh:
                                   t.matmul(pg[stbk][:, :],
                                            kh[:, kt * 128:(kt + 1) * 128], qh,
                                            start=True, stop=True),
                                   r=khk + [("qb", ji)],
                                   w=[("pg%d" % stbk,)])
                            Sc.cop("act", lambda a_, prc=prc, stbk=stbk, ji=ji:
                                   a_.activation(
                                       out=pbuf[:, prc * 512:(prc + 1) * 512],
                                       in_=pg[stbk][:, :], func=AF.Exp,
                                       scale=SCALE,
                                       bias=smalls[:, 8 + ji:9 + ji]),
                                   r=[("pg%d" % stbk,), ("sm", 8 + ji)],
                                   w=[("pb", prc)])
                            if kt >= nk - 4:
                                o = 384 - (kt - (nk - 4)) * 128
                                Sc.cop("dve", lambda v, prc=prc, o=o:
                                       v.tensor_mul(
                                           pbuf[:, prc * 512:(prc + 1) * 512],
                                           pbuf[:, prc * 512:(prc + 1) * 512],
                                           cmask[:, o:o + 512]),
                                       r=[("pb", prc), ("cmask",)],
                                       w=[("pb", prc)])
                for ji in range(2):
                    Sc.cop("dve", lambda v, ji=ji: v.tensor_copy(
                        denr, pg[4 + ji][0:1, :]),
                        r=[("pg%d" % (4 + ji),)], w=[("den",)])
                    Sc.cop("pe", lambda t, ji=ji: t.matmul(
                        pg[ji][:, :], onesf[0:1, :], denr,
                        start=True, stop=True),
                        r=[("den",), ("onesf",)], w=[("pg%d" % ji,)])
                    def _recip(v, ji=ji):
                        with nc.allow_low_precision(
                                reason="bf16 reciprocal of softmax denom"):
                            return v.reciprocal(
                                rds[:, ji * 512:(ji + 1) * 512], pg[ji][:, :])
                    Sc.cop("dve", _recip, r=[("pg%d" % ji,)], w=[("rd", ji)])
                    dst = big1[:, h * TOK + ji * 512: h * TOK + (ji + 1) * 512]
                    Sc.cop("dve", lambda v, ji=ji, dst=dst:
                           v.tensor_mul(dst, pg[2 + ji][:, :],
                                        rds[:, ji * 512:(ji + 1) * 512]),
                           r=[("pg%d" % (2 + ji),), ("rd", ji)],
                           w=[("b1", h, ji)])
            states.append(Sc.snap())
        Sc.merge(states)
        Sc.alias([("atl2",)], [("den",)])
        Sc.alias([("stg", 0), ("stg", 1)],
                 [("pb", i) for i in range(6)] + [("qb", 0), ("qb", 1)])
        Sc.alias([("ustg", 0), ("ustg", 1)], [("rd", 0), ("rd", 1)])


    def emit_outproj():
        def ao_cb(rt, quar, bk):
            evac_store(bk, ap["ao_in"][rt * 128:(rt + 1) * 128,
                                       quar * 512:(quar + 1) * 512],
                       [("ao_in",)])
        emit_down(ap["w_out"], 0, HEADS, ao_cb)
        Sc.dma("pool", lambda g: g.collective_compute(
            "AllGather", ALU.bypass,
            replica_groups=[[0, 1, 2, 3], [4, 5, 6, 7]],
            ins=[ap["ao_in"].opt()], outs=[ap["ao_out"].opt()]),
            r=[("ao_in",)], w=[("ao_out",)], sem="ccO", inc=1)

    def emit_attn_resid():
        engs = [Sc.h["sp"]]
        axi = AxisInfo(NC, 1)
        idxr = [nc.axis_index(axi, engines=[e.engine]) for e in engs]
        s0 = Sc.snap()
        states = []
        for c in nc.Switch(engines=engs, index=idxr, n=NC):
            Sc.restore(s0)
            b, a = c // 4, c % 4
            for gi, g in enumerate((2 * a, 2 * a + 1)):
                r_g = min(g, 7 - g)
                off4 = 0 if g <= 3 else 4
                Sc.dma("sp", lambda s, r_g=r_g, off4=off4, gi=gi: s.dma_start(
                    out=ufeat[:, gi * 4 * H:(gi + 1) * 4 * H]
                    .rearrange("p (q f) -> p q f", f=H),
                    in_=ap["ao_out"].rearrange("(rr p) f -> p rr f", p=128)
                    [:, r_g * 8 + off4: r_g * 8 + off4 + 4, :]),
                    r=[("ao_out",)], w=ufk(gi * 8, gi * 8 + 8),
                    sem="aop%d" % gi)
            states.append(Sc.snap())
        Sc.merge(states)
        for q in range(NT):
            gi, qq = q // 4, q % 4
            pk = ufeat[:, gi * 4 * H + qq * H: gi * 4 * H + (qq + 1) * H]
            pkk = ufk(gi * 8 + 2 * qq, gi * 8 + 2 * qq + 2)
            xs = x_sb[:, q * H:(q + 1) * H]
            Sc.cop("dve", lambda v, xs=xs, pk=pk: v.tensor_add(xs, xs, pk),
                   r=[("x", q)] + pkk, w=[("x", q)])
            Sc.cop("dve", lambda v, xs=xs: v.tensor_add(xs, xs, brep[:, :]),
                   r=[("x", q), ("brep",)], w=[("x", q)])

    def dump(name, src_ap, keys, cast=False):
        if not dbg:
            return
        q = "pool" if cast else "sp"
        Sc.dma(q, lambda s: s.dma_start(out=dbg_outs[name], in_=src_ap),
               r=keys, w=(), sem="dbg")

    emit_init()
    if "ffn0" in parts:
        emit_ffn(0)
        if dbg:
            dump("d_u", ap["u_tok"][:, :], [("u_tok", q) for q in range(NT)])
            dump("d_p", p_sb[:, :], [("p", q) for q in range(NT)])
            dump("d_idx4", idx4f[:, :], [("idx4", q) for q in range(NT)])
            dump("d_w4", w4[:, :], [("w4", q) for q in range(NT)])
            dump("d_slot", slot_tok[:, :], [("slot", q) for q in range(NT)])
            dump("d_gidx", gidx[:, :], [("gidx", q) for q in range(NT)])
            dump("d_wfin", wfin[:, :], [("wfin", q) for q in range(NT)])
            dump("d_eo", ap["eo_in"][:, :], [("eo_in", 0), ("eo_in", 1)])
            dump("d_atl", ap["at_in"][:, :], [("at_in",)])
            dump("d_bufidx", bufidx[:, :], [("bufidx",)])
            dump("d_slotc", slotcols[:, :], [("slc", tt2) for tt2 in range(64)])
            dump("d_x1", x_sb[:].rearrange("p (q f) -> p q f", f=H),
                 [("x", q) for q in range(NT)], cast=True)
    if "attn" in parts:
        emit_qkv()
        emit_attn_core()
        emit_outproj()
        emit_attn_resid()
        if dbg:
            dump("d_km", kmax8[:, :], [("m2s",), ("kmx",), ("qmx",)])
            dump("d_negm", smalls[:, 8:10], [("sm", 8), ("sm", 9)])
            dump("d_x2", x_sb[:].rearrange("p (q f) -> p q f", f=H),
                 [("x", q) for q in range(NT)], cast=True)
    if "ffn1" in parts:
        emit_ffn(1)
    for q in range(NT):
        Sc.dma("pool", lambda g, q=q: g.dma_start(
            out=ap["y"][q * 128:(q + 1) * 128, :],
            in_=x_sb[:, q * H:(q + 1) * H]),
            r=[("x", q)], w=(), sem="yo")
    Sc.h["pool"].wait_ge(Sc.sems["yo"], Sc.cnt["yo"])
    return nc


def _gu_blocks(n_of):
    ogs, o = [], 0
    while o < n_of:
        m = min(2, n_of - o)
        ogs.append((o, m))
        o += m
    return ogs


def pack_gateup(w, width=None):
    Wd = w.shape[1]
    a = w.reshape(KC, 128, Wd)
    blocks = []
    for (o, m) in _gu_blocks(Wd // 128):
        blocks.append(np.ascontiguousarray(
            a[:, :, o * 128:(o + m) * 128].transpose(1, 0, 2).reshape(128, -1)))
    return np.ascontiguousarray(np.concatenate(blocks, axis=1))


def pack_down(w):
    n_ic = w.shape[0] // 128
    a = w.reshape(n_ic, 128, w.shape[1])
    blocks = [np.ascontiguousarray(
        a[:, :, q * 512:(q + 1) * 512].transpose(1, 0, 2).reshape(128, -1))
        for q in range(4)]
    return np.ascontiguousarray(np.concatenate(blocks, axis=1))


def pack_qkv(w):
    a = w.reshape(KC, 128, 3 * H)
    blocks = []
    for og in range(16):
        blocks.append(np.ascontiguousarray(
            a[:, :, og * 256:(og + 1) * 256].transpose(1, 0, 2)
            .reshape(128, -1)))
    for vog in range(4):
        blocks.append(np.ascontiguousarray(
            a[:, :, 2 * H + vog * 512: 2 * H + (vog + 1) * 512]
            .transpose(1, 0, 2).reshape(128, -1)))
    return np.ascontiguousarray(np.concatenate(blocks, axis=1))


def make_inmaps(inputs):
    X = np.ascontiguousarray(np.asarray(inputs["L0_attn_x"],
                                        dtype=np.float32).reshape(T, H))
    wg = np.ascontiguousarray(np.asarray(inputs["L0_ffn_w_gate"], np.float32))
    wu = np.ascontiguousarray(np.asarray(inputs["L0_ffn_w_up"], np.float32))
    wd = np.ascontiguousarray(np.asarray(inputs["L0_ffn_w_down"], np.float32))
    w_qkv = np.ascontiguousarray(np.asarray(inputs["L0_attn_w_qkv"], np.float32))
    w_out = np.ascontiguousarray(np.asarray(inputs["L0_attn_w_out"], np.float32))
    b_out = np.ascontiguousarray(np.asarray(inputs["L0_attn_b_out"],
                                            np.float32).reshape(1, H))
    gate_w = np.ascontiguousarray(np.asarray(inputs["L0_ffn_gate_w"], np.float32))
    swg = np.ascontiguousarray(np.asarray(inputs["L0_ffn_sw_gate"], np.float32))
    swu = np.ascontiguousarray(np.asarray(inputs["L0_ffn_sw_up"], np.float32))
    swd = np.ascontiguousarray(np.asarray(inputs["L0_ffn_sw_down"], np.float32))
    swg_p = pack_gateup(swg)
    swu_p = pack_gateup(swu)
    swd_p = pack_down(swd)
    wqkv_p = pack_qkv(w_qkv)
    wout_p = pack_down(w_out)
    c64v = np.array([[64.0], [1.0]], np.float32)
    in_maps = []
    for c in range(NC):
        selT = np.zeros((E, 2), np.float32)
        selT[2 * c, 0] = 1.0
        selT[2 * c + 1, 1] = 1.0
        pm = np.zeros((E, 16), np.float32)
        pm[:, :2 * c] = 1.0
        wg2 = wg.reshape(E, H, II)[2 * c:2 * c + 2]
        wu2 = wu.reshape(E, H, II)[2 * c:2 * c + 2]
        wd2 = wd.reshape(E, II, H)[2 * c:2 * c + 2]
        xr = np.ascontiguousarray(
            X[c * TOK:(c + 1) * TOK].reshape(NT, 128, H)
            .transpose(1, 0, 2).reshape(128, NT * H))
        in_maps.append({
            "x": xr,
            "w_qkv": wqkv_p, "w_out": wout_p, "b_out": b_out,
            "gate_w": gate_w,
            "wg": np.concatenate([pack_gateup(wg2[e]) for e in range(2)],
                                 axis=1),
            "wu": np.concatenate([pack_gateup(wu2[e]) for e in range(2)],
                                 axis=1),
            "wd": np.concatenate([pack_down(wd2[e]) for e in range(2)],
                                 axis=1),
            "swg": swg_p, "swu": swu_p, "swd": swd_p,
            "selT": selT, "pm": pm, "c64": c64v,
        })
    return in_maps


def kernel(**inputs):
    nc = build()
    in_maps = make_inmaps(inputs)
    res = run_bass_kernel_spmd(nc, in_maps, core_ids=list(range(NC)))
    y = np.concatenate([res.results[c]["y"] for c in range(NC)], axis=0)
    return y.reshape(B, S, H).astype(np.float32)


# version 11
# speedup vs baseline: 1.0666x; 1.0257x over previous
import numpy as np
import concourse.bass as bass
import concourse.mybir as mybir
from concourse.bass_types import AxisInfo
from concourse.bass_utils import run_bass_kernel_spmd

F32 = mybir.dt.float32
BF16 = mybir.dt.bfloat16
I32 = mybir.dt.int32
U32 = mybir.dt.uint32
AF = mybir.ActivationFunctionType
ALU = mybir.AluOpType
AX = mybir.AxisListType

NC = 8
B, S, H = 2, 4096, 2048
T = B * S
TOK = T // NC
E, K4, II, IS, CAP = 16, 4, 1408, 2816, 1024
HEADS, DH = 16, 128
KC = H // 128
NT = TOK // 128
QKR = 2 * H + 32
SCALE = DH ** -0.5
EPS = 1e-5
DUMP = 2 * CAP
CKW = 512


class Sched:
    ENG = ("pe", "act", "dve", "pool", "sp")

    def __init__(self, nc):
        self.nc = nc
        self.h = {"pe": nc.tensor, "act": nc.scalar, "dve": nc.vector,
                  "pool": nc.gpsimd, "sp": nc.sync}
        self.sems = {}
        self.cnt = {}
        for e in self.ENG:
            self.sems[e] = nc.alloc_semaphore("sem_" + e)
            self.cnt[e] = 0
        self.lw = {}
        self.lr = {}
        self.waited = {e: {} for e in self.ENG}

    def sem(self, name):
        if name not in self.sems:
            self.sems[name] = self.nc.alloc_semaphore("sem_" + name)
        if name not in self.cnt:
            self.cnt[name] = 0
        return self.sems[name]

    def _wait(self, eng, sname, val):
        if val <= 0:
            return
        if eng == "pe" and sname == "pe":
            return
        if self.waited[eng].get(sname, 0) >= val:
            return
        self.h[eng].wait_ge(self.sems[sname], val)
        self.waited[eng][sname] = val

    def _pre(self, eng, r, w):
        for k in r:
            for sname, val in self.lw.get(k, {}).items():
                self._wait(eng, sname, val)
        for k in w:
            for sname, val in self.lw.get(k, {}).items():
                self._wait(eng, sname, val)
            for sname, val in self.lr.get(k, {}).items():
                self._wait(eng, sname, val)

    def _post(self, sname, val, r, w):
        for k in w:
            self.lw[k] = {sname: val}
            self.lr[k] = {}
        for k in r:
            d = self.lr.setdefault(k, {})
            if d.get(sname, 0) < val:
                d[sname] = val

    def cop(self, eng, emit, r=(), w=(), inc=1):
        self._pre(eng, r, w)
        inst = emit(self.h[eng])
        self.cnt[eng] += inc
        inst.then_inc(self.sems[eng], inc)
        self._post(eng, self.cnt[eng], r, w)

    def dma(self, q, emit, r=(), w=(), sem=None, inc=16):
        sname = sem or q
        self.sem(sname)
        self._pre(q, r, w)
        inst = emit(self.h[q])
        self.cnt[sname] += inc
        inst.then_inc(self.sems[sname], inc)
        self._post(sname, self.cnt[sname], r, w)

    def alias(self, dst_keys, src_keys):
        agg = {}
        for k in src_keys:
            for d in (self.lw.get(k, {}), self.lr.get(k, {})):
                for sn, v in d.items():
                    if agg.get(sn, 0) < v:
                        agg[sn] = v
        for k in dst_keys:
            self.lw[k] = dict(agg)
            self.lr[k] = {}

    def snap(self):
        import copy
        return (dict(self.cnt), copy.deepcopy(self.lw), copy.deepcopy(self.lr),
                copy.deepcopy(self.waited))

    def restore(self, s):
        import copy
        self.cnt = dict(s[0])
        self.lw = copy.deepcopy(s[1])
        self.lr = copy.deepcopy(s[2])
        self.waited = copy.deepcopy(s[3])

    def merge(self, states):
        cnts = [s[0] for s in states]
        for c in cnts[1:]:
            if c != cnts[0]:
                diff = {k: (cnts[0].get(k), c.get(k))
                        for k in set(cnts[0]) | set(c)
                        if cnts[0].get(k) != c.get(k)}
                raise AssertionError(f"switch branches unbalanced: {diff}")
        self.cnt = dict(cnts[0])
        lw, lr = {}, {}
        for s in states:
            for k, d in s[1].items():
                dd = lw.setdefault(k, {})
                for sn, val in d.items():
                    if dd.get(sn, 0) < val:
                        dd[sn] = val
            for k, d in s[2].items():
                dd = lr.setdefault(k, {})
                for sn, val in d.items():
                    if dd.get(sn, 0) < val:
                        dd[sn] = val
        self.lw, self.lr = lw, lr
        wt = {}
        for e in Sched.ENG:
            wt[e] = {}
            allk = set()
            for s in states:
                allk |= set(s[3][e].keys())
            for k in allk:
                vals = [s[3][e].get(k, 0) for s in states]
                if min(vals) > 0:
                    wt[e][k] = min(vals)
        self.waited = wt


def b2k(c0, c1):
    return [("b2g", g) for g in range(c0 // 512, (c1 + 511) // 512)]


def build(parts=("ffn0", "attn", "ffn1"), dbg=False):
    nc = bass.Bass("TRN2", target_bir_lowering=False)
    ap = {}

    def din(name, shape, dt=F32):
        ap[name] = nc.dram_tensor(name, shape, dt, kind="ExternalInput").ap()

    def dout(name, shape, dt=F32):
        ap[name] = nc.dram_tensor(name, shape, dt, kind="ExternalOutput").ap()

    def dint(name, shape, dt):
        ap[name] = nc.dram_tensor(name, shape, dt, kind="Internal").ap()

    din("x", [128, NT * H])
    din("w_qkv", [128, 3 * H * KC]); din("w_out", [128, H * KC])
    din("b_out", [1, H])
    din("gate_w", [E, H])
    din("wg", [128, 2 * H * II // 128]); din("wu", [128, 2 * H * II // 128])
    din("wd", [128, 2 * II * H // 128])
    din("swg", [128, H * IS // 128]); din("swu", [128, H * IS // 128])
    din("swd", [128, IS * H // 128])
    din("selT", [E, 2]); din("pm", [E, 16]); din("c64", [2, 1])
    dout("y", [TOK, H])

    dint("u_tok", [TOK, H], BF16)
    dint("ag_u", [T, H], BF16)
    dint("at_in", [E, TOK], F32)
    dint("at_out", [NC * E, TOK], F32)
    dint("eo_in", [2 * CAP, H], BF16)
    dint("eo_out", [E * CAP, H], BF16)
    dint("vln", [TOK, H], BF16)
    dint("qk_in", [QKR, TOK], BF16)
    dint("qk_out", [4 * QKR, TOK], BF16)
    dint("v_in", [TOK, H], BF16)
    dint("v_out", [4 * TOK, H], BF16)
    dint("ao_in", [TOK, H], BF16)
    dint("ao_out", [4 * TOK, H], BF16)

    dbg_outs = {}
    if dbg:
        for name, shape, dt in [
            ("d_u", [TOK, H], BF16), ("d_p", [TOK, E], F32),
            ("d_idx4", [TOK, K4], F32), ("d_w4", [TOK, K4], F32),
            ("d_slot", [TOK, E], F32), ("d_gidx", [TOK, K4], I32),
            ("d_wfin", [TOK, K4], F32),
            ("d_x1", [TOK, H], F32), ("d_x2", [TOK, H], F32),
            ("d_eo", [2 * CAP, H], BF16), ("d_atl", [E, TOK], F32),
            ("d_bufidx", [128, 16], I32), ("d_slotc", [128, 128], F32),
            ("d_km", [1, 16], F32), ("d_negm", [128, 2], F32),
        ]:
            dout(name, shape, dt)
            dbg_outs[name] = ap[name]

    def asb(name, shape, dt):
        return nc.alloc_sbuf_tensor(name, shape, dt)

    x_sb = asb("x_sb", [128, NT * H], BF16)
    ufeat = asb("ufeat", [128, KC * TOK], BF16)
    big1 = asb("big1", [128, 22 * TOK], BF16)
    big2 = asb("big2", [128, 2 * 11264], BF16)
    stg = asb("stg", [128, 2 * H], BF16)
    ustg = asb("ustg", [128, 2 * H], BF16)
    evst = asb("evst", [128, 2 * 512], BF16)
    brep = asb("brep", [128, H], BF16)
    scr = asb("scr", [128, 2048], F32)
    iotc = asb("iotc", [128, 1], F32)
    iot16 = asb("iot16", [128, 16], F32)
    identf = asb("identf", [128, 128], F32)
    identb = asb("identb", [128, 128], BF16)
    onesb = asb("onesb", [128, 128], BF16)
    cmask = asb("cmask", [128, 896], BF16)
    gwT = asb("gwT", [128, KC * E], BF16)
    tokid = asb("tokid", [128, 128], I32)
    w4 = asb("w4", [128, NT * K4], F32)
    idx4f = asb("idx4f", [128, NT * K4], F32)
    a_tok = asb("a_tok", [128, NT * E], F32)
    slot_tok = asb("slot_tok", [128, NT * E], F32)
    gidx = asb("gidx", [128, NT * K4], I32)
    wfin = asb("wfin", [128, NT * K4], F32)
    p_sb = asb("p_sb", [128, NT * E], F32)
    mx8 = asb("mx8", [128, 8], F32)
    ixt = asb("ixt", [128, 8], U32)
    smalls = asb("smalls", [128, 16], F32)
    at_loc = asb("at_loc", [16, TOK], F32)
    at_loc2 = asb("at_loc2", [16, TOK], F32)
    colsc = asb("colsc", [16, 16], F32)
    scA = asb("scA", [16, CKW], F32)
    scB = asb("scB", [16, 2 * CKW], F32)
    scC = asb("scC", [16, CKW], F32)
    selT_sb = asb("selT_sb", [16, 2], F32)
    pm_sb = asb("pm_sb", [16, 16], F32)
    c64_sb = asb("c64_sb", [2, 1], F32)
    c64b = asb("c64b", [2, 1], BF16)
    onesf = asb("onesf", [1, 128], F32)
    slotcols = asb("slotcols", [128, 128], F32)
    tokhi = asb("tokhi", [128, 64], BF16)
    toklo = asb("toklo", [128, 1], F32)
    hilo = asb("hilo", [128, 128], BF16)

    bufidx = asb("bufidx", [128, 16], I32)
    ismall = asb("ismall", [128, 2], I32)
    buff8 = asb("buff8", [128, 16], F32)
    norm48 = asb("norm48", [1, 64], F32)
    norm48b = asb("norm48b", [1, 64], BF16)
    normsc = asb("normsc", [1, 384], F32)
    pbuf = stg[:, 0:3072]
    qbuf = stg[:, 3072:4096]
    rds = ustg[:, 0:1024]
    ksq = evst[:, 0:512]
    kn2c = at_loc[0:1, 0:512]
    qn2 = at_loc[0:1, 512:1024]
    denr = at_loc2[0:1, 0:512]
    bufrow = ustg[0:2, 0:2048]
    buffv = [at_loc2[0:1, 0:1024], at_loc[0:1, 0:1024]]
    mrow = ustg[0:1, 0:512]
    gwstg = big1[0:16, 0:2048]
    boutstg = big1[0:1, 2048:4096]

    def wAv(slot):
        return big2[:, slot * 4096:(slot + 1) * 4096]

    def wBv(slot):
        return big2[:, 8192 + slot * 4096: 8192 + (slot + 1) * 4096]

    def wVv(slot):
        return big2[:, slot * 8192:(slot + 1) * 8192]

    wAk = [b2k(0, 4096), b2k(4096, 8192)]
    wBk = [b2k(8192, 12288), b2k(12288, 16384)]
    wVk = [b2k(0, 8192), b2k(8192, 16384)]

    pg = [nc.alloc_psum_tensor(f"pg{i}", [128, 512], F32) for i in range(6)]
    pbf2 = [nc.alloc_psum_tensor(f"pbf{i}", [128, 1024], BF16) for i in range(2)]
    pbf = pbf2[0]

    Sc = Sched(nc)
    evctr = [0]

    iq = []

    def drain(n):
        k = 0
        while iq and k < n:
            if iq[0] == "BAR":
                return
            iq.pop(0)()
            k += 1

    def drain_bar():
        if iq and iq[0] == "BAR":
            iq.pop(0)

    def flushq():
        while iq:
            it = iq.pop(0)
            if it != "BAR":
                it()

    def evac_store(bk, out_dram_ap, dkeys):
        sl = evctr[0] % 2
        evctr[0] += 1
        ev = evst[:, sl * 512:(sl + 1) * 512]
        Sc.cop("act", lambda a: a.activation(out=ev, in_=pg[bk][:, :],
                                             func=AF.Copy),
               r=[("pg%d" % bk,)], w=[("ev", sl)])
        Sc.dma("sp", lambda s: s.dma_start(out=out_dram_ap, in_=ev),
               r=[("ev", sl)], w=dkeys, sem="evd%d" % sl)

    def ufk(c0, c1):
        return [("uf", k) for k in range(c0, c1)]

    def emit_init():
        Sc.dma("pool", lambda g: g.dma_start(out=x_sb[:, :], in_=ap["x"]),
               w=[("x", q) for q in range(NT)], sem="xin")
        Sc.cop("pool", lambda g: g.iota(iotc[:, :], pattern=[[0, 1]], base=0,
               channel_multiplier=1, allow_small_or_imprecise_dtypes=True),
               w=[("iotc",)])
        Sc.cop("pool", lambda g: g.iota(iot16[:, :], pattern=[[1, 16]], base=0,
               channel_multiplier=0, allow_small_or_imprecise_dtypes=True),
               w=[("iot16",)])
        Sc.cop("pool", lambda g: g.iota(scr[:, 0:896], pattern=[[1, 896]], base=0,
               channel_multiplier=0, allow_small_or_imprecise_dtypes=True),
               w=[("scr",)])
        Sc.cop("pool", lambda g: g.iota(scr[:, 1024:1088], pattern=[[2, 64]],
               base=0, channel_multiplier=0,
               allow_small_or_imprecise_dtypes=True),
               w=[("scrE",)])
        Sc.cop("dve", lambda v: v.tensor_scalar(out=identf[:, :], in0=scr[:, 0:128],
               scalar1=iotc[:, 0:1], scalar2=None, op0=ALU.is_equal),
               r=[("scr",), ("iotc",)], w=[("identf",)])
        Sc.cop("dve", lambda v: v.tensor_copy(identb[:, :], identf[:, :]),
               r=[("identf",)], w=[("identb",)])
        Sc.cop("dve", lambda v: v.memset(onesb[:, :], 1.0), w=[("onesb",)])
        Sc.cop("dve", lambda v: v.memset(onesf[:, :], 1.0), w=[("onesf",)])
        Sc.cop("dve", lambda v: v.tensor_scalar(out=smalls[:, 2:3],
               in0=iotc[:, 0:1], scalar1=63.5, scalar2=None, op0=ALU.is_gt),
               r=[("iotc",)], w=[("sm", 2)])
        Sc.cop("dve", lambda v: v.tensor_scalar(out=tokhi[:, :],
               in0=scr[:, 1024:1088], scalar1=smalls[:, 2:3], scalar2=None,
               op0=ALU.add),
               r=[("scrE",), ("sm", 2)], w=[("tokhi",)])
        Sc.cop("dve", lambda v: v.tensor_scalar(out=smalls[:, 3:4],
               in0=smalls[:, 2:3], scalar1=64.0, scalar2=None, op0=ALU.mult),
               r=[("sm", 2)], w=[("sm", 3)])
        Sc.cop("dve", lambda v: v.tensor_sub(smalls[:, 4:5], iotc[:, 0:1],
               smalls[:, 3:4]),
               r=[("iotc",), ("sm", 3)], w=[("sm", 4)])
        Sc.cop("dve", lambda v: v.tensor_copy(toklo[:, :], smalls[:, 4:5]),
               r=[("sm", 4)], w=[("toklo",)])
        Sc.cop("dve", lambda v: v.tensor_copy(
            hilo[:, :].rearrange("p (t two) -> p t two", two=2)[:, :, 0],
            tokhi[:, :]),
            r=[("tokhi",)], w=[("hilo",)])
        Sc.cop("dve", lambda v: v.tensor_scalar(
            out=hilo[:, :].rearrange("p (t two) -> p t two", two=2)[:, :, 1],
            in0=onesb[:, 0:64], scalar1=toklo[:, 0:1], scalar2=None,
            op0=ALU.mult),
            r=[("onesb",), ("toklo",)], w=[("hilo",)])

        Sc.cop("dve", lambda v: v.tensor_scalar(out=smalls[:, 0:1], in0=iotc[:, 0:1],
               scalar1=384.0, scalar2=None, op0=ALU.add),
               r=[("iotc",)], w=[("sm", 0)])
        Sc.cop("dve", lambda v: v.tensor_scalar(out=cmask[:, :], in0=scr[:, 0:896],
               scalar1=smalls[:, 0:1], scalar2=None, op0=ALU.is_ge),
               r=[("scr",), ("sm", 0)], w=[("cmask",)])

        for i, (nm, tile, key) in enumerate([
                ("selT", selT_sb, "selT"), ("pm", pm_sb, "pm"),
                ("c64", c64_sb, "c64")]):
            Sc.dma("sp", lambda s, nm=nm, tile=tile: s.dma_start(
                out=tile[:, :], in_=ap[nm]), w=[(key,)], sem="sml%d" % i)
        Sc.dma("pool", lambda g: g.dma_start(out=gwstg, in_=ap["gate_w"]),
               w=[("b1", 0, 0), ("b1", 0, 1), ("b1", 1, 0), ("b1", 1, 1)],
               sem="smlg")
        Sc.dma("pool", lambda g: g.dma_start(out=boutstg, in_=ap["b_out"]),
               w=[("b1", 2, 0), ("b1", 2, 1), ("b1", 3, 0), ("b1", 3, 1)],
               sem="smlb")
        Sc.cop("dve", lambda v: v.tensor_copy(c64b[:, :], c64_sb[:, :]),
               r=[("c64",)], w=[("c64b",)])
        for k in range(KC):
            pb = pbf2[k % 2]
            pbk = "pbf%d" % (k % 2)
            Sc.cop("pe", lambda t, k=k, pb=pb: t.transpose(
                pb[:, 0:16], gwstg[:, k * 128:(k + 1) * 128],
                identb[0:16, 0:16]),
                r=[("b1", k // 8, 0), ("b1", k // 8, 1), ("identb",)],
                w=[(pbk,)])
            Sc.cop("dve", lambda v, k=k, pb=pb: v.tensor_copy(
                gwT[:, k * 16:(k + 1) * 16], pb[:, 0:16]),
                r=[(pbk,)], w=[("gwT",)])
        for j in range(4):
            Sc.cop("pe", lambda t, j=j: t.matmul(
                pg[5][:, :], onesb[0:1, :], boutstg[:, j * 512:(j + 1) * 512],
                start=True, stop=True),
                r=[("b1", 2, 0), ("b1", 2, 1), ("b1", 3, 0), ("b1", 3, 1),
                   ("onesb",)],
                w=[("pg5",)])
            Sc.cop("dve", lambda v, j=j: v.tensor_copy(
                brep[:, j * 512:(j + 1) * 512], pg[5][:, :]),
                r=[("pg5",)], w=[("brep",)])

    def emit_ln(dname):
        dest = ap[dname] if dname is not None else None
        for q in range(NT):
            xs = x_sb[:, q * H:(q + 1) * H]
            us = ustg[:, (q % 2) * H:((q % 2) + 1) * H]
            c0, c1 = 2 * (q % 8), 2 * (q % 8) + 1
            Sc.cop("act", lambda a, xs=xs, c0=c0: a.activation(
                out=scr[:, 0:H], in_=xs, func=AF.Copy,
                accum_out=smalls[:, c0:c0 + 1]),
                r=[("x", q)], w=[("scr",), ("sm", c0)])
            Sc.cop("act", lambda a, xs=xs, c1=c1: a.activation(
                out=scr[:, 0:H], in_=xs, func=AF.Square,
                accum_out=smalls[:, c1:c1 + 1]),
                r=[("x", q)], w=[("scr",), ("sm", c1)])
            Sc.cop("dve", lambda v, c0=c0: v.tensor_scalar(
                out=smalls[:, c0:c0 + 1], in0=smalls[:, c0:c0 + 1],
                scalar1=-1.0 / H, scalar2=None, op0=ALU.mult),
                r=[("sm", c0)], w=[("sm", c0)])
            Sc.cop("dve", lambda v, c0=c0: v.tensor_scalar(
                out=colsc[0:16, 15:16] if False else smalls[:, 14:15],
                in0=smalls[:, c0:c0 + 1],
                scalar1=smalls[:, c0:c0 + 1], scalar2=None, op0=ALU.mult),
                r=[("sm", c0)], w=[("m2",)])
            Sc.cop("dve", lambda v, c1=c1: v.tensor_scalar(
                out=smalls[:, c1:c1 + 1], in0=smalls[:, c1:c1 + 1],
                scalar1=1.0 / H, scalar2=EPS, op0=ALU.mult, op1=ALU.add),
                r=[("sm", c1)], w=[("sm", c1)])
            Sc.cop("dve", lambda v, c1=c1: v.tensor_sub(
                smalls[:, c1:c1 + 1], smalls[:, c1:c1 + 1], smalls[:, 14:15]),
                r=[("sm", c1), ("m2",)], w=[("sm", c1)])
            Sc.cop("act", lambda a, c1=c1: a.activation(
                out=smalls[:, c1:c1 + 1], in_=smalls[:, c1:c1 + 1],
                func=AF.Sqrt),
                r=[("sm", c1)], w=[("sm", c1)])
            Sc.cop("dve", lambda v, c1=c1: v.reciprocal(
                smalls[:, c1:c1 + 1], smalls[:, c1:c1 + 1]),
                r=[("sm", c1)], w=[("sm", c1)])
            Sc.cop("dve", lambda v, xs=xs, us=us, c0=c0, c1=c1: v.tensor_scalar(
                out=us, in0=xs, scalar1=smalls[:, c0:c0 + 1],
                scalar2=smalls[:, c1:c1 + 1], op0=ALU.add, op1=ALU.mult),
                r=[("x", q), ("sm", c0), ("sm", c1)], w=[("ustg", q % 2)])
            if dname is not None:
                Sc.dma("sp", lambda s, us=us, q=q: s.dma_start(
                    out=dest[q * 128:(q + 1) * 128, :], in_=us),
                    r=[("ustg", q % 2)], w=[(dname, q)], sem="us%d" % (q % 2))
            for k in range(KC):
                pb = pbf2[k % 2]
                pbk = "pbf%d" % (k % 2)
                Sc.cop("pe", lambda t, us=us, k=k, pb=pb: t.transpose(
                    pb[:, 0:128], us[:, k * 128:(k + 1) * 128],
                    identb[:, :]),
                    r=[("ustg", q % 2), ("identb",)], w=[(pbk,)])
                Sc.cop("dve", lambda v, k=k, q=q, pb=pb: v.tensor_copy(
                    ufeat[:, k * TOK + q * 128: k * TOK + (q + 1) * 128],
                    pb[:, 0:128]),
                    r=[(pbk,)], w=[("uf", k)] if False else [("uf", k),
                                                             ("ufq", k, q)])

    def emit_routing(inv):
        for tau in range(2):
            bank = 4 + tau

            def mmgrp(t, tau=tau, bank=bank):
                last = None
                for k in range(KC):
                    last = t.matmul(
                        pg[bank][0:16, :], gwT[:, k * 16:(k + 1) * 16],
                        ufeat[:, k * TOK + tau * 512:k * TOK + (tau + 1) * 512],
                        start=(k == 0), stop=(k == KC - 1))
                return last
            Sc.cop("pe", mmgrp, r=[("gwT",)] + ufk(0, KC), w=[("pg%d" % bank,)])
            Sc.cop("dve", lambda v, tau=tau, bank=bank: v.tensor_copy(
                at_loc2[:, tau * 512:(tau + 1) * 512], pg[bank][0:16, :]),
                r=[("pg%d" % bank,)], w=[("atl2",)])
        for q in range(NT):
            Sc.cop("pe", lambda t, q=q: t.transpose(
                pg[5][:, 0:16], at_loc2[:, q * 128:(q + 1) * 128],
                identf[0:16, 0:16]),
                r=[("atl2",), ("identf",)], w=[("pg6",)])
            Sc.cop("dve", lambda v, q=q: v.tensor_copy(
                p_sb[:, q * E:(q + 1) * E], pg[5][0:128, 0:16]),
                r=[("pg6",)], w=[("p", q)])
        for q in range(NT):
            pq = p_sb[:, q * E:(q + 1) * E]
            Sc.cop("dve", lambda v, pq=pq: v.reduce_max(
                smalls[:, 0:1], pq, axis=AX.X), r=[("p", q)], w=[("sm", 0)])
            Sc.cop("dve", lambda v: v.tensor_scalar(
                out=smalls[:, 0:1], in0=smalls[:, 0:1], scalar1=-1.0,
                scalar2=None, op0=ALU.mult), r=[("sm", 0)], w=[("sm", 0)])
            Sc.cop("act", lambda a, pq=pq: a.activation(
                out=pq, in_=pq, func=AF.Exp, bias=smalls[:, 0:1],
                accum_out=smalls[:, 1:2]),
                r=[("p", q), ("sm", 0)], w=[("p", q), ("sm", 1)])
            Sc.cop("dve", lambda v: v.reciprocal(smalls[:, 1:2], smalls[:, 1:2]),
                   r=[("sm", 1)], w=[("sm", 1)])
            Sc.cop("dve", lambda v, pq=pq: v.tensor_scalar(
                out=pq, in0=pq, scalar1=smalls[:, 1:2], scalar2=None,
                op0=ALU.mult), r=[("p", q), ("sm", 1)], w=[("p", q)])
            Sc.cop("dve", lambda v, pq=pq: v.max(out=mx8[:, :], in_=pq),
                   r=[("p", q)], w=[("mx8",)])
            Sc.cop("dve", lambda v, pq=pq: v.max_index(
                out=ixt[:, :], in_max=mx8[:, :], in_values=pq),
                r=[("p", q), ("mx8",)], w=[("ixt",)])
            Sc.cop("dve", lambda v, q=q: v.tensor_copy(
                idx4f[:, q * K4:(q + 1) * K4], ixt[:, 0:4]),
                r=[("ixt",)], w=[("idx4", q)])
            Sc.cop("dve", lambda v: v.tensor_reduce(
                smalls[:, 2:3], mx8[:, 0:4], axis=AX.X, op=ALU.add),
                r=[("mx8",)], w=[("sm", 2)])
            Sc.cop("dve", lambda v: v.tensor_scalar(
                out=smalls[:, 2:3], in0=smalls[:, 2:3], scalar1=1e-20,
                scalar2=None, op0=ALU.add), r=[("sm", 2)], w=[("sm", 2)])
            Sc.cop("dve", lambda v: v.reciprocal(smalls[:, 2:3], smalls[:, 2:3]),
                   r=[("sm", 2)], w=[("sm", 2)])
            Sc.cop("dve", lambda v, q=q: v.tensor_scalar(
                out=w4[:, q * K4:(q + 1) * K4], in0=mx8[:, 0:4],
                scalar1=smalls[:, 2:3], scalar2=None, op0=ALU.mult),
                r=[("mx8",), ("sm", 2)], w=[("w4", q)])
            dst = a_tok[:, q * E:(q + 1) * E]
            for k in range(K4):
                if k == 0:
                    Sc.cop("dve", lambda v, q=q, k=k, dst=dst: v.tensor_scalar(
                        out=dst, in0=iot16[:, :],
                        scalar1=idx4f[:, q * K4 + k:q * K4 + k + 1],
                        scalar2=None, op0=ALU.is_equal),
                        r=[("iot16",), ("idx4", q)], w=[("A", q)])
                else:
                    Sc.cop("dve", lambda v, q=q, k=k: v.tensor_scalar(
                        out=p_sb[:, q * E:(q + 1) * E] if False
                        else smalls[:, 0:16] if False else ixt[:, 0:4] if False
                        else scr[0:128, 1024:1040], in0=iot16[:, :],
                        scalar1=idx4f[:, q * K4 + k:q * K4 + k + 1],
                        scalar2=None, op0=ALU.is_equal),
                        r=[("iot16",), ("idx4", q)], w=[("scrE",)])
                    Sc.cop("dve", lambda v, dst=dst, q=q: v.tensor_add(
                        dst, dst, scr[0:128, 1024:1040]),
                        r=[("A", q), ("scrE",)], w=[("A", q)])
        for q in range(NT):
            Sc.cop("pe", lambda t, q=q: t.transpose(
                pg[5][0:16, 0:128], a_tok[:, q * E:(q + 1) * E], identf[:, :]),
                r=[("A", q), ("identf",)], w=[("pg6",)])
            Sc.cop("dve", lambda v, q=q: v.tensor_copy(
                at_loc[0:16, q * 128:(q + 1) * 128], pg[5][0:16, 0:128]),
                r=[("pg6",)], w=[("atl",)])
        Sc.dma("sp", lambda s: s.dma_start(out=ap["at_in"][:, :], in_=at_loc[:, :]),
               r=[("atl",)], w=[("at_in",)], sem="atw")
        Sc.dma("pool", lambda g: g.collective_compute(
            "AllGather", ALU.bypass, replica_groups=[list(range(NC))],
            ins=[ap["at_in"].opt()], outs=[ap["at_out"].opt()]),
            r=[("at_in",)], w=[("at_out",)], sem="ccA%d" % inv, inc=1)

    def emit_gateup(wsrc_g, goff, wsrc_u, uoff, n_of, dr=0, dr_from=0):
        ogs = []
        o = 0
        while o < n_of:
            m = min(2, n_of - o)
            ogs.append((o, m))
            o += m
        coff = 0
        for oi, (o, m) in enumerate(ogs):
            slot = oi % 2
            fw = KC * m * 128
            for wsrc, base, wview, wkeys, wsem in (
                    (wsrc_g, goff, wAv(slot), wAk[slot], "wA"),
                    (wsrc_u, uoff, wBv(slot), wBk[slot], "wB")):
                Sc.dma("pool", lambda g, wsrc=wsrc, base=base, wview=wview,
                       coff=coff, fw=fw:
                       g.dma_start(out=wview[:, 0:fw],
                                   in_=wsrc[:, base + coff: base + coff + fw]),
                       w=wkeys, sem=wsem + str(slot))
            coff += fw
            for wview, wkeys, is_gate in ((wAv(slot), wAk[slot], True),
                                          (wBv(slot), wBk[slot], False)):
                for mi in range(m):
                    for tau in range(2):
                        bk = tau
                        wspan = m * 128

                        def mm(t, mi=mi, tau=tau, bk=bk, wspan=wspan,
                               wview=wview):
                            last = None
                            for k in range(KC):
                                base = k * wspan + mi * 128
                                last = t.matmul(
                                    pg[bk][:, :], wview[:, base:base + 128],
                                    ufeat[:, k * TOK + tau * 512:
                                          k * TOK + (tau + 1) * 512],
                                    start=(k == 0), stop=(k == KC - 1))
                            return last
                        Sc.cop("pe", mm, r=list(wkeys) + ufk(0, KC),
                               w=[("pg%d" % bk,)])
                        c = o + mi
                        dst = big1[:, c * TOK + tau * 512:
                                   c * TOK + (tau + 1) * 512]
                        if is_gate:
                            Sc.cop("act", lambda a, bk=bk, dst=dst: a.activation(
                                out=dst, in_=pg[bk][:, :], func=AF.Silu),
                                r=[("pg%d" % bk,)], w=[("b1", c, tau)])
                        else:
                            Sc.cop("dve", lambda v, bk=bk, dst=dst: v.tensor_mul(
                                dst, dst, pg[bk][:, :]),
                                r=[("pg%d" % bk,), ("b1", c, tau)],
                                w=[("b1", c, tau)])
            if dr and oi >= dr_from:
                drain(dr)

    def emit_down(wsrc, base, n_ic, out_cb, dr=0):
        fw = n_ic * 512
        for quar in range(4):
            slot = quar % 2
            qsl = big2[:, slot * 11264: slot * 11264 + n_ic * 512]
            qkeys = b2k(slot * 11264, slot * 11264 + n_ic * 512)
            Sc.dma("pool", lambda g, quar=quar, qsl=qsl: g.dma_start(
                out=qsl[:, 0:fw],
                in_=wsrc[:, base + quar * fw: base + (quar + 1) * fw]),
                w=qkeys, sem="b2%d" % slot)
            for rt in range(NT):
                bk = rt % 2

                def dmm(t, rt=rt, bk=bk, qsl=qsl):
                    last = None
                    for ic in range(n_ic):
                        last = t.matmul(
                            pg[bk][:, :],
                            big1[:, ic * TOK + rt * 128: ic * TOK + rt * 128 + 128],
                            qsl[:, ic * 512:(ic + 1) * 512],
                            start=(ic == 0), stop=(ic == n_ic - 1))
                    return last
                rr = list(qkeys)
                for ic in range(n_ic):
                    rr += [("b1", ic, 0), ("b1", ic, 1)]
                Sc.cop("pe", dmm, r=rr, w=[("pg%d" % bk,)])
                out_cb(rt, quar, bk)
                if dr:
                    drain(dr)

    def queue_meta(inv):
        def c_iota():
            Sc.cop("pool", lambda g: g.iota(scr[:, 1024:2048],
                   pattern=[[1, 1024]], base=0, channel_multiplier=0,
                   allow_small_or_imprecise_dtypes=True),
                   w=[("scr",), ("scrE",), ("msl",)])
        iq.append(c_iota)

        def c_pre():
            Sc.cop("dve", lambda v: v.tensor_tensor_scan(
                out=at_loc2[:, :], data0=at_loc[:, :], data1=at_loc[:, :],
                initial=0.0, op0=ALU.add, op1=ALU.bypass),
                r=[("atl",)], w=[("atl2",)])
            Sc.cop("dve", lambda v: v.tensor_sub(at_loc2[:, :], at_loc2[:, :],
                   at_loc[:, :]),
                   r=[("atl",), ("atl2",)], w=[("atl2",)])
        iq.append(c_pre)

        nchunk = T // CKW
        for ck in range(nchunk):
            def c_chunk(ck=ck):
                par = ck % 2
                rnk, hf = ck // 2, ck % 2
                Sc.dma("sp", lambda s, rnk=rnk, hf=hf: s.dma_start(
                    out=scA[:, :],
                    in_=ap["at_out"][rnk * 16:(rnk + 1) * 16,
                                     hf * CKW:(hf + 1) * CKW]),
                    r=[("at_out",)], w=[("scA",)], sem="atc%d" % par)
                Sc.cop("dve", lambda v, ck=ck: v.tensor_reduce(
                    colsc[0:16, ck:ck + 1], scA[:, :], axis=AX.X, op=ALU.add),
                    r=[("scA",)], w=[("cnt", ck)])
                if ck == 0:
                    Sc.cop("dve", lambda v, par=par: v.tensor_tensor_scan(
                        out=scB[:, par * CKW:(par + 1) * CKW], data0=scA[:, :],
                        data1=scA[:, :], initial=0.0, op0=ALU.add,
                        op1=ALU.bypass),
                        r=[("scA",)], w=[("scB", par)])
                else:
                    pp = (ck - 1) % 2
                    Sc.cop("dve", lambda v, par=par, pp=pp: v.tensor_tensor_scan(
                        out=scB[:, par * CKW:(par + 1) * CKW], data0=scA[:, :],
                        data1=scA[:, :],
                        initial=scB[:, pp * CKW + CKW - 1: pp * CKW + CKW],
                        op0=ALU.add, op1=ALU.bypass),
                        r=[("scA",), ("scB", pp)], w=[("scB", par)])
                Sc.cop("dve", lambda v, par=par: v.tensor_sub(
                    scC[:, :], scB[:, par * CKW:(par + 1) * CKW], scA[:, :]),
                    r=[("scA",), ("scB", par)], w=[("scC",)])
                Sc.cop("dve", lambda v: v.tensor_scalar(
                    out=at_loc[0:16, 0:CKW], in0=scC[:, :],
                    scalar1=float(CAP), scalar2=None, op0=ALU.is_lt),
                    r=[("scC",)], w=[("atl",)])
                Sc.cop("dve", lambda v: v.tensor_mul(scA[:, :], scA[:, :],
                       at_loc[0:16, 0:CKW]),
                       r=[("scA",), ("atl",)], w=[("scA",)])
                Sc.cop("dve", lambda v: v.tensor_scalar(
                    out=scC[:, :], in0=scC[:, :], scalar1=1.0, scalar2=None,
                    op0=ALU.add),
                    r=[("scC",)], w=[("scC",)])
                Sc.cop("dve", lambda v: v.tensor_mul(scC[:, :], scC[:, :],
                       scA[:, :]),
                       r=[("scA",), ("scC",)], w=[("scC",)])
                Sc.cop("dve", lambda v: v.tensor_scalar(
                    out=scC[:, :], in0=scC[:, :], scalar1=1.0, scalar2=None,
                    op0=ALU.subtract),
                    r=[("scC",)], w=[("scC",)])
                Sc.cop("pe", lambda t: t.matmul(
                    pg[5][0:2, :], selT_sb[:, :], scC[:, :],
                    start=True, stop=True),
                    r=[("scC",), ("selT",)], w=[("pg5",)])
                Sc.cop("dve", lambda v: v.tensor_copy(scr[0:2, 0:CKW],
                       pg[5][0:2, :]),
                       r=[("pg5",)], w=[("msl",)])
                for i4 in range(CKW // 128):
                    tt = ck * (CKW // 128) + i4
                    Sc.cop("pe", lambda t, i4=i4: t.transpose(
                        pg[5][0:128, 16:18],
                        scr[0:2, i4 * 128:(i4 + 1) * 128], identf[0:2, 0:2]),
                        r=[("msl",), ("identf",)], w=[("pg5",)])
                    Sc.cop("dve", lambda v, tt=tt: v.tensor_copy(
                        slotcols[:, tt * 2:(tt + 1) * 2], pg[5][0:128, 16:18]),
                        r=[("pg5",)], w=[("slc", tt)])
            iq.append(c_chunk)

        def c_prefix():
            Sc.cop("dve", lambda v: v.tensor_mul(
                at_loc[0:16, 0:16], colsc[0:16, 0:16], pm_sb[0:16, :]),
                r=[("pm",)] + [("cnt", ck2) for ck2 in range(16)],
                w=[("atl",)])
            Sc.cop("dve", lambda v: v.tensor_reduce(
                colsc[0:16, 0:1], at_loc[0:16, 0:16], axis=AX.X, op=ALU.add),
                r=[("atl",)], w=[("pfx",)])
            Sc.cop("dve", lambda v: v.tensor_scalar(
                out=at_loc2[:, :], in0=at_loc2[:, :], scalar1=colsc[0:16, 0:1],
                scalar2=None, op0=ALU.add),
                r=[("atl2",), ("pfx",)], w=[("atl2",)])
        iq.append(c_prefix)

        for q in range(NT):
            def c_st(q=q):
                Sc.cop("pe", lambda t, q=q: t.transpose(
                    pg[5][:, 0:16], at_loc2[:, q * 128:(q + 1) * 128],
                    identf[0:16, 0:16]),
                    r=[("atl2",), ("identf",)], w=[("pg6",)])
                Sc.cop("dve", lambda v, q=q: v.tensor_copy(
                    slot_tok[:, q * E:(q + 1) * E], pg[5][0:128, 0:16]),
                    r=[("pg6",)], w=[("slot", q)])
            iq.append(c_st)

        def eq_build(k):
            tt, el = k // 2, k % 2
            ering = k % 4
            eqt = ustg[:, ering * 1024:(ering + 1) * 1024]
            Sc.cop("dve", lambda v, eqt=eqt, tt=tt, el=el: v.tensor_scalar(
                out=eqt, in0=scr[:, 1024:2048],
                scalar1=slotcols[:, tt * 2 + el:tt * 2 + el + 1],
                scalar2=None, op0=ALU.is_equal),
                r=[("scr",), ("slc", tt)], w=[("eqr", ering)])

        def eq_mms(k):
            tt, el = k // 2, k % 2
            ering = k % 4
            eqt = ustg[:, ering * 1024:(ering + 1) * 1024]
            for sh in range(2):
                bk = 2 + sh * 2 + el
                Sc.cop("pe", lambda t, tt=tt, eqt=eqt, sh=sh, bk=bk:
                       t.matmul(pg[bk][0:2, :], hilo[:, 2 * tt:2 * tt + 2],
                                eqt[:, sh * 512:(sh + 1) * 512],
                                start=(tt == 0), stop=(tt == 63)),
                       r=[("eqr", ering), ("hilo",)], w=[("pg%d" % bk,)])

        iq.append(lambda: Sc.alias(
            [("eqr", 0), ("eqr", 1), ("eqr", 2), ("eqr", 3)],
            [("ustg", 0), ("ustg", 1)]))
        for k in range(128):
            def c_eq(k=k):
                eq_build(k)
                if k >= 2:
                    eq_mms(k - 2)
            iq.append(c_eq)
        iq.append(lambda: eq_mms(126))
        iq.append(lambda: eq_mms(127))

        def c_buf():
            Sc.alias([("bufrow",)], [("ustg", 0), ("ustg", 1),
                                     ("eqr", 0), ("eqr", 1), ("eqr", 2),
                                     ("eqr", 3)])
            Sc.alias([("buff", 0)], [("atl2",)])
            Sc.alias([("buff", 1)], [("atl",)])
            for el in range(2):
                for sh in range(2):
                    bk = 2 + sh * 2 + el
                    Sc.cop("dve", lambda v, el=el, sh=sh, bk=bk: v.tensor_copy(
                        bufrow[0:2, el * 1024 + sh * 512:
                               el * 1024 + (sh + 1) * 512],
                        pg[bk][0:2, :]),
                        r=[("pg%d" % bk,)], w=[("bufrow",)])
            for el in range(2):
                for sh in range(2):
                    Sc.cop("pe", lambda t, el=el, sh=sh: t.matmul(
                        pg[2][0:1, :], c64b[:, :],
                        bufrow[0:2, el * 1024 + sh * 512:
                               el * 1024 + (sh + 1) * 512],
                        start=True, stop=True),
                        r=[("bufrow",), ("c64b",)], w=[("pg2",)])
                    Sc.cop("dve", lambda v, el=el, sh=sh: v.tensor_copy(
                        buffv[el][0:1, sh * 512:(sh + 1) * 512], pg[2][0:1, :]),
                        r=[("pg2",)], w=[("buff", el)])
            for el in range(2):
                for jj in range(8):
                    Sc.cop("pe", lambda t, el=el, jj=jj: t.transpose(
                        pg[5][0:128, 20:21],
                        buffv[el][0:1, jj * 128:(jj + 1) * 128],
                        identf[0:1, 0:1]),
                        r=[("buff", el), ("identf",)], w=[("pg5",)])
                    Sc.cop("dve", lambda v, el=el, jj=jj: v.tensor_copy(
                        buff8[:, el * 8 + jj: el * 8 + jj + 1],
                        pg[5][0:128, 20:21]),
                        r=[("pg5",)], w=[("buff8",)])
            Sc.cop("dve", lambda v: v.tensor_copy(bufidx[:, :], buff8[:, :]),
                   r=[("buff8",)], w=[("bufidx",)])
            Sc.alias([("ustg", 0), ("ustg", 1)], [("bufrow",)])
            Sc.alias([("atl2",)], [("buff", 0)])
            Sc.alias([("atl",)], [("buff", 1)])
        iq.append(c_buf)
        iq.append("BAR")

        for q in range(NT):
            for k in range(K4):
                def c_tok(q=q, k=k):
                    col = q * K4 + k
                    Sc.cop("dve", lambda v, q=q, col=col: v.tensor_scalar(
                        out=scr[0:128, 1024:1040], in0=iot16[:, :],
                        scalar1=idx4f[:, col:col + 1], scalar2=None,
                        op0=ALU.is_equal),
                        r=[("iot16",), ("idx4", q), ("scr",)], w=[("scrE",)])
                    Sc.cop("dve", lambda v, q=q: v.tensor_mul(
                        scr[0:128, 1024:1040], scr[0:128, 1024:1040],
                        slot_tok[:, q * E:(q + 1) * E]),
                        r=[("scrE",), ("slot", q)], w=[("scrE",)])
                    Sc.cop("dve", lambda v: v.tensor_reduce(
                        smalls[:, 3:4], scr[0:128, 1024:1040], axis=AX.X,
                        op=ALU.add),
                        r=[("scrE",), ("scr",)], w=[("sm", 3)])
                    Sc.cop("dve", lambda v: v.tensor_scalar(
                        out=smalls[:, 4:5], in0=smalls[:, 3:4],
                        scalar1=float(CAP), scalar2=None, op0=ALU.is_lt),
                        r=[("sm", 3)], w=[("sm", 4)])
                    Sc.cop("dve", lambda v, col=col, q=q: v.tensor_copy(
                        ismall[:, 0:1], idx4f[:, col:col + 1]),
                        r=[("idx4", q)], w=[("ism",)])
                    Sc.cop("dve", lambda v: v.tensor_scalar(
                        out=ismall[:, 1:2], in0=ismall[:, 0:1], scalar1=1,
                        scalar2=None, op0=ALU.bitwise_and),
                        r=[("ism",)], w=[("ism",)])
                    Sc.cop("dve", lambda v: v.tensor_copy(
                        smalls[:, 6:7], ismall[:, 1:2]),
                        r=[("ism",)], w=[("sm", 6)])
                    Sc.cop("dve", lambda v, col=col, q=q: v.tensor_scalar(
                        out=smalls[:, 5:6], in0=idx4f[:, col:col + 1],
                        scalar1=smalls[:, 6:7], scalar2=512.0,
                        op0=ALU.subtract, op1=ALU.mult),
                        r=[("idx4", q), ("sm", 6)], w=[("sm", 5)])
                    Sc.cop("dve", lambda v: v.tensor_scalar(
                        out=smalls[:, 6:7], in0=smalls[:, 6:7], scalar1=8192.0,
                        scalar2=smalls[:, 3:4], op0=ALU.mult, op1=ALU.add),
                        r=[("sm", 6), ("sm", 3)], w=[("sm", 6)])
                    Sc.cop("dve", lambda v: v.tensor_add(
                        smalls[:, 5:6], smalls[:, 5:6], smalls[:, 6:7]),
                        r=[("sm", 5), ("sm", 6)], w=[("sm", 5)])
                    Sc.cop("dve", lambda v: v.tensor_mul(
                        smalls[:, 5:6], smalls[:, 5:6], smalls[:, 4:5]),
                        r=[("sm", 4), ("sm", 5)], w=[("sm", 5)])
                    Sc.cop("dve", lambda v, col=col: v.tensor_copy(
                        gidx[:, col:col + 1], smalls[:, 5:6]),
                        r=[("sm", 5)], w=[("gidx", q)])
                    Sc.cop("dve", lambda v, col=col, q=q: v.tensor_scalar(
                        out=wfin[:, col:col + 1], in0=w4[:, col:col + 1],
                        scalar1=smalls[:, 4:5], scalar2=0.5,
                        op0=ALU.mult, op1=ALU.mult),
                        r=[("w4", q), ("sm", 4)], w=[("wfin", q)])
                iq.append(c_tok)

    def emit_experts(inv):
        while iq and iq[0] != "BAR":
            iq.pop(0)()
        for el in range(2):
            for j in range(8):
                sl = j % 2
                st = stg[:, sl * H:(sl + 1) * H]
                col = el * 8 + j
                Sc.dma("pool", lambda g, col=col, st=st: g.indirect_dma_start(
                    out=st, out_offset=None,
                    in_=ap["ag_u"][:, :],
                    in_offset=bass.IndirectOffsetOnAxis(
                        bufidx[:, col:col + 1], 0)),
                    r=[("bufidx",), ("ag_u",)], w=[("stg", sl)],
                    sem="xgg%d" % sl)
                for k in range(KC):
                    pb = pbf2[k % 2]
                    pbk = "pbf%d" % (k % 2)
                    Sc.cop("pe", lambda t, st=st, k=k, pb=pb: t.transpose(
                        pb[:, 0:128], st[:, k * 128:(k + 1) * 128],
                        identb[:, :]),
                        r=[("stg", sl), ("identb",)], w=[(pbk,)])
                    Sc.cop("dve", lambda v, k=k, j=j, pb=pb: v.tensor_copy(
                        ufeat[:, k * TOK + j * 128: k * TOK + (j + 1) * 128],
                        pb[:, 0:128]),
                        r=[(pbk,)], w=[("uf", k)])
            if el == 0:
                drain_bar()
            emit_gateup(ap["wg"], el * (H * II // 128),
                        ap["wu"], el * (H * II // 128), II // 128,
                        dr=(6 if el == 0 else 0))

            def eo_cb(rt, quar, bk, el=el):
                evac_store(bk,
                           ap["eo_in"][el * CAP + rt * 128:
                                       el * CAP + (rt + 1) * 128,
                                       quar * 512:(quar + 1) * 512],
                           [("eo_in", el)])
            emit_down(ap["wd"], el * (II * H // 128), II // 128, eo_cb)
            Sc.dma("pool", lambda g, el=el: g.collective_compute(
                "AllGather", ALU.bypass, replica_groups=[list(range(NC))],
                ins=[ap["eo_in"][el * CAP:(el + 1) * CAP, :].opt()],
                outs=[ap["eo_out"][el * NC * CAP:(el + 1) * NC * CAP, :].opt()]),
                r=[("eo_in", el)], w=[("eo_out", el)],
                sem="ccE%d_%d" % (inv, el), inc=1)

    def emit_combine(inv):
        flushq()
        for q in range(NT):
            for k in range(K4):
                sl = k % 2
                st = stg[:, sl * H:(sl + 1) * H]
                col = q * K4 + k
                Sc.dma("pool", lambda g, col=col, st=st: g.indirect_dma_start(
                    out=st, out_offset=None,
                    in_=ap["eo_out"][:, :],
                    in_offset=bass.IndirectOffsetOnAxis(gidx[:, col:col + 1], 0)),
                    r=[("gidx", q), ("eo_out", 0), ("eo_out", 1)],
                    w=[("stg", sl)],
                    sem="eg%d" % sl)
                if k == 0:
                    Sc.cop("dve", lambda v, col=col, st=st: v.tensor_scalar(
                        out=scr[:, 0:H], in0=st, scalar1=wfin[:, col:col + 1],
                        scalar2=None, op0=ALU.mult),
                        r=[("stg", sl), ("wfin", q)], w=[("scr",)])
                else:
                    Sc.cop("dve", lambda v, col=col, st=st: v.scalar_tensor_tensor(
                        out=scr[:, 0:H], in0=st, scalar=wfin[:, col:col + 1],
                        in1=scr[:, 0:H], op0=ALU.mult, op1=ALU.add),
                        r=[("stg", sl), ("wfin", q), ("scr",)], w=[("scr",)])
            xs = x_sb[:, q * H:(q + 1) * H]
            Sc.cop("dve", lambda v, xs=xs, q=q: v.tensor_add(xs, xs, scr[:, 0:H]),
                   r=[("scr",), ("x", q)], w=[("x", q)])

    def emit_ffn(inv):
        emit_ln("u_tok")
        Sc.dma("pool", lambda g: g.collective_compute(
            "AllGather", ALU.bypass, replica_groups=[list(range(NC))],
            ins=[ap["u_tok"].opt()], outs=[ap["ag_u"].opt()]),
            r=[("u_tok", q) for q in range(NT)], w=[("ag_u",)],
            sem="ccU%d" % inv, inc=1)
        emit_routing(inv)
        queue_meta(inv)
        emit_gateup(ap["swg"], 0, ap["swu"], 0, IS // 128, dr=5,
                    dr_from=(4 if inv == 0 else 7))

        def sh_cb(rt, quar, bk):
            xs = x_sb[:, rt * H + quar * 512: rt * H + (quar + 1) * 512]
            Sc.cop("dve", lambda v, bk=bk, xs=xs, rt=rt: v.scalar_tensor_tensor(
                out=xs, in0=pg[bk][:, :], scalar=0.5, in1=xs,
                op0=ALU.mult, op1=ALU.add),
                r=[("pg%d" % bk,), ("x", rt)], w=[("x", rt)])
        emit_down(ap["swd"], 0, IS // 128, sh_cb, dr=5)
        emit_experts(inv)
        emit_combine(inv)

    def emit_qkv():
        emit_ln(None)
        for og in range(16):
            slot = og % 2
            Sc.dma("pool", lambda g, og=og, slot=slot: g.dma_start(
                out=wAv(slot)[:, 0:4096],
                in_=ap["w_qkv"][:, og * 4096:(og + 1) * 4096]),
                w=wAk[slot], sem="wA%d" % slot)
            for mi in range(2):
                for tau in range(2):
                    bk = mi * 2 + tau

                    def qmm(t, slot=slot, mi=mi, tau=tau, bk=bk):
                        last = None
                        for k in range(KC):
                            base = k * 256 + mi * 128
                            last = t.matmul(
                                pg[bk][:, :], wAv(slot)[:, base:base + 128],
                                ufeat[:, k * TOK + tau * 512:
                                      k * TOK + (tau + 1) * 512],
                                start=(k == 0), stop=(k == KC - 1))
                        return last
                    Sc.cop("pe", qmm, r=list(wAk[slot]) + ufk(0, KC),
                           w=[("pg%d" % bk,)])
                    of = og * 2 + mi
                    sl = evctr[0] % 2
                    evac_store(bk, ap["qk_in"][of * 128:(of + 1) * 128,
                                               tau * 512:(tau + 1) * 512],
                               [("qk_in",)])
                    nsl = evctr[0] % 2
                    nbk = 4 + nsl
                    Sc.cop("act", lambda a, bk=bk, nsl=nsl: a.activation(
                        out=stg[:, nsl * 512:(nsl + 1) * 512], in_=pg[bk][:, :],
                        func=AF.Square),
                        r=[("pg%d" % bk,)], w=[("stg", nsl)])
                    Sc.cop("pe", lambda t, nbk=nbk, nsl=nsl: t.matmul(
                        pg[nbk][0:1, :], onesb[:, 0:1],
                        stg[:, nsl * 512:(nsl + 1) * 512],
                        start=True, stop=True),
                        r=[("stg", nsl), ("onesb",)], w=[("pg%d" % nbk,)])
                    if of < 16:
                        ncol = tau * 16 + of
                        Sc.cop("dve", lambda v, nbk=nbk, ncol=ncol:
                               v.reduce_max(norm48[0:1, ncol:ncol + 1],
                                            pg[nbk][0:1, :], axis=AX.X),
                               r=[("pg%d" % nbk,)], w=[("n48", ncol)])
                    else:
                        ncol = 32 + (of - 16)
                        if tau == 0:
                            Sc.cop("dve", lambda v, nbk=nbk, ncol=ncol:
                                   v.reduce_max(norm48[0:1, ncol:ncol + 1],
                                                pg[nbk][0:1, :], axis=AX.X),
                                   r=[("pg%d" % nbk,)], w=[("n48", ncol)])
                        else:
                            Sc.cop("dve", lambda v, nbk=nbk, ncol=ncol:
                                   v.reduce_max(norm48[0:1, 48:49],
                                                pg[nbk][0:1, :], axis=AX.X),
                                   r=[("pg%d" % nbk,)], w=[("n48x",)])
                            Sc.cop("dve", lambda v, ncol=ncol: v.tensor_max(
                                norm48[0:1, ncol:ncol + 1],
                                norm48[0:1, ncol:ncol + 1], norm48[0:1, 48:49]),
                                r=[("n48", ncol), ("n48x",)],
                                w=[("n48", ncol)])
        Sc.cop("dve", lambda v: v.tensor_copy(norm48b[0:1, 0:48],
               norm48[0:1, 0:48]),
               r=[("n48", i) for i in range(48)], w=[("n48b",)])
        Sc.dma("sp", lambda s: s.dma_start(
            out=ap["qk_in"][4096:4097, 0:64], in_=norm48b[0:1, :]),
            r=[("n48b",)], w=[("qk_in",)], sem="nrw")
        Sc.dma("pool", lambda g: g.collective_compute(
            "AllGather", ALU.bypass,
            replica_groups=[[0, 1, 2, 3], [4, 5, 6, 7]],
            ins=[ap["qk_in"].opt()], outs=[ap["qk_out"].opt()]),
            r=[("qk_in",)], w=[("qk_out",)], sem="ccQ", inc=1)
        Sc.dma("pool", lambda g: g.dma_start(
            out=normsc[0:1, 0:192].rearrange("a (r c) -> a r c", c=48),
            in_=ap["qk_out"].rearrange("(r f) t -> f r t", f=QKR)
            [4096:4097, :, 0:48]),
            r=[("qk_out",)], w=[("nsc",)], sem="nrr")
        Sc.cop("dve", lambda v: v.tensor_copy(
            normsc[0:1, 192:208], normsc[0:1, 32:48]),
            r=[("nsc",)], w=[("nsc",)])
        for rq in range(1, 4):
            Sc.cop("dve", lambda v, rq=rq: v.tensor_max(
                normsc[0:1, 192 + rq * 16:192 + (rq + 1) * 16],
                normsc[0:1, 192 + (rq - 1) * 16:192 + rq * 16],
                normsc[0:1, rq * 48 + 32:rq * 48 + 48]),
                r=[("nsc",)], w=[("nsc",)])
        for rq in range(4):
            for tau in range(2):
                base = 256 + rq * 32 + tau * 16
                Sc.cop("dve", lambda v, rq=rq, tau=tau, base=base: v.tensor_mul(
                    normsc[0:1, base:base + 16],
                    normsc[0:1, rq * 48 + tau * 16:rq * 48 + (tau + 1) * 16],
                    normsc[0:1, 192 + rq * 16:192 + (rq + 1) * 16]),
                    r=[("nsc",)], w=[("nsc",)])
        Sc.cop("act", lambda a: a.activation(
            out=normsc[0:1, 256:384], in_=normsc[0:1, 256:384], func=AF.Ln,
            scale=(1.05 * SCALE) ** 2),
            r=[("nsc",)], w=[("nsc",)])
        Sc.cop("act", lambda a: a.activation(
            out=normsc[0:1, 256:384], in_=normsc[0:1, 256:384], func=AF.Exp,
            scale=0.5),
            r=[("nsc",)], w=[("nsc",)])
        Sc.cop("dve", lambda v: v.tensor_scalar(
            out=normsc[0:1, 256:384], in0=normsc[0:1, 256:384],
            scalar1=-1.0, scalar2=None, op0=ALU.mult),
            r=[("nsc",)], w=[("nsc",)])
        for vog in range(4):
            slot = vog % 2
            vsl = wVv(slot)
            Sc.dma("pool", lambda g, vog=vog, vsl=vsl: g.dma_start(
                out=vsl[:, 0:8192],
                in_=ap["w_qkv"][:, 65536 + vog * 8192: 65536 + (vog + 1) * 8192]),
                w=wVk[slot], sem="wV%d" % slot)
            for rt in range(NT):
                bk = 4 + (rt % 2)

                def vmm(t, rt=rt, bk=bk, vsl=vsl):
                    last = None
                    for k in range(KC):
                        last = t.matmul(
                            pg[bk][:, :],
                            ufeat[:, k * TOK + rt * 128: k * TOK + rt * 128 + 128],
                            vsl[:, k * 512:(k + 1) * 512],
                            start=(k == 0), stop=(k == KC - 1))
                    return last
                Sc.cop("pe", vmm, r=list(wVk[slot]) + ufk(0, KC),
                       w=[("pg%d" % bk,)])
                evac_store(bk, ap["v_in"][rt * 128:(rt + 1) * 128,
                                          vog * 512:(vog + 1) * 512],
                           [("v_in",)])
        Sc.dma("pool", lambda g: g.collective_compute(
            "AllGather", ALU.bypass,
            replica_groups=[[0, 1, 2, 3], [4, 5, 6, 7]],
            ins=[ap["v_in"].opt()], outs=[ap["v_out"].opt()]),
            r=[("v_in",)], w=[("v_out",)], sem="ccV", inc=1)

    def emit_attn_core():
        Sc.alias([("den",)], [("atl2",)])
        Sc.alias([("pb", i) for i in range(6)] + [("qb", 0), ("qb", 1)],
                 [("stg", 0), ("stg", 1)])
        Sc.alias([("rd", 0), ("rd", 1)], [("ustg", 0), ("ustg", 1)])

        engs = [Sc.h[e] for e in ("pe", "act", "dve", "sp")]
        axi = AxisInfo(NC, 1)
        idxr = [nc.axis_index(axi, engines=[e.engine]) for e in engs]
        s0 = Sc.snap()
        states = []
        for c in nc.Switch(engines=engs, index=idxr, n=NC):
            Sc.restore(s0)
            b, a = c // 4, c % 4
            for h in range(HEADS):
                ksl = h % 2
                kh = ufeat[:, ksl * 4096:(ksl + 1) * 4096]
                vh = ufeat[:, 8192 + ksl * 4096: 8192 + (ksl + 1) * 4096]
                khk = ufk(4 * ksl, 4 * ksl + 4)
                vhk = ufk(8 + 4 * ksl, 8 + 4 * ksl + 4)
                Sc.dma("sp", lambda s, h=h, kh=kh: s.dma_start(
                    out=kh.rearrange("p (r t) -> p r t", t=TOK),
                    in_=ap["qk_out"].rearrange("(r f) t -> f r t", f=QKR)
                    [2048 + h * 128: 2048 + (h + 1) * 128, :, :]),
                    r=[("qk_out",)], w=khk, sem="kk%d" % ksl)
                Sc.dma("sp", lambda s, h=h, vh=vh: s.dma_start(
                    out=vh.rearrange("p (kt d) -> p kt d", d=128),
                    in_=ap["v_out"].rearrange("(kt p) f -> p kt f", p=128)
                    [:, :, h * 128:(h + 1) * 128]),
                    r=[("v_out",)], w=vhk, sem="vv%d" % ksl)
                jqs = (a, 7 - a)
                nks = (4 * (a + 1), 4 * (8 - a))
                for ji, jq in enumerate(jqs):
                    rq = jq // 2
                    qh = qbuf[:, ji * 512:(ji + 1) * 512]
                    Sc.dma("sp", lambda s, rq=rq, h=h, jq=jq, qh=qh: s.dma_start(
                        out=qh,
                        in_=ap["qk_out"][rq * QKR + h * 128:
                                         rq * QKR + (h + 1) * 128,
                                         (jq % 2) * 512:((jq % 2) + 1) * 512]),
                        r=[("qk_out",)], w=[("qb", ji)], sem="qq%d" % ji)
                    ncol = 256 + rq * 32 + (jq % 2) * 16 + h
                    Sc.cop("pe", lambda t, ji=ji, ncol=ncol: t.matmul(
                        pg[4 + ji][:, 0:1], onesf[0:1, :],
                        normsc[0:1, ncol:ncol + 1],
                        start=True, stop=True),
                        r=[("nsc",), ("onesf",)], w=[("pg%d" % (4 + ji),)])
                    Sc.cop("dve", lambda v, ji=ji: v.tensor_copy(
                        smalls[:, 8 + ji:9 + ji], pg[4 + ji][:, 0:1]),
                        r=[("pg%d" % (4 + ji),)], w=[("sm", 8 + ji)])
                for step in range(max(nks) + 2):
                    for ji in range(2):
                        nk = nks[ji]
                        qh = qbuf[:, ji * 512:(ji + 1) * 512]
                        if 2 <= step < nk + 2:
                            kt = step - 2
                            prc = ji * 3 + kt % 3
                            Sc.cop("pe", lambda t, kt=kt, prc=prc, vh=vh,
                                   ji=ji, nk=nk:
                                   t.matmul(pg[2 + ji][:, :],
                                            vh[:, kt * 128:(kt + 1) * 128],
                                            pbuf[:, prc * 512:(prc + 1) * 512],
                                            start=(kt == 0),
                                            stop=(kt == nk - 1)),
                                   r=vhk + [("pb", prc)],
                                   w=[("pg%d" % (2 + ji),)])
                            Sc.cop("pe", lambda t, kt=kt, prc=prc, ji=ji,
                                   nk=nk:
                                   t.matmul(
                                       pg[4 + ji][0:1, :], onesb[:, 0:1],
                                       pbuf[:, prc * 512:(prc + 1) * 512],
                                       start=(kt == 0), stop=(kt == nk - 1)),
                                   r=[("pb", prc), ("onesb",)],
                                   w=[("pg%d" % (4 + ji),)])
                        if step < nk:
                            kt = step
                            stbk = ji
                            prc = ji * 3 + kt % 3
                            Sc.cop("pe", lambda t, kt=kt, stbk=stbk, kh=kh,
                                   qh=qh:
                                   t.matmul(pg[stbk][:, :],
                                            kh[:, kt * 128:(kt + 1) * 128], qh,
                                            start=True, stop=True),
                                   r=khk + [("qb", ji)],
                                   w=[("pg%d" % stbk,)])
                            Sc.cop("act", lambda a_, prc=prc, stbk=stbk, ji=ji:
                                   a_.activation(
                                       out=pbuf[:, prc * 512:(prc + 1) * 512],
                                       in_=pg[stbk][:, :], func=AF.Exp,
                                       scale=SCALE,
                                       bias=smalls[:, 8 + ji:9 + ji]),
                                   r=[("pg%d" % stbk,), ("sm", 8 + ji)],
                                   w=[("pb", prc)])
                            if kt >= nk - 4:
                                o = 384 - (kt - (nk - 4)) * 128
                                Sc.cop("dve", lambda v, prc=prc, o=o:
                                       v.tensor_mul(
                                           pbuf[:, prc * 512:(prc + 1) * 512],
                                           pbuf[:, prc * 512:(prc + 1) * 512],
                                           cmask[:, o:o + 512]),
                                       r=[("pb", prc), ("cmask",)],
                                       w=[("pb", prc)])
                for ji in range(2):
                    Sc.cop("dve", lambda v, ji=ji: v.tensor_copy(
                        denr, pg[4 + ji][0:1, :]),
                        r=[("pg%d" % (4 + ji),)], w=[("den",)])
                    Sc.cop("pe", lambda t, ji=ji: t.matmul(
                        pg[ji][:, :], onesf[0:1, :], denr,
                        start=True, stop=True),
                        r=[("den",), ("onesf",)], w=[("pg%d" % ji,)])
                    def _recip(v, ji=ji):
                        with nc.allow_low_precision(
                                reason="bf16 reciprocal of softmax denom"):
                            return v.reciprocal(
                                rds[:, ji * 512:(ji + 1) * 512], pg[ji][:, :])
                    Sc.cop("dve", _recip, r=[("pg%d" % ji,)], w=[("rd", ji)])
                    dst = big1[:, h * TOK + ji * 512: h * TOK + (ji + 1) * 512]
                    Sc.cop("dve", lambda v, ji=ji, dst=dst:
                           v.tensor_mul(dst, pg[2 + ji][:, :],
                                        rds[:, ji * 512:(ji + 1) * 512]),
                           r=[("pg%d" % (2 + ji),), ("rd", ji)],
                           w=[("b1", h, ji)])
            states.append(Sc.snap())
        Sc.merge(states)
        Sc.alias([("atl2",)], [("den",)])
        Sc.alias([("stg", 0), ("stg", 1)],
                 [("pb", i) for i in range(6)] + [("qb", 0), ("qb", 1)])
        Sc.alias([("ustg", 0), ("ustg", 1)], [("rd", 0), ("rd", 1)])


    def emit_outproj():
        def ao_cb(rt, quar, bk):
            evac_store(bk, ap["ao_in"][rt * 128:(rt + 1) * 128,
                                       quar * 512:(quar + 1) * 512],
                       [("ao_in",)])
        emit_down(ap["w_out"], 0, HEADS, ao_cb)
        Sc.dma("pool", lambda g: g.collective_compute(
            "AllGather", ALU.bypass,
            replica_groups=[[0, 1, 2, 3], [4, 5, 6, 7]],
            ins=[ap["ao_in"].opt()], outs=[ap["ao_out"].opt()]),
            r=[("ao_in",)], w=[("ao_out",)], sem="ccO", inc=1)

    def emit_attn_resid():
        engs = [Sc.h["sp"]]
        axi = AxisInfo(NC, 1)
        idxr = [nc.axis_index(axi, engines=[e.engine]) for e in engs]
        s0 = Sc.snap()
        states = []
        for c in nc.Switch(engines=engs, index=idxr, n=NC):
            Sc.restore(s0)
            b, a = c // 4, c % 4
            for gi, g in enumerate((2 * a, 2 * a + 1)):
                r_g = min(g, 7 - g)
                off4 = 0 if g <= 3 else 4
                Sc.dma("sp", lambda s, r_g=r_g, off4=off4, gi=gi: s.dma_start(
                    out=ufeat[:, gi * 4 * H:(gi + 1) * 4 * H]
                    .rearrange("p (q f) -> p q f", f=H),
                    in_=ap["ao_out"].rearrange("(rr p) f -> p rr f", p=128)
                    [:, r_g * 8 + off4: r_g * 8 + off4 + 4, :]),
                    r=[("ao_out",)], w=ufk(gi * 8, gi * 8 + 8),
                    sem="aop%d" % gi)
            states.append(Sc.snap())
        Sc.merge(states)
        for q in range(NT):
            gi, qq = q // 4, q % 4
            pk = ufeat[:, gi * 4 * H + qq * H: gi * 4 * H + (qq + 1) * H]
            pkk = ufk(gi * 8 + 2 * qq, gi * 8 + 2 * qq + 2)
            xs = x_sb[:, q * H:(q + 1) * H]
            Sc.cop("dve", lambda v, xs=xs, pk=pk: v.tensor_add(xs, xs, pk),
                   r=[("x", q)] + pkk, w=[("x", q)])
            Sc.cop("dve", lambda v, xs=xs: v.tensor_add(xs, xs, brep[:, :]),
                   r=[("x", q), ("brep",)], w=[("x", q)])

    def dump(name, src_ap, keys, cast=False):
        if not dbg:
            return
        q = "pool" if cast else "sp"
        Sc.dma(q, lambda s: s.dma_start(out=dbg_outs[name], in_=src_ap),
               r=keys, w=(), sem="dbg")

    emit_init()
    if "ffn0" in parts:
        emit_ffn(0)
        if dbg:
            dump("d_u", ap["u_tok"][:, :], [("u_tok", q) for q in range(NT)])
            dump("d_p", p_sb[:, :], [("p", q) for q in range(NT)])
            dump("d_idx4", idx4f[:, :], [("idx4", q) for q in range(NT)])
            dump("d_w4", w4[:, :], [("w4", q) for q in range(NT)])
            dump("d_slot", slot_tok[:, :], [("slot", q) for q in range(NT)])
            dump("d_gidx", gidx[:, :], [("gidx", q) for q in range(NT)])
            dump("d_wfin", wfin[:, :], [("wfin", q) for q in range(NT)])
            dump("d_eo", ap["eo_in"][:, :], [("eo_in", 0), ("eo_in", 1)])
            dump("d_atl", ap["at_in"][:, :], [("at_in",)])
            dump("d_bufidx", bufidx[:, :], [("bufidx",)])
            dump("d_slotc", slotcols[:, :], [("slc", tt2) for tt2 in range(64)])
            dump("d_x1", x_sb[:].rearrange("p (q f) -> p q f", f=H),
                 [("x", q) for q in range(NT)], cast=True)
    if "attn" in parts:
        emit_qkv()
        emit_attn_core()
        emit_outproj()
        emit_attn_resid()
        if dbg:
            dump("d_km", kmax8[:, :], [("m2s",), ("kmx",), ("qmx",)])
            dump("d_negm", smalls[:, 8:10], [("sm", 8), ("sm", 9)])
            dump("d_x2", x_sb[:].rearrange("p (q f) -> p q f", f=H),
                 [("x", q) for q in range(NT)], cast=True)
    if "ffn1" in parts:
        emit_ffn(1)
    for q in range(NT):
        Sc.dma("pool", lambda g, q=q: g.dma_start(
            out=ap["y"][q * 128:(q + 1) * 128, :],
            in_=x_sb[:, q * H:(q + 1) * H]),
            r=[("x", q)], w=(), sem="yo")
    Sc.h["pool"].wait_ge(Sc.sems["yo"], Sc.cnt["yo"])
    return nc


def _gu_blocks(n_of):
    ogs, o = [], 0
    while o < n_of:
        m = min(2, n_of - o)
        ogs.append((o, m))
        o += m
    return ogs


def pack_gateup(w, width=None):
    Wd = w.shape[1]
    a = w.reshape(KC, 128, Wd)
    blocks = []
    for (o, m) in _gu_blocks(Wd // 128):
        blocks.append(np.ascontiguousarray(
            a[:, :, o * 128:(o + m) * 128].transpose(1, 0, 2).reshape(128, -1)))
    return np.ascontiguousarray(np.concatenate(blocks, axis=1))


def pack_down(w):
    n_ic = w.shape[0] // 128
    a = w.reshape(n_ic, 128, w.shape[1])
    blocks = [np.ascontiguousarray(
        a[:, :, q * 512:(q + 1) * 512].transpose(1, 0, 2).reshape(128, -1))
        for q in range(4)]
    return np.ascontiguousarray(np.concatenate(blocks, axis=1))


def pack_qkv(w):
    a = w.reshape(KC, 128, 3 * H)
    blocks = []
    for og in range(16):
        blocks.append(np.ascontiguousarray(
            a[:, :, og * 256:(og + 1) * 256].transpose(1, 0, 2)
            .reshape(128, -1)))
    for vog in range(4):
        blocks.append(np.ascontiguousarray(
            a[:, :, 2 * H + vog * 512: 2 * H + (vog + 1) * 512]
            .transpose(1, 0, 2).reshape(128, -1)))
    return np.ascontiguousarray(np.concatenate(blocks, axis=1))


def make_inmaps(inputs):
    X = np.ascontiguousarray(np.asarray(inputs["L0_attn_x"],
                                        dtype=np.float32).reshape(T, H))
    wg = np.ascontiguousarray(np.asarray(inputs["L0_ffn_w_gate"], np.float32))
    wu = np.ascontiguousarray(np.asarray(inputs["L0_ffn_w_up"], np.float32))
    wd = np.ascontiguousarray(np.asarray(inputs["L0_ffn_w_down"], np.float32))
    w_qkv = np.ascontiguousarray(np.asarray(inputs["L0_attn_w_qkv"], np.float32))
    w_out = np.ascontiguousarray(np.asarray(inputs["L0_attn_w_out"], np.float32))
    b_out = np.ascontiguousarray(np.asarray(inputs["L0_attn_b_out"],
                                            np.float32).reshape(1, H))
    gate_w = np.ascontiguousarray(np.asarray(inputs["L0_ffn_gate_w"], np.float32))
    swg = np.ascontiguousarray(np.asarray(inputs["L0_ffn_sw_gate"], np.float32))
    swu = np.ascontiguousarray(np.asarray(inputs["L0_ffn_sw_up"], np.float32))
    swd = np.ascontiguousarray(np.asarray(inputs["L0_ffn_sw_down"], np.float32))
    swg_p = pack_gateup(swg)
    swu_p = pack_gateup(swu)
    swd_p = pack_down(swd)
    wqkv_p = pack_qkv(w_qkv)
    wout_p = pack_down(w_out)
    c64v = np.array([[64.0], [1.0]], np.float32)
    in_maps = []
    for c in range(NC):
        selT = np.zeros((E, 2), np.float32)
        selT[2 * c, 0] = 1.0
        selT[2 * c + 1, 1] = 1.0
        pm = np.zeros((E, 16), np.float32)
        pm[:, :2 * c] = 1.0
        wg2 = wg.reshape(E, H, II)[2 * c:2 * c + 2]
        wu2 = wu.reshape(E, H, II)[2 * c:2 * c + 2]
        wd2 = wd.reshape(E, II, H)[2 * c:2 * c + 2]
        xr = np.ascontiguousarray(
            X[c * TOK:(c + 1) * TOK].reshape(NT, 128, H)
            .transpose(1, 0, 2).reshape(128, NT * H))
        in_maps.append({
            "x": xr,
            "w_qkv": wqkv_p, "w_out": wout_p, "b_out": b_out,
            "gate_w": gate_w,
            "wg": np.concatenate([pack_gateup(wg2[e]) for e in range(2)],
                                 axis=1),
            "wu": np.concatenate([pack_gateup(wu2[e]) for e in range(2)],
                                 axis=1),
            "wd": np.concatenate([pack_down(wd2[e]) for e in range(2)],
                                 axis=1),
            "swg": swg_p, "swu": swu_p, "swd": swd_p,
            "selT": selT, "pm": pm, "c64": c64v,
        })
    return in_maps


def kernel(**inputs):
    nc = build()
    in_maps = make_inmaps(inputs)
    res = run_bass_kernel_spmd(nc, in_maps, core_ids=list(range(NC)))
    y = np.concatenate([res.results[c]["y"] for c in range(NC)], axis=0)
    return y.reshape(B, S, H).astype(np.float32)
